# version 10
# baseline (speedup 1.0000x reference)
import contextlib
import numpy as np
import ml_dtypes
import concourse.bass as bass
import concourse.mybir as mybir
from concourse.bass_core_rust import IndirectOffsetOnAxis
from concourse.bass_utils import run_bass_kernel_spmd

F32 = mybir.dt.float32
BF16 = mybir.dt.bfloat16
I32 = mybir.dt.int32
AF = mybir.ActivationFunctionType
ALU = mybir.AluOpType
AX = mybir.AxisListType

D = 2048
S = 8192
NB = 2
TOK = 2048
NCH = D // 128
NE = 18
ICH = 11
NEGBIG = -1.0e30
E_SPLIT = 6
BF = ml_dtypes.bfloat16

DEBUG_TAPS = []
PHASE_LIMIT = 99


class KB:
    def __init__(self, nc, ctx):
        self.nc = nc
        self.ctx = ctx
        self.cnt = {}
        self.sems = {n: ctx.enter_context(nc.semaphore(n))
                     for n in ("LD", "GD", "PE", "VE", "AC", "PL", "CC", "OUT",
                               "CW", "UP", "DN", "AA")}
        self.ops = None

    def phase_begin(self):
        self.ops = {k: [] for k in ("pe", "ve", "ac", "gp", "sy")}

    def phase_flush(self, block):
        def runner(fns):
            return lambda e: [f(e) for f in fns]
        if self.ops["pe"]:
            block.tensor(runner(self.ops["pe"]))
        if self.ops["ve"]:
            block.vector(runner(self.ops["ve"]))
        if self.ops["ac"]:
            block.scalar(runner(self.ops["ac"]))
        if self.ops["gp"]:
            block.gpsimd(runner(self.ops["gp"]))
        if self.ops["sy"]:
            block.sync(runner(self.ops["sy"]))
        self.ops = None

    def em(self, eng, fn):
        self.ops[eng].append(fn)

    def wait(self, eng, sem, v):
        if v > 0:
            self.em(eng, lambda e, s=self.sems[sem], v=v: e.wait_ge(s, v))

    def bump(self, sem, k=1):
        self.cnt[sem] = self.cnt.get(sem, 0) + k
        return self.sems[sem], self.cnt[sem]

    def val(self, sem):
        return self.cnt.get(sem, 0)

    def emi(self, eng, sem, fn, k=1):
        h, _ = self.bump(sem, k)
        self.em(eng, lambda e, h=h, k=k, fn=fn: fn(e).then_inc(h, k))
        return self.val(sem)

    def semi(self, eng, sem, fn):
        self.wait(eng, sem, self.val(sem))
        return self.emi(eng, sem, fn)


def build_program():
    ctx = contextlib.ExitStack()
    nc = bass.Bass("TRN2", target_bir_lowering=False)
    kb = KB(nc, ctx)

    x_in = nc.dram_tensor("x", [TOK, D], F32, kind="ExternalInput").ap()
    wqkv = nc.dram_tensor("wqkv", [D, 3 * D], F32, kind="ExternalInput").ap()
    outw = nc.dram_tensor("outw", [D, D], F32, kind="ExternalInput").ap()
    upT = nc.dram_tensor("upT", [NE, D, 2816], F32, kind="ExternalInput").ap()
    downT = nc.dram_tensor("downT", [NE, 1408, D], F32, kind="ExternalInput").ap()
    gwT = nc.dram_tensor("gwT", [D, 16], F32, kind="ExternalInput").ap()
    qg_in = nc.dram_tensor("qg", [128, NCH], F32, kind="ExternalInput").ap()
    kg_in = nc.dram_tensor("kg", [128, NCH], F32, kind="ExternalInput").ap()
    idb_in = nc.dram_tensor("idb", [128, 128], BF16, kind="ExternalInput").ap()
    idf_in = nc.dram_tensor("idf", [128, 128], F32, kind="ExternalInput").ap()
    amask_in = nc.dram_tensor("amask", [128, 4 * 512], F32, kind="ExternalInput").ap()
    aidx_in = nc.dram_tensor("aidx", [128, 48], I32, kind="ExternalInput").ap()
    eidx_in = nc.dram_tensor("eidx", [128, 64], I32, kind="ExternalInput").ap()

    out_t = nc.dram_tensor("out", [TOK, D], F32, kind="ExternalOutput").ap()

    contrib = nc.dram_tensor("contrib", [3 * D, TOK], BF16).ap()
    gath = nc.dram_tensor("gath", [4 * 3 * D, TOK], BF16).ap()
    ctxc = nc.dram_tensor("ctxc", [TOK, TOK], BF16).ap()
    gath2 = nc.dram_tensor("gath2", [4 * TOK, TOK], BF16).ap()
    qraw = nc.dram_tensor("qraw", [D, TOK], BF16).ap()
    kraw = nc.dram_tensor("kraw", [D, TOK], BF16).ap()
    rowsAB = nc.dram_tensor("rowsAB", [4, TOK], F32).ap()
    cwT_d = nc.dram_tensor("cwT_d", [NE, TOK], F32).ap()
    ffnacc = nc.dram_tensor("ffnacc", [TOK, D], F32).ap()
    dbg = {}
    for name in DEBUG_TAPS:
        if name in ("xn_sb", "ab_sb"):
            continue
        src = {"contrib": contrib, "gath": gath, "ctxc": ctxc, "gath2": gath2,
               "qraw": qraw, "kraw": kraw, "rowsAB": rowsAB, "cwT_d": cwT_d,
               "ffnacc": ffnacc}[name]
        dbg[name] = (nc.dram_tensor("dbg_" + name, list(src.shape), src.dtype,
                                    kind="ExternalOutput").ap(), src)

    A = lambda *a, **k2: ctx.enter_context(nc.sbuf_tensor(*a, side="left", **k2))
    xn = A("xn", [128, NCH * TOK], BF16)
    idb = A("idb_s", [128, 128], BF16)
    idf = A("idf_s", [128, 128], F32)
    ones_b = A("ones_b", [128, 1], BF16)
    ones_f = A("ones_f", [1, 128], F32)
    ones_fb = A("ones_fb", [1, 128], BF16)
    ones_row = A("ones_row", [1, TOK], F32)
    qg = A("qg_s", [128, NCH], F32)
    kg = A("kg_s", [128, NCH], F32)
    amask = A("amask_s", [128, 4 * 512], F32)
    aidx = A("aidx_s", [128, 48], I32)
    eidx = A("eidx_s", [128, 64], I32)
    gwT_s = A("gwT_s", [128, NCH * 16], BF16)
    eps5 = A("eps5", [128, 1], F32)
    eps6 = A("eps6", [128, 1], F32)
    cwT_s = A("cwT_s", [16, TOK], F32)

    em, wait, emi, val = kb.em, kb.wait, kb.emi, kb.val

    with nc.Block() as block:
        kb.phase_begin()
        for dst, src in [(idb, idb_in), (idf, idf_in), (qg, qg_in), (kg, kg_in),
                         (amask, amask_in), (aidx, aidx_in), (eidx, eidx_in)]:
            emi("sy", "LD", lambda e, d=dst, s=src: e.dma_start(out=d[:], in_=s[:]), 16)
        ld_consts = val("LD")
        wait("ve", "LD", ld_consts)
        em("ve", lambda e: e.memset(ones_b[:], 1.0))
        em("ve", lambda e: e.memset(ones_f[:], 1.0))
        em("ve", lambda e: e.memset(ones_fb[:], 1.0))
        em("ve", lambda e: e.memset(eps5[:], 1.0e-5))
        em("ve", lambda e: e.memset(eps6[:], 1.0e-6))
        ve_ones = emi("ve", "VE", lambda e: e.memset(ones_row[:], 1.0))
        wait("sy", "VE", ve_ones)
        for r in (16, 17):
            emi("sy", "LD", lambda e, r=r: e.dma_start(
                out=cwT_d[r:r + 1, :], in_=ones_row[:]), 16)
        kb.phase_flush(block)

        kb.phase_begin()
        with contextlib.ExitStack() as ph:
            Ar = lambda *a: ph.enter_context(nc.sbuf_tensor(*a, side="right"))
            xbuf = [Ar(f"xb{i}", [128, D], F32) for i in range(2)]
            xnt = [Ar(f"xnt{i}", [128, D], BF16) for i in range(2)]
            sq = Ar("sq1a", [128, D], F32)
            ssum = Ar("ssum", [128, 1], F32)
            mean = Ar("mean", [128, 1], F32)
            msq = Ar("msq", [128, 1], F32)
            var = Ar("var", [128, 1], F32)
            sd = Ar("sd", [128, 1], F32)
            nb_ = Ar("nb", [128, 1], F32)
            tps = [ph.enter_context(nc.psum_tensor(f"tp{i}", [128, 512], BF16))
                   for i in range(2)]

            ac_done, pe_done = {}, {}
            ve_copy = {}
            wait("pe", "LD", ld_consts)
            wait("ve", "LD", ld_consts)
            for tc in range(TOK // 128):
                buf, xb = xbuf[tc % 2], xnt[tc % 2]
                if tc >= 2:
                    wait("sy", "VE", ac_done[tc - 2])
                ld_x = emi("sy", "LD", lambda e, b=buf, tc=tc: e.dma_start(
                    out=b[:], in_=x_in[tc * 128:(tc + 1) * 128, :]), 16)
                wait("ve", "LD", ld_x)
                kb.semi("ve", "VE", lambda e, b=buf: e.reduce_sum(out=ssum[:], in_=b[:], axis=AX.X))
                kb.semi("ve", "VE", lambda e: e.tensor_scalar_mul(out=mean[:], in0=ssum[:], scalar1=1.0 / D))
                ve_mean = kb.semi("ve", "VE", lambda e: e.tensor_tensor(
                    out=msq[:], in0=mean[:], in1=mean[:], op=ALU.mult))
                wait("ac", "LD", ld_x)
                wait("ac", "VE", ve_mean)
                ac_sq = emi("ac", "AC", lambda e, b=buf: e.activation(
                    out=sq[:], in_=b[:], func=AF.Square))
                wait("ve", "AC", ac_sq)
                kb.semi("ve", "VE", lambda e: e.reduce_sum(out=ssum[:], in_=sq[:], axis=AX.X))
                kb.semi("ve", "VE", lambda e: e.tensor_scalar_mul(out=var[:], in0=ssum[:], scalar1=1.0 / D))
                ve_var = kb.semi("ve", "VE", lambda e: e.tensor_tensor(
                    out=var[:], in0=var[:], in1=msq[:], op=ALU.subtract))
                wait("ac", "VE", ve_var)
                ac_sd = emi("ac", "AC", lambda e: e.activation(
                    out=sd[:], in_=var[:], func=AF.Sqrt, bias=eps5[:]))
                wait("ve", "AC", ac_sd)
                kb.semi("ve", "VE", lambda e: e.reciprocal(out=sd[:], in_=sd[:]))
                ve_nb = kb.semi("ve", "VE", lambda e: e.scalar_tensor_tensor(
                    out=nb_[:], in0=mean[:], scalar=-1.0, in1=sd[:],
                    op0=ALU.mult, op1=ALU.mult))
                if tc >= 2:
                    wait("ve", "PE", pe_done[tc - 2])
                ve_norm = kb.semi("ve", "VE", lambda e, b=buf, xb=xb: e.tensor_scalar(
                    out=xb[:], in0=b[:], scalar1=sd[:], scalar2=nb_[:],
                    op0=ALU.mult, op1=ALU.add))
                ac_done[tc] = ve_norm
                wait("pe", "VE", ve_norm)
                for g in range(4):
                    bank = tps[g % 2]
                    prev = (tc, g - 2) if g >= 2 else (tc - 1, g + 2)
                    if prev in ve_copy:
                        wait("pe", "VE", ve_copy[prev])
                    for d4 in range(4):
                        d = g * 4 + d4
                        fn = lambda e, d=d, d4=d4, bank=bank, xb=xb: e.transpose(
                            out=bank[:, d4 * 128:(d4 + 1) * 128],
                            in_=xb[:, d * 128:(d + 1) * 128], identity=idb[:])
                        if d4 < 3:
                            em("pe", fn)
                        else:
                            emi("pe", "PE", fn)
                    wait("ve", "PE", val("PE"))
                    ve_copy[(tc, g)] = emi("ve", "VE", lambda e, tc=tc, g=g, bank=bank: e.tensor_copy(
                        out=xn[:].rearrange("p (n t) -> p n t", t=TOK)[:, 4 * g:4 * g + 4, tc * 128:(tc + 1) * 128],
                        in_=bank[:].rearrange("p (n t) -> p n t", t=128)))
                pe_done[tc] = val("PE")
            xn_ready = ve_copy[(TOK // 128 - 1, 3)]
        kb.phase_flush(block)

        ld_contrib_done = 0
        ld_cwt = 0
        cc1 = 0
        if PHASE_LIMIT >= 2:
            kb.phase_begin()
            with contextlib.ExitStack() as ph:
                Ar = lambda *a: ph.enter_context(nc.sbuf_tensor(*a, side="right"))
                Pp = lambda n, s, dt=F32: ph.enter_context(nc.psum_tensor(n, s, dt))
                wbuf = [Ar(f"wb{i}", [128, NCH * 128], BF16) for i in range(2)]
                stg = [Ar(f"stg{i}", [128, TOK], BF16) for i in range(2)]
                sqb = Ar("sqb", [128, TOK], BF16)
                qk_ps = [Pp(f"qk{i}", [128, 512]) for i in range(2)]
                sum_q, ssq_q = Pp("sum_q", [128, 512]), Pp("ssq_q", [128, 512])
                sum_k, ssq_k = Pp("sum_k", [128, 512]), Pp("ssq_k", [128, 512])

                pe_qk, ac_qk, ld_sp = {}, {}, {}
                for f in range(48):
                    isv = f >= 32
                    wb, st = wbuf[f % 2], stg[f % 2]
                    if f >= 2:
                        wait("gp", "PE", pe_qk[(f - 2, 3)])
                    else:
                        wait("gp", "VE", xn_ready)
                    gd_w = emi("gp", "GD", lambda e, f=f, wb=wb: e.dma_start(
                        out=wb[:].rearrange("p (n c) -> p n c", c=128),
                        in_=wqkv[:, f * 128:(f + 1) * 128].rearrange("(n p) c -> p n c", p=128)), 16)
                    for ts in range(4):
                        bank = qk_ps[ts % 2]
                        wait("pe", "GD", gd_w)
                        if f == 0 and ts == 0:
                            wait("pe", "VE", xn_ready)
                        prev = (f, ts - 2) if ts >= 2 else ((f - 1, ts + 2) if f >= 1 else None)
                        if prev in ac_qk:
                            wait("pe", "AC", ac_qk[prev])
                        for d in range(NCH):
                            em("pe", lambda e, wb=wb, d=d: e.ldweights(wb[:, d * 128:(d + 1) * 128]))
                            fn = lambda e, wb=wb, d=d, ts=ts, bank=bank: e.matmul(
                                out=bank[:], lhsT=wb[:, d * 128:(d + 1) * 128],
                                rhs=xn[:, d * TOK + ts * 512:d * TOK + (ts + 1) * 512],
                                start=(d == 0), stop=(d == NCH - 1))
                            if d < NCH - 1:
                                em("pe", fn)
                            else:
                                emi("pe", "PE", fn)
                        mm_done = val("PE")
                        wait("ac", "PE", mm_done)
                        if f >= 2 and ts == 0:
                            wait("ac", "LD", ld_sp[f - 2])
                            wait("ac", "PE", pe_qk[(f - 2, 3)])
                        ac1 = emi("ac", "AC", lambda e, st=st, ts=ts, bank=bank: e.activation(
                            out=st[:, ts * 512:(ts + 1) * 512], in_=bank[:], func=AF.Copy))
                        if not isv:
                            ac1 = emi("ac", "AC", lambda e, ts=ts, bank=bank: e.activation(
                                out=sqb[:, ts * 512:(ts + 1) * 512], in_=bank[:], func=AF.Square))
                        ac_qk[(f, ts)] = ac1
                        if not isv:
                            sm = sum_q if f < 16 else sum_k
                            ss = ssq_q if f < 16 else ssq_k
                            first, last = (f % 16 == 0), (f % 16 == 15)
                            wait("pe", "AC", ac1)
                            em("pe", lambda e: e.ldweights(ones_b[:]))
                            em("pe", lambda e, st=st, ts=ts, sm=sm, first=first, last=last: e.matmul(
                                out=sm[ts * 32:ts * 32 + 1, :], lhsT=ones_b[:],
                                rhs=st[:, ts * 512:(ts + 1) * 512],
                                start=first, stop=last, tile_position=(0, ts * 32),
                                skip_group_check=True))
                            emi("pe", "PE", lambda e, ts=ts, ss=ss, first=first, last=last: e.matmul(
                                out=ss[ts * 32:ts * 32 + 1, :], lhsT=ones_b[:],
                                rhs=sqb[:, ts * 512:(ts + 1) * 512],
                                start=first, stop=last, tile_position=(0, ts * 32),
                                skip_group_check=True))
                        pe_qk[(f, ts)] = val("PE")
                    wait("sy", "AC", ac_qk[(f, 3)])
                    if not isv:
                        wait("sy", "PE", pe_qk[(f, 3)])
                        tgt, r0 = (qraw, f * 128) if f < 16 else (kraw, (f - 16) * 128)
                        ld_sp[f] = emi("sy", "LD", lambda e, st=st, tgt=tgt, r0=r0: e.dma_start(
                            out=tgt[r0:r0 + 128, :], in_=st[:]), 16)
                    else:
                        r0 = 2 * D + (f - 32) * 128
                        ld_sp[f] = emi("sy", "LD", lambda e, st=st, r0=r0: e.dma_start(
                            out=contrib[r0:r0 + 128, :], in_=st[:]), 16)

                fmean = Ar("fmean", [128, 512], F32)
                fvar = Ar("fvar", [128, 512], F32)
                fA = Ar("fA", [128, 512], F32)
                fB = Ar("fB", [128, 512], F32)
                for which, (sm, ss, f_last) in enumerate(
                        [(sum_q, ssq_q, 15), (sum_k, ssq_k, 31)]):
                    wait("ve", "PE", pe_qk[(f_last, 3)])
                    for q4 in range(4):
                        r = q4 * 32
                        kb.semi("ve", "VE", lambda e, sm=sm, r=r: e.tensor_scalar_mul(
                            out=fmean[r:r + 1, :], in0=sm[r:r + 1, :], scalar1=1.0 / D))
                        kb.semi("ve", "VE", lambda e, ss=ss, r=r: e.tensor_scalar_mul(
                            out=fvar[r:r + 1, :], in0=ss[r:r + 1, :], scalar1=1.0 / D))
                        kb.semi("ve", "VE", lambda e, r=r: e.tensor_tensor(
                            out=fA[r:r + 1, :], in0=fmean[r:r + 1, :],
                            in1=fmean[r:r + 1, :], op=ALU.mult))
                        ve1 = kb.semi("ve", "VE", lambda e, r=r: e.tensor_tensor(
                            out=fvar[r:r + 1, :], in0=fvar[r:r + 1, :],
                            in1=fA[r:r + 1, :], op=ALU.subtract))
                        wait("ac", "VE", ve1)
                        ac1 = emi("ac", "AC", lambda e, r=r: e.activation(
                            out=fvar[r:r + 1, :], in_=fvar[r:r + 1, :], func=AF.Sqrt,
                            bias=eps6[r:r + 1, :]))
                        wait("ve", "AC", ac1)
                        kb.semi("ve", "VE", lambda e, r=r: e.reciprocal(out=fA[r:r + 1, :], in_=fvar[r:r + 1, :]))
                        ve2 = kb.semi("ve", "VE", lambda e, r=r: e.scalar_tensor_tensor(
                            out=fB[r:r + 1, :], in0=fmean[r:r + 1, :], scalar=-1.0,
                            in1=fA[r:r + 1, :], op0=ALU.mult, op1=ALU.mult))
                        wait("sy", "VE", ve2)
                        emi("sy", "LD", lambda e, which=which, q4=q4, r=r: e.dma_start(
                            out=rowsAB[2 * which:2 * which + 1, q4 * 512:(q4 + 1) * 512],
                            in_=fA[r:r + 1, :]), 16)
                        emi("sy", "LD", lambda e, which=which, q4=q4, r=r: e.dma_start(
                            out=rowsAB[2 * which + 1:2 * which + 2, q4 * 512:(q4 + 1) * 512],
                            in_=fB[r:r + 1, :]), 16)
                ld_rows = val("LD")

                Abb = [Ar(f"Ab{i}", [128, TOK], F32) for i in range(2)]
                Bbb = [Ar(f"Bb{i}", [128, TOK], F32) for i in range(2)]
                raws = [Ar(f"raw{i}", [128, TOK], BF16) for i in range(2)]
                tmp1 = Ar("tmp1", [128, TOK], F32)
                nstg = [Ar(f"nstg{i}", [128, TOK], BF16) for i in range(2)]
                wait("gp", "LD", ld_rows)
                for which in range(2):
                    emi("gp", "GD", lambda e, which=which: e.dma_start(
                        out=Abb[which][:],
                        in_=rowsAB[2 * which:2 * which + 1, :].partition_broadcast(128)), 16)
                    emi("gp", "GD", lambda e, which=which: e.dma_start(
                        out=Bbb[which][:],
                        in_=rowsAB[2 * which + 1:2 * which + 2, :].partition_broadcast(128)), 16)
                gd_ab = val("GD")
                ld_raw, ve_n, ac_n, ld_c = {}, {}, {}, {}
                for j in range(32):
                    which, f = (0, j) if j < 16 else (1, j - 16)
                    src = qraw if which == 0 else kraw
                    gam = qg if which == 0 else kg
                    rb, ns = raws[j % 2], nstg[j % 2]
                    if j >= 2:
                        wait("sy", "VE", ve_n[j - 2])
                    ld_raw[j] = emi("sy", "LD", lambda e, rb=rb, src=src, f=f: e.dma_start(
                        out=rb[:], in_=src[f * 128:(f + 1) * 128, :]), 16)
                    wait("ve", "LD", ld_raw[j])
                    wait("ve", "GD", gd_ab)
                    if j >= 2:
                        wait("ve", "LD", ld_c[j - 2])
                    kb.semi("ve", "VE", lambda e, rb=rb, which=which: e.tensor_tensor(
                        out=tmp1[:], in0=rb[:], in1=Abb[which][:], op=ALU.mult))
                    kb.semi("ve", "VE", lambda e, which=which: e.tensor_tensor(
                        out=tmp1[:], in0=tmp1[:], in1=Bbb[which][:], op=ALU.add))
                    ve_n[j] = kb.semi("ve", "VE", lambda e, ns=ns, gam=gam, f=f: e.tensor_scalar_mul(
                        out=ns[:], in0=tmp1[:], scalar1=gam[:, f:f + 1]))
                    ac_n[j] = ve_n[j]
                    wait("sy", "VE", ve_n[j])
                    r0 = which * D + f * 128
                    ld_c[j] = emi("sy", "LD", lambda e, ns=ns, r0=r0: e.dma_start(
                        out=contrib[r0:r0 + 128, :], in_=ns[:]), 16)
                ld_contrib_done = val("LD")
            kb.phase_flush(block)

            kb.phase_begin()
            with contextlib.ExitStack() as ph:
                Ar = lambda *a: ph.enter_context(nc.sbuf_tensor(*a, side="right"))
                Pp = lambda n, s: ph.enter_context(nc.psum_tensor(n, s, F32))
                g_ps = [Pp(f"g_ps{i}", [128, 16]) for i in range(2)]
                cwt_ps = [Pp(f"cwt{i}", [16, 128]) for i in range(2)]
                lg = Ar("lg", [128, 16], F32)
                m1 = Ar("m1", [128, 1], F32)
                pr = Ar("pr", [128, 16], F32)
                gs = Ar("gs", [128, 4], F32)
                gm = Ar("gm", [128, 4], F32)
                mp = Ar("mp", [128, 16], F32)
                mpw = Ar("mpw", [128, 16], F32)
                cw = Ar("cw", [128, 16], F32)
                s1 = Ar("s1", [128, 1], F32)

                ve_g, pe_gmm = {}, {}
                wait("gp", "VE", xn_ready)
                gd_gwt = emi("gp", "GD", lambda e: e.dma_start(
                    out=gwT_s[:].rearrange("p (n c) -> p n c", c=16),
                    in_=gwT.rearrange("(n p) c -> p n c", p=128)), 16)
                wait("pe", "GD", gd_gwt)
                for tc in range(16):
                    bank = g_ps[tc % 2]
                    if tc >= 2:
                        wait("pe", "VE", ve_g[tc - 2])
                    for d in range(NCH):
                        em("pe", lambda e, d=d, tc=tc: e.ldweights(
                            xn[:, d * TOK + tc * 128:d * TOK + (tc + 1) * 128]))
                        fn = lambda e, d=d, tc=tc, bank=bank: e.matmul(
                            out=bank[:], lhsT=xn[:, d * TOK + tc * 128:d * TOK + (tc + 1) * 128],
                            rhs=gwT_s[:, d * 16:(d + 1) * 16],
                            start=(d == 0), stop=(d == NCH - 1))
                        if d < NCH - 1:
                            em("pe", fn)
                        else:
                            emi("pe", "PE", fn)
                    wait("ve", "PE", val("PE"))
                    if tc >= 1:
                        wait("ve", "PE", pe_gmm[tc - 1])
                    SV = lambda fn: kb.semi("ve", "VE", fn)
                    SV(lambda e, bank=bank: e.reduce_max(out=m1[:], in_=bank[:], axis=AX.X))
                    SV(lambda e: e.tensor_scalar_mul(out=m1[:], in0=m1[:], scalar1=-1.0))
                    ve1 = SV(lambda e, bank=bank: e.tensor_copy(out=lg[:], in_=bank[:]))
                    wait("ac", "VE", ve1)
                    ac1 = emi("ac", "AC", lambda e: e.activation(
                        out=pr[:], in_=lg[:], func=AF.Exp, bias=m1[:]))
                    wait("ve", "AC", ac1)
                    SV(lambda e: e.reduce_sum(out=s1[:], in_=pr[:], axis=AX.X))
                    SV(lambda e: e.reciprocal(out=s1[:], in_=s1[:]))
                    SV(lambda e: e.tensor_scalar_mul(out=pr[:], in0=pr[:], scalar1=s1[:]))
                    SV(lambda e: e.tensor_tensor(
                        out=gs[:], in0=pr[:, 0::4], in1=pr[:, 1::4], op=ALU.max))
                    SV(lambda e: e.tensor_tensor(
                        out=gs[:], in0=gs[:], in1=pr[:, 2::4], op=ALU.max))
                    SV(lambda e: e.tensor_tensor(
                        out=gs[:], in0=gs[:], in1=pr[:, 3::4], op=ALU.max))
                    SV(lambda e: e.reduce_max(out=m1[:], in_=gs[:], axis=AX.X))
                    SV(lambda e: e.tensor_scalar(
                        out=gm[:], in0=gs[:], scalar1=m1[:], scalar2=None, op0=ALU.is_equal))
                    SV(lambda e: e.scalar_tensor_tensor(
                        out=gm[:], in0=gm[:], scalar=NEGBIG, in1=gs[:], op0=ALU.mult, op1=ALU.add))
                    SV(lambda e: e.reduce_max(out=m1[:], in_=gm[:], axis=AX.X))
                    SV(lambda e: e.tensor_scalar(
                        out=gm[:], in0=gs[:], scalar1=m1[:], scalar2=None, op0=ALU.is_ge))
                    for k4 in range(4):
                        SV(lambda e, k4=k4: e.tensor_tensor(
                            out=mp[:, k4::4], in0=pr[:, k4::4], in1=gm[:], op=ALU.mult))
                    SV(lambda e: e.tensor_copy(out=mpw[:], in_=mp[:]))
                    for _ in range(3):
                        SV(lambda e: e.reduce_max(out=m1[:], in_=mpw[:], axis=AX.X))
                        SV(lambda e: e.tensor_scalar(
                            out=cw[:], in0=mpw[:], scalar1=m1[:], scalar2=None, op0=ALU.is_equal))
                        SV(lambda e: e.scalar_tensor_tensor(
                            out=mpw[:], in0=cw[:], scalar=NEGBIG, in1=mpw[:],
                            op0=ALU.mult, op1=ALU.add))
                    SV(lambda e: e.reduce_max(out=m1[:], in_=mpw[:], axis=AX.X))
                    SV(lambda e: e.tensor_scalar(
                        out=cw[:], in0=mp[:], scalar1=m1[:], scalar2=None, op0=ALU.is_ge))
                    SV(lambda e: e.tensor_tensor(out=cw[:], in0=cw[:], in1=mp[:], op=ALU.mult))
                    SV(lambda e: e.reduce_sum(out=s1[:], in_=cw[:], axis=AX.X))
                    SV(lambda e: e.tensor_scalar_add(out=s1[:], in0=s1[:], scalar1=1.0e-20))
                    SV(lambda e: e.reciprocal(out=s1[:], in_=s1[:]))
                    ve_cw = SV(lambda e: e.tensor_scalar_mul(
                        out=cw[:], in0=cw[:], scalar1=s1[:]))
                    cb = cwt_ps[tc % 2]
                    wait("pe", "VE", ve_cw)
                    pe_gmm[tc] = emi("pe", "PE", lambda e, cb=cb: e.matmul(
                        out=cb[:], lhsT=cw[:], rhs=idf[:], start=True, stop=True))
                    wait("ve", "PE", pe_gmm[tc])
                    ve_g[tc] = emi("ve", "VE", lambda e, cb=cb, tc=tc: e.tensor_copy(
                        out=cwT_s[:, tc * 128:(tc + 1) * 128], in_=cb[:]))
                wait("sy", "VE", ve_g[15])
                ld_cwt = emi("sy", "LD", lambda e: e.dma_start(
                    out=cwT_d[0:16, :], in_=cwT_s[:]), 16)
            kb.phase_flush(block)

            kb.phase_begin()
            wait("gp", "LD", ld_contrib_done)
            for s_ in ("GD", "UP", "DN", "AA", "CW"):
                wait("gp", s_, val(s_))
            cc1 = emi("gp", "CC", lambda e: e.collective_compute(
                "AllGather", ALU.bypass,
                replica_groups=[[0, 1, 2, 3], [4, 5, 6, 7]],
                ins=[contrib[:, :].opt()], outs=[gath[:, :].opt()]))
            kb.phase_flush(block)

        st_ffn = {"piece_free": {}, "down_free": {}, "down_ac": {}, "cwb_free": {},
                  "up_prev_ve": {}, "down_strip_free": {}, "acc_cell": {},
                  "gd_piece": {}, "cwb_mm": {}, "cwb_ac_prev": {}}

        def make_piece_emitter(bufs):
            (upg_sl, dwn, ab_, cwb, t1, fstage, g_ps, u_ps, d_banks, cwrow, cw_ps) = bufs
            nsl = len(upg_sl)

            def emit_piece_dma(e_idx, i):
                if (e_idx, i) in st_ffn["gd_piece"]:
                    return
                sl = upg_sl[i % nsl]
                jprev = ICH - 1 - ((ICH - 1 - (i % nsl)) % nsl)
                prev = st_ffn["piece_free"].get((e_idx, i - nsl)) or \
                    st_ffn["piece_free"].get((e_idx - 1, jprev))
                if prev:
                    wait("gp", "PE", prev)
                emi("gp", "UP", lambda e, sl=sl, i=i, e_idx=e_idx: e.dma_start(
                    out=sl[:, 0:NCH * 128].rearrange("p (n c) -> p n c", c=128),
                    in_=upT[e_idx, :, i * 128:(i + 1) * 128].rearrange("(n p) c -> p n c", p=128)), 16)
                st_ffn["gd_piece"][(e_idx, i)] = emi("gp", "UP", lambda e, sl=sl, i=i, e_idx=e_idx: e.dma_start(
                    out=sl[:, NCH * 128:].rearrange("p (n c) -> p n c", c=128),
                    in_=upT[e_idx, :, 1408 + i * 128:1408 + (i + 1) * 128].rearrange("(n p) c -> p n c", p=128)), 16)
            return emit_piece_dma

        def ffn_expert(e_idx, first_expert, bufs, emit_piece_dma):
            (upg_sl, dwn, ab_, cwb, t1, fstage, g_ps, u_ps, d_banks, cwrow, cw_ps) = bufs
            nsl = len(upg_sl)
            if e_idx == first_expert:
                wait("gp", "CC", val("CC"))
            wait("sy", "LD", ld_cwt)
            prev_mm = st_ffn["cwb_mm"].get(e_idx - 1, 0)
            if prev_mm:
                wait("sy", "PE", prev_mm)
            ld_row = emi("sy", "LD", lambda e, e_idx=e_idx: e.dma_start(
                out=cwrow[:], in_=cwT_d[e_idx:e_idx + 1, :]), 16)
            prev_free = st_ffn["cwb_free"].get(e_idx - 1, 0)
            ac_cwb = 0
            for q4 in range(4):
                wait("pe", "LD", ld_row)
                if q4 == 0 and st_ffn["cwb_ac_prev"].get(e_idx - 1):
                    wait("pe", "AC", st_ffn["cwb_ac_prev"][e_idx - 1])
                pe_b = emi("pe", "PE", lambda e, q4=q4: e.matmul(
                    out=cw_ps[:], lhsT=ones_f[:], rhs=cwrow[0:1, q4 * 512:(q4 + 1) * 512],
                    start=True, stop=True))
                wait("ac", "PE", pe_b)
                if q4 == 0 and prev_free:
                    wait("ac", "VE", prev_free)
                ac_cwb = emi("ac", "AC", lambda e, q4=q4: e.activation(
                    out=cwb[:, q4 * 512:(q4 + 1) * 512], in_=cw_ps[:], func=AF.Copy))
                if q4 < 3:
                    wait("pe", "AC", ac_cwb)
            st_ffn["cwb_mm"][e_idx] = val("PE")
            st_ffn["cwb_ac_prev"][e_idx] = ac_cwb
            gd_cwb = ac_cwb
            for i in range(nsl):
                emit_piece_dma(e_idx, i)
            ve_a = {}
            first_up_wait = max(st_ffn["down_free"].get(e_idx - 1, 0), 0)
            ac_prev_expert = st_ffn["down_ac"].get(e_idx - 1, 0)
            for i in range(ICH):
                sl = upg_sl[i % nsl]
                for ts in range(4):
                    par = (i * 4 + ts) % 2
                    gb, ub = g_ps[par], u_ps[par]
                    wait("pe", "UP", st_ffn["gd_piece"][(e_idx, i)])
                    if i == 0 and ts == 0:
                        if e_idx == first_expert and first_expert == 0:
                            wait("pe", "VE", xn_ready)
                        if ac_prev_expert:
                            wait("pe", "AC", ac_prev_expert)
                    prev = ve_a.get((i, ts - 2)) or ve_a.get((i - 1, ts + 2)) or \
                        st_ffn["up_prev_ve"].get(e_idx, 0)
                    if prev:
                        wait("pe", "VE", prev)
                    for half, bank in ((0, gb), (1, ub)):
                        off = half * NCH * 128
                        for d in range(NCH):
                            em("pe", lambda e, sl=sl, off=off, d=d: e.ldweights(
                                sl[:, off + d * 128:off + (d + 1) * 128]))
                            fn = lambda e, sl=sl, off=off, d=d, ts=ts, bank=bank: e.matmul(
                                out=bank[:], lhsT=sl[:, off + d * 128:off + (d + 1) * 128],
                                rhs=xn[:, d * TOK + ts * 512:d * TOK + (ts + 1) * 512],
                                start=(d == 0), stop=(d == NCH - 1))
                            if d < NCH - 1:
                                em("pe", fn)
                            else:
                                emi("pe", "PE", fn)
                    pe_ui = val("PE")
                    wait("ac", "PE", pe_ui)
                    if (i - 1, ts) in ve_a:
                        wait("ac", "VE", ve_a[(i - 1, ts)])
                    ac_s = emi("ac", "AC", lambda e, gb=gb, ts=ts: e.activation(
                        out=t1[:, ts * 512:(ts + 1) * 512], in_=gb[:], func=AF.Silu))
                    wait("ve", "AC", ac_s)
                    wait("ve", "AC", gd_cwb)
                    if i == 0 and ts == 0:
                        dpf = st_ffn["down_free"].get(e_idx - 1, 0)
                        if dpf:
                            wait("ve", "PE", dpf)
                    kb.semi("ve", "VE", lambda e, ub=ub, ts=ts: e.tensor_tensor(
                        out=t1[:, ts * 512:(ts + 1) * 512],
                        in0=t1[:, ts * 512:(ts + 1) * 512], in1=ub[:], op=ALU.mult))
                    ve_a[(i, ts)] = kb.semi("ve", "VE", lambda e, i=i, ts=ts: e.tensor_tensor(
                        out=ab_[:, i * TOK + ts * 512:i * TOK + (ts + 1) * 512],
                        in0=t1[:, ts * 512:(ts + 1) * 512],
                        in1=cwb[:, ts * 512:(ts + 1) * 512], op=ALU.mult))
                st_ffn["piece_free"][(e_idx, i)] = val("PE")
                if i + nsl < ICH:
                    emit_piece_dma(e_idx, i + nsl)
            st_ffn["up_prev_ve"][e_idx + 1] = val("VE")
            st_ffn["cwb_free"][e_idx] = val("VE")
            ve_a_done = val("VE")
            ac_ev, gd_acc = {}, {}
            gd_down = {}

            def emit_down_dma(dsi, guard_pe):
                dw = dwn[dsi % 2]
                if guard_pe:
                    wait("gp", "PE", guard_pe)
                for i in range(ICH):
                    gd_down[dsi] = emi("gp", "DN", lambda e, i=i, dsi=dsi, dw=dw, e_idx=e_idx: e.dma_start(
                        out=dw[:, i * 512:(i + 1) * 512],
                        in_=downT[e_idx, i * 128:(i + 1) * 128, dsi * 512:(dsi + 1) * 512]), 16)

            for dsi in (0, 1):
                emit_down_dma(dsi, st_ffn["down_strip_free"].get((e_idx - 1, dsi + 2), 0))
            for dsi in range(4):
                dw = dwn[dsi % 2]
                if dsi == 2 and e_idx + 1 < NE and not st_ffn.get("no_prefetch"):
                    emit_piece_dma(e_idx + 1, 0)
                    emit_piece_dma(e_idx + 1, 1)
                wait("pe", "DN", gd_down[dsi])
                if dsi == 0:
                    wait("pe", "VE", ve_a_done)
                for sub in range(16):
                    par = sub % 2
                    bank = d_banks[par]
                    prev = ac_ev.get((dsi, sub - 2)) or ac_ev.get((dsi - 1, sub + 14))
                    if prev:
                        wait("pe", "AC", prev)
                    for i in range(ICH):
                        em("pe", lambda e, i=i, sub=sub: e.ldweights(
                            ab_[:, i * TOK + sub * 128:i * TOK + (sub + 1) * 128]))
                        fn = lambda e, i=i, sub=sub, dsi=dsi, bank=bank, dw=dw: e.matmul(
                            out=bank[:], lhsT=ab_[:, i * TOK + sub * 128:i * TOK + (sub + 1) * 128],
                            rhs=dw[:, i * 512:(i + 1) * 512],
                            start=(i == 0), stop=(i == ICH - 1))
                        if i < ICH - 1:
                            em("pe", fn)
                        else:
                            emi("pe", "PE", fn)
                    pe_sub = val("PE")
                    fs = fstage[par]
                    wait("ac", "PE", pe_sub)
                    prevg = gd_acc.get((dsi, sub - 2)) or gd_acc.get((dsi - 1, sub + 14))
                    if prevg:
                        wait("ac", "AA", prevg)
                    ac_ev[(dsi, sub)] = emi("ac", "AC", lambda e, fs=fs, bank=bank: e.activation(
                        out=fs[:], in_=bank[:], func=AF.Copy))
                    wait("gp", "AC", ac_ev[(dsi, sub)])
                    prev_cell = st_ffn["acc_cell"].get((dsi, sub), 0)
                    if prev_cell:
                        wait("gp", "AA", prev_cell)
                    op = ALU.bypass if (e_idx == 0 and first_expert == 0) else ALU.add
                    gd_acc[(dsi, sub)] = emi("gp", "AA", lambda e, fs=fs, sub=sub, dsi=dsi, op=op: e.dma_start(
                        out=ffnacc[sub * 128:(sub + 1) * 128, dsi * 512:(dsi + 1) * 512],
                        in_=fs[:], accum_op=op), 16)
                    st_ffn["acc_cell"][(dsi, sub)] = gd_acc[(dsi, sub)]
                st_ffn["down_strip_free"][(e_idx, dsi)] = val("PE")
                if dsi + 2 < 4:
                    emit_down_dma(dsi + 2, st_ffn["down_strip_free"][(e_idx, dsi)])
            st_ffn["down_free"][e_idx] = val("PE")
            st_ffn["down_ac"][e_idx] = val("AC")
            return val("AA")

        def ffn_phase_bufs(ph, tag):
            Ar = lambda *a: ph.enter_context(nc.sbuf_tensor(*a, side="right"))
            Pp = lambda n, s: ph.enter_context(nc.psum_tensor(n, s, F32))
            upg_sl = [Ar(f"upsl{i}{tag}", [128, 2 * NCH * 128], BF16) for i in range(2)]
            dwn = [Ar(f"dwn{i}{tag}", [128, ICH * 512], BF16) for i in range(2)]
            ab_ = Ar(f"ab{tag}", [128, ICH * TOK], BF16)
            cwb = Ar(f"cwb{tag}", [128, TOK], F32)
            cwrow = Ar(f"cwrow{tag}", [1, TOK], F32)
            t1 = Ar(f"t1{tag}", [128, TOK], F32)
            fstage = [Ar(f"fst{i}{tag}", [128, 512], F32) for i in range(2)]
            g_ps = [Pp(f"gps{i}{tag}", [128, 512]) for i in range(2)]
            u_ps = [Pp(f"ups{i}{tag}", [128, 512]) for i in range(2)]
            d_banks = [Pp(f"dps{i}{tag}", [128, 512]) for i in range(2)]
            cw_ps = Pp(f"cwps{tag}", [128, 512])
            return (upg_sl, dwn, ab_, cwb, t1, fstage, g_ps, u_ps, d_banks, cwrow, cw_ps)

        gd_ffn_done = 0
        if PHASE_LIMIT >= 3:
            kb.phase_begin()
            with contextlib.ExitStack() as ph:
                bufs = ffn_phase_bufs(ph, "a")
                epd = make_piece_emitter(bufs)
                st_ffn["no_prefetch"] = False
                for e_i in range(E_SPLIT):
                    st_ffn["no_prefetch"] = (e_i == E_SPLIT - 1)
                    gd_ffn_done = ffn_expert(e_i, 0, bufs, epd)
                if "ab_sb" in DEBUG_TAPS:
                    abdbg = nc.dram_tensor("dbg_ab", [128, ICH * TOK], BF16,
                                           kind="ExternalOutput").ap()
                    wait("sy", "VE", val("VE"))
                    emi("sy", "OUT", lambda e: e.dma_start(
                        out=abdbg[:, :], in_=bufs[2][:]), 16)
            kb.phase_flush(block)

        ld_ctx_done = 0
        cc2 = 0
        if PHASE_LIMIT >= 4:
            kb.phase_begin()
            with contextlib.ExitStack() as ph:
                Ar = lambda *a: ph.enter_context(nc.sbuf_tensor(*a, side="right"))
                Pp = lambda n, s, dt=F32: ph.enter_context(nc.psum_tensor(n, s, dt))
                qh = Ar("qh", [128, S], BF16)
                kh = Ar("kh", [128, S], BF16)
                vf = Ar("vf", [128, S], BF16)
                vt = Ar("vt", [128, S], BF16)
                ssb = Ar("ssb", [128, 512], F32)
                ssb2 = Ar("ssb2", [128, 512], F32)
                mq = Ar("mq", [128, 4], F32)
                msl = Ar("msl", [128, 16], F32)
                nrow = Ar("nrow", [1, 512], BF16)
                nmb = Ar("nmb", [128, 512], F32)
                rlrow = Ar("rlrow", [1, 512], F32)
                rlb = Ar("rlb", [128, 512], F32)
                pt = [Ar(f"pt{i}", [128, 512], BF16) for i in range(2)]
                ctxo = Ar("ctxo", [128, 512], BF16)
                sps = [Pp(f"sps{i}", [128, 512]) for i in range(2)]
                stp = [Pp(f"stp{i}", [128, 512]) for i in range(2)]
                ctx_ps = Pp("ctx_ps", [128, 512])
                br_ps = Pp("br_ps", [128, 512])
                vt_ps = [Pp(f"vtp{i}", [128, 512], BF16) for i in range(2)]

                kb.last_ctxo_ve = 0
                kb.last_ctx_ld = 0
                kb.last_rlb_ac = 0
                kb.last_pe_tr = 0
                kb.passa_prev_red = 0
                kb.passa_prev_combine = 0
                wait("ve", "LD", ld_consts)
                wait("gp", "LD", ld_consts)
                pair_pe, pair_ve, pair_ac, pair_pl = 0, 0, 0, 0
                for pair in range(4):
                    wait("gp", "CC", cc1)
                    if pair > 0:
                        wait("gp", "PE", pair_pe)
                        wait("gp", "VE", pair_ve)
                        wait("gp", "AC", pair_ac)
                    for t_i, dst in ((0, qh), (1, kh), (2, vf)):
                        for si in range(4):
                            col = (pair * 3 + t_i) * 4 + si
                            emi("gp", "GD", lambda e, dst=dst, si=si, col=col: e.indirect_dma_start(
                                out=dst[:, si * TOK:(si + 1) * TOK], out_offset=None,
                                in_=gath[:, :],
                                in_offset=IndirectOffsetOnAxis(aidx[:, col:col + 1], 0)), 16)
                    gd_qkv = val("GD")
                    wait("pe", "GD", gd_qkv)
                    ve_vt = {}
                    for g in range(16):
                        bank = vt_ps[g % 2]
                        if g >= 2:
                            wait("pe", "VE", ve_vt[g - 2])
                        for b4 in range(4):
                            kk = g * 4 + b4
                            fn = lambda e, kk=kk, b4=b4, bank=bank: e.transpose(
                                out=bank[:, b4 * 128:(b4 + 1) * 128],
                                in_=vf[:, kk * 128:(kk + 1) * 128], identity=idb[:])
                            if b4 < 3:
                                em("pe", fn)
                            else:
                                emi("pe", "PE", fn)
                        wait("ve", "PE", val("PE"))
                        ve_vt[g] = emi("ve", "VE", lambda e, g=g, bank=bank: e.tensor_copy(
                            out=vt[:, g * 512:(g + 1) * 512], in_=bank[:]))
                    ve_vt_done = val("VE")

                    def passA_units(st):
                        units = []
                        state = {"prev_combine": kb.passa_prev_combine}
                        for b in range(4):
                            qt = st * 4 + b
                            nkc = qt // 4 + 1
                            ve_red = {}
                            for kc in range(nkc):
                                def unit(b=b, qt=qt, kc=kc, nkc=nkc, ve_red=ve_red, state=state):
                                    bank = sps[kc % 2]
                                    if kc >= 2:
                                        wait("pe", "VE", ve_red[kc - 2])
                                    elif kb.passa_prev_red:
                                        wait("pe", "VE", kb.passa_prev_red)
                                    em("pe", lambda e, qt=qt: e.ldweights(
                                        qh[:, qt * 128:(qt + 1) * 128]))
                                    pe_s = emi("pe", "PE", lambda e, qt=qt, kc=kc, bank=bank: e.matmul(
                                        out=bank[:], lhsT=qh[:, qt * 128:(qt + 1) * 128],
                                        rhs=kh[:, kc * 512:(kc + 1) * 512], start=True, stop=True))
                                    wait("ve", "PE", pe_s)
                                    if kc == 0 and state["prev_combine"]:
                                        wait("ve", "VE", state["prev_combine"])
                                    if kc == nkc - 1:
                                        emi("ve", "VE", lambda e, bank=bank, b=b: e.scalar_tensor_tensor(
                                            out=ssb2[:], in0=bank[:], scalar=1.0,
                                            in1=amask[:, b * 512:(b + 1) * 512],
                                            op0=ALU.mult, op1=ALU.add))
                                        wait("ve", "VE", val("VE"))
                                        ve_red[kc] = emi("ve", "VE", lambda e, kc=kc: e.reduce_max(
                                            out=msl[:, kc:kc + 1], in_=ssb2[:], axis=AX.X))
                                        kb.passa_prev_red = ve_red[kc]
                                        wait("ve", "VE", val("VE"))
                                        if kb.last_pe_tr:
                                            wait("ve", "PE", kb.last_pe_tr)
                                        state["prev_combine"] = emi("ve", "VE",
                                            lambda e, b=b, nkc=nkc: e.reduce_max(
                                                out=mq[:, b:b + 1], in_=msl[:, 0:nkc], axis=AX.X))
                                        kb.passa_prev_combine = state["prev_combine"]
                                    else:
                                        ve_red[kc] = emi("ve", "VE", lambda e, bank=bank, kc=kc: e.reduce_max(
                                            out=msl[:, kc:kc + 1], in_=bank[:], axis=AX.X))
                                        kb.passa_prev_red = ve_red[kc]
                                units.append(unit)
                        return units

                    def emit_negm(st):
                        wait("pe", "VE", kb.passa_prev_combine)
                        if kb.last_rlb_ac:
                            wait("pe", "AC", kb.last_rlb_ac)
                        pe_tr = 0
                        for b in range(4):
                            fn = lambda e, b=b: e.transpose(
                                out=br_ps[0:1, b * 128:(b + 1) * 128],
                                in_=mq[:, b:b + 1], identity=idf[:])
                            if b < 3:
                                em("pe", fn)
                            else:
                                pe_tr = emi("pe", "PE", fn)
                        kb.last_pe_tr = pe_tr
                        wait("ve", "PE", pe_tr)
                        return emi("ve", "VE", lambda e: e.tensor_scalar_mul(
                            out=nrow[:], in0=br_ps[0:1, 0:512], scalar1=-1.0))

                    def passB_units(st, ve_nrow, shared):
                        nkc8 = 4 * (st + 1)
                        ve_es, ac_pt, ctx_pe, pl_pt = {}, {}, {}, {}
                        shared["ctx_pe"] = ctx_pe
                        shared["nkc8"] = nkc8

                        def emit_ctx_l(kc):
                            p = pt[kc % 2]
                            if kc >= 4 * st:
                                wait("pe", "PL", pl_pt[kc])
                            else:
                                wait("pe", "AC", ac_pt[kc])
                            em("pe", lambda e, kc=kc: e.ldweights(vt[:, kc * 128:(kc + 1) * 128]))
                            em("pe", lambda e, kc=kc, p=p, nkc8=nkc8: e.matmul(
                                out=ctx_ps[:], lhsT=vt[:, kc * 128:(kc + 1) * 128], rhs=p[:],
                                start=(kc == 0), stop=(kc == nkc8 - 1),
                                skip_group_check=True))
                            em("pe", lambda e: e.ldweights(ones_b[:]))
                            ctx_pe[kc] = emi("pe", "PE", lambda e, kc=kc, p=p, nkc8=nkc8: e.matmul(
                                out=br_ps[0:1, :], lhsT=ones_b[:], rhs=p[:],
                                start=(kc == 0), stop=(kc == nkc8 - 1),
                                skip_group_check=True))

                        units = []
                        for kc in range(nkc8):
                            def unit(kc=kc, st=st, ve_nrow=ve_nrow):
                                bank = stp[kc % 2]
                                if kc == 0:
                                    wait("pe", "VE", max(ve_vt_done, kb.last_ctxo_ve, ve_nrow))
                                prev_exp = ac_pt.get(kc - 2)
                                if prev_exp:
                                    wait("pe", "AC", prev_exp)
                                em("pe", lambda e: e.ldweights(ones_fb[:]))
                                em("pe", lambda e, st=st, bank=bank: e.matmul(
                                    out=bank[:], lhsT=ones_fb[:],
                                    rhs=nrow[0:1, :], start=True, stop=False,
                                    skip_group_check=True))
                                em("pe", lambda e, kc=kc: e.ldweights(kh[:, kc * 128:(kc + 1) * 128]))
                                pe_st = emi("pe", "PE", lambda e, kc=kc, st=st, bank=bank: e.matmul(
                                    out=bank[:], lhsT=kh[:, kc * 128:(kc + 1) * 128],
                                    rhs=qh[:, st * 512:(st + 1) * 512], start=False, stop=True,
                                    skip_group_check=True))
                                wait("ac", "PE", pe_st)
                                prev_pe = ctx_pe.get(kc - 2)
                                if prev_pe:
                                    wait("ac", "PE", prev_pe)
                                p = pt[kc % 2]
                                ac_pt[kc] = emi("ac", "AC", lambda e, p=p, bank=bank: e.activation(
                                    out=p[:], in_=bank[:], func=AF.Exp))
                                if kc >= 4 * st:
                                    wait("gp", "AC", ac_pt[kc])
                                    pl_pt[kc] = emi("gp", "PL", lambda e, p=p, st=st, kc=kc: e.affine_select(
                                        out=p[:], in_=p[:], pattern=[[1, 512]],
                                        base=512 * st - 128 * kc, channel_multiplier=-1,
                                        compare_op=ALU.is_ge, fill=0.0))
                                if kc >= 1:
                                    emit_ctx_l(kc - 1)
                                if kc == nkc8 - 1:
                                    emit_ctx_l(kc)
                            units.append(unit)
                        return units

                    def finish_strip(st, shared):
                        ctx_pe = shared["ctx_pe"]
                        nkc8 = shared["nkc8"]
                        wait("ve", "PE", ctx_pe[nkc8 - 1])
                        ve_rl = emi("ve", "VE", lambda e: e.reciprocal(
                            out=rlrow[:], in_=br_ps[0:1, :]))
                        wait("pe", "VE", ve_rl)
                        pe_rlb = emi("pe", "PE", lambda e: e.matmul(
                            out=br_ps[:], lhsT=ones_f[:], rhs=rlrow[:], start=True, stop=True))
                        wait("ac", "PE", pe_rlb)
                        kb.last_rlb_ac = emi("ac", "AC", lambda e: e.activation(
                            out=rlb[:], in_=br_ps[:], func=AF.Copy))
                        wait("ve", "AC", kb.last_rlb_ac)
                        if kb.last_ctx_ld:
                            wait("ve", "LD", kb.last_ctx_ld)
                        kb.last_ctxo_ve = emi("ve", "VE", lambda e: e.tensor_tensor(
                            out=ctxo[:], in0=ctx_ps[:], in1=rlb[:], op=ALU.mult))
                        wait("sy", "VE", kb.last_ctxo_ve)
                        rj, stl = st // 4, st % 4
                        r0 = 512 * rj + 128 * pair
                        kb.last_ctx_ld = emi("sy", "LD", lambda e, r0=r0, stl=stl: e.dma_start(
                            out=ctxc[r0:r0 + 128, stl * 512:(stl + 1) * 512], in_=ctxo[:]), 16)

                    for u in passA_units(0):
                        u()
                    for st_i in range(16):
                        ve_nrow = emit_negm(st_i)
                        ua = passA_units(st_i + 1) if st_i < 15 else []
                        shared = {}
                        ub = passB_units(st_i, ve_nrow, shared)
                        ia, ib = 0, 0
                        while ia < len(ua) or ib < len(ub):
                            if ib < len(ub):
                                ub[ib]()
                                ib += 1
                            if ia < len(ua):
                                ua[ia]()
                                ia += 1
                        finish_strip(st_i, shared)
                    pair_pe, pair_ve, pair_ac = val("PE"), val("VE"), val("AC")
                ld_ctx_done = val("LD")
            kb.phase_flush(block)

            kb.phase_begin()
            wait("gp", "LD", ld_ctx_done)
            for s_ in ("GD", "UP", "DN", "AA", "CW"):
                wait("gp", s_, val(s_))
            cc2 = emi("gp", "CC", lambda e: e.collective_compute(
                "AllGather", ALU.bypass,
                replica_groups=[[0, 1, 2, 3], [4, 5, 6, 7]],
                ins=[ctxc[:, :].opt()], outs=[gath2[:, :].opt()]))
            kb.phase_flush(block)

        if PHASE_LIMIT >= 5:
            kb.phase_begin()
            with contextlib.ExitStack() as ph:
                bufs = ffn_phase_bufs(ph, "b")
                st_ffn["piece_free"] = {}
                st_ffn["down_free"] = {}
                st_ffn["down_ac"] = {}
                st_ffn["cwb_free"] = {}
                st_ffn["up_prev_ve"] = {}
                st_ffn["down_strip_free"] = {}
                st_ffn["gd_piece"] = {}
                epd = make_piece_emitter(bufs)
                st_ffn["no_prefetch"] = False
                for e_i in range(E_SPLIT, NE):
                    st_ffn["no_prefetch"] = (e_i == NE - 1)
                    gd_ffn_done = ffn_expert(e_i, E_SPLIT, bufs, epd)
            kb.phase_flush(block)

        if PHASE_LIMIT >= 6:
            kb.phase_begin()
            with contextlib.ExitStack() as ph:
                Ar = lambda *a: ph.enter_context(nc.sbuf_tensor(*a, side="right"))
                Pp = lambda n, s: ph.enter_context(nc.psum_tensor(n, s, F32))
                oww = Ar("oww", [128, NCH * D], BF16)
                octx = Ar("octx", [128, NCH * 512], BF16)
                ffn_sb = [Ar(f"ffn_sb{i}", [128, D], F32) for i in range(2)]
                x_sb = [Ar(f"x_sb{i}", [128, D], F32) for i in range(2)]
                o_ps = [Pp(f"ops{i}", [128, 512]) for i in range(8)]

                wait("gp", "AA", gd_ffn_done)
                wait("sy", "AA", gd_ffn_done)
                for chn in range(NCH):
                    emi("gp", "GD", lambda e, chn=chn: e.dma_start(
                        out=oww[:, chn * D:(chn + 1) * D],
                        in_=outw[chn * 128:(chn + 1) * 128, :]), 16)
                gd_oww = val("GD")
                ve_fin = {}
                out_cnt = {}
                pe_strip = {}
                for st_i in range(4):
                    wait("gp", "CC", cc2)
                    if st_i >= 1:
                        wait("gp", "PE", pe_strip[st_i - 1])
                    for chn in range(NCH):
                        col = st_i * 16 + chn
                        emi("gp", "GD", lambda e, chn=chn, col=col: e.indirect_dma_start(
                            out=octx[:, chn * 512:(chn + 1) * 512], out_offset=None,
                            in_=gath2.rearrange("r (s c) -> (r s) c", c=512),
                            in_offset=IndirectOffsetOnAxis(eidx[:, col:col + 1], 0)), 16)
                    gd_ctx = val("GD")
                    for sub in range(4):
                        tok0 = st_i * 512 + sub * 128
                        par = sub % 2
                        prevv = ve_fin.get((st_i, sub - 2)) or ve_fin.get((st_i - 1, sub + 2))
                        if prevv:
                            wait("sy", "VE", prevv)
                        emi("sy", "LD", lambda e, par=par, tok0=tok0: e.dma_start(
                            out=ffn_sb[par][:], in_=ffnacc[tok0:tok0 + 128, :]), 16)
                        ld_f = emi("sy", "LD", lambda e, par=par, tok0=tok0: e.dma_start(
                            out=x_sb[par][:], in_=x_in[tok0:tok0 + 128, :]), 16)
                        wait("pe", "GD", gd_ctx)
                        wait("pe", "GD", gd_oww)
                        if prevv:
                            wait("pe", "VE", prevv)
                        for chn in range(NCH):
                            em("pe", lambda e, chn=chn, sub=sub: e.ldweights(
                                octx[:, chn * 512 + sub * 128:chn * 512 + (sub + 1) * 128]))
                            for dsi in range(4):
                                bank = o_ps[par * 4 + dsi]
                                fn = lambda e, chn=chn, sub=sub, dsi=dsi, bank=bank: e.matmul(
                                    out=bank[:],
                                    lhsT=octx[:, chn * 512 + sub * 128:chn * 512 + (sub + 1) * 128],
                                    rhs=oww[:, chn * D + dsi * 512:chn * D + (dsi + 1) * 512],
                                    start=(chn == 0), stop=(chn == NCH - 1),
                                    skip_group_check=True)
                                if not (chn == NCH - 1 and dsi == 3):
                                    em("pe", fn)
                                else:
                                    emi("pe", "PE", fn)
                        pe_sub = val("PE")
                        pe_strip[st_i] = pe_sub
                        wait("ve", "PE", pe_sub)
                        wait("ve", "LD", ld_f)
                        prev_out = out_cnt.get((st_i, sub - 2)) or out_cnt.get((st_i - 1, sub + 2))
                        if prev_out:
                            wait("ve", "OUT", prev_out)
                        for dsi in range(4):
                            bank = o_ps[par * 4 + dsi]
                            fn = lambda e, par=par, dsi=dsi, bank=bank: e.tensor_tensor(
                                out=ffn_sb[par][:, dsi * 512:(dsi + 1) * 512], in0=bank[:],
                                in1=ffn_sb[par][:, dsi * 512:(dsi + 1) * 512], op=ALU.add)
                            if dsi < 3:
                                em("ve", fn)
                            else:
                                emi("ve", "VE", fn)
                        ve_fin[(st_i, sub)] = kb.semi("ve", "VE", lambda e, par=par: e.tensor_tensor(
                            out=x_sb[par][:], in0=ffn_sb[par][:], in1=x_sb[par][:], op=ALU.add))
                        wait("sy", "VE", ve_fin[(st_i, sub)])
                        out_cnt[(st_i, sub)] = emi("sy", "OUT", lambda e, par=par, tok0=tok0: e.dma_start(
                            out=out_t[tok0:tok0 + 128, :], in_=x_sb[par][:]), 16)
            kb.phase_flush(block)

        kb.phase_begin()
        if "xn_sb" in DEBUG_TAPS:
            xndbg = nc.dram_tensor("dbg_xn", [128, NCH * TOK], BF16,
                                   kind="ExternalOutput").ap()
            for s in ("PE", "VE", "AC"):
                wait("sy", s, val(s))
            emi("sy", "OUT", lambda e: e.dma_start(out=xndbg[:, :], in_=xn[:]), 16)
        for name, (dst, src) in dbg.items():
            for s in ("GD", "LD", "CC", "PE", "VE", "AC", "PL"):
                wait("sy", s, val(s))
            emi("sy", "OUT", lambda e, dst=dst, src=src: e.dma_start(
                out=dst[:, :], in_=src[:, :]), 16)
        for s in ("GD", "LD", "CC", "PE", "VE", "AC", "PL", "OUT",
                  "CW", "UP", "DN", "AA"):
            wait("sy", s, val(s))
        for eng in ("pe", "ve", "ac", "gp"):
            wait(eng, "OUT", val("OUT"))
            wait(eng, "LD", val("LD"))
            wait(eng, "GD", val("GD"))
        kb.phase_flush(block)

    ctx.close()
    return nc


def _host_inputs(inputs):
    x = np.asarray(inputs["L0_attn_x"])
    wqkv = np.ascontiguousarray(np.asarray(inputs["L0_attn_Wqkv"], dtype=np.float32))
    outw = np.ascontiguousarray(np.asarray(inputs["L0_attn_out_W"], dtype=np.float32))
    qls = np.asarray(inputs["L0_attn_q_ln_scale"], dtype=np.float32)
    kls = np.asarray(inputs["L0_attn_k_ln_scale"], dtype=np.float32)
    gate = np.asarray(inputs["L0_ffn_gate_w"], dtype=np.float32)
    eup = np.asarray(inputs["L0_ffn_expert_up_w"], dtype=np.float32)
    edn = np.asarray(inputs["L0_ffn_expert_down_w"], dtype=np.float32)
    sup = np.asarray(inputs["L0_ffn_shared_up_w"], dtype=np.float32)
    sdn = np.asarray(inputs["L0_ffn_shared_down_w"], dtype=np.float32)

    upT = np.empty((NE, D, 2816), np.float32)
    upT[:16] = eup.transpose(0, 2, 1)
    supT = sup.T
    upT[16] = np.concatenate([supT[:, 0:1408], supT[:, 2816:4224]], axis=1)
    upT[17] = np.concatenate([supT[:, 1408:2816], supT[:, 4224:5632]], axis=1)
    downT = np.empty((NE, 1408, D), np.float32)
    downT[:16] = edn.transpose(0, 2, 1)
    sdnT = sdn.T
    downT[16] = sdnT[0:1408]
    downT[17] = sdnT[1408:2816]
    upT = np.ascontiguousarray(upT)
    downT = np.ascontiguousarray(downT)
    gwT = np.ascontiguousarray(gate.T)
    qgc = np.ascontiguousarray((qls / np.sqrt(128.0)).reshape(NCH, 128).T)
    kgc = np.ascontiguousarray(kls.reshape(NCH, 128).T)
    idb = np.eye(128, dtype=BF)
    idf = np.eye(128, dtype=np.float32)
    am = np.empty((128, 4 * 512), np.float32)
    ii = np.arange(128)[:, None]
    jj = np.arange(512)[None, :]
    for b in range(4):
        am[:, b * 512:(b + 1) * 512] = np.where((jj - ii) <= 128 * b, 0.0, NEGBIG)

    per_core = []
    for c in range(8):
        bb, rr = c // 4, c % 4
        xs = np.ascontiguousarray(x[bb, rr * TOK:(rr + 1) * TOK, :], dtype=np.float32)
        a_idx = np.empty((128, 48), np.int32)
        for p in range(4):
            for t_i in range(3):
                for si in range(4):
                    col = (p * 3 + t_i) * 4 + si
                    base = 6144 * si + 2048 * t_i + 512 * rr + 128 * p
                    a_idx[:, col] = base + np.arange(128)
        e_idx = np.empty((128, 64), np.int32)
        for st_i in range(4):
            for chn in range(16):
                si, j = divmod(chn, 4)
                base = 2048 * si + 512 * rr + 128 * j
                e_idx[:, st_i * 16 + chn] = 4 * (base + np.arange(128)) + st_i
        per_core.append({
            "x": xs, "wqkv": wqkv, "outw": outw, "upT": upT, "downT": downT,
            "gwT": gwT, "qg": qgc, "kg": kgc, "idb": idb, "idf": idf,
            "amask": am, "aidx": a_idx, "eidx": e_idx,
        })
    return per_core


def kernel(**inputs):
    per_core = _host_inputs(inputs)
    nc = build_program()
    res = run_bass_kernel_spmd(nc, per_core, core_ids=list(range(8)))
    y = np.empty((NB, S, D), np.float32)
    for c in range(8):
        bb, rr = c // 4, c % 4
        y[bb, rr * TOK:(rr + 1) * TOK, :] = res.results[c]["out"]
    return y


# version 11
# speedup vs baseline: 1.0540x; 1.0540x over previous
import contextlib
import numpy as np
import ml_dtypes
import concourse.bass as bass
import concourse.mybir as mybir
from concourse.bass_core_rust import IndirectOffsetOnAxis
from concourse.bass_utils import run_bass_kernel_spmd

F32 = mybir.dt.float32
BF16 = mybir.dt.bfloat16
I32 = mybir.dt.int32
AF = mybir.ActivationFunctionType
ALU = mybir.AluOpType
AX = mybir.AxisListType

D = 2048
S = 8192
NB = 2
TOK = 2048
NCH = D // 128
NE = 18
ICH = 11
NEGBIG = -1.0e30
E_SPLIT = 6
BF = ml_dtypes.bfloat16

DEBUG_TAPS = []
PHASE_LIMIT = 99


class KB:
    def __init__(self, nc, ctx):
        self.nc = nc
        self.ctx = ctx
        self.cnt = {}
        self.sems = {n: ctx.enter_context(nc.semaphore(n))
                     for n in ("LD", "GD", "PE", "VE", "AC", "PL", "CC", "OUT",
                               "CW", "UP", "DN", "AA")}
        self.ops = None

    def phase_begin(self):
        self.ops = {k: [] for k in ("pe", "ve", "ac", "gp", "sy")}

    def phase_flush(self, block):
        def runner(fns):
            return lambda e: [f(e) for f in fns]
        if self.ops["pe"]:
            block.tensor(runner(self.ops["pe"]))
        if self.ops["ve"]:
            block.vector(runner(self.ops["ve"]))
        if self.ops["ac"]:
            block.scalar(runner(self.ops["ac"]))
        if self.ops["gp"]:
            block.gpsimd(runner(self.ops["gp"]))
        if self.ops["sy"]:
            block.sync(runner(self.ops["sy"]))
        self.ops = None

    def em(self, eng, fn):
        self.ops[eng].append(fn)

    def wait(self, eng, sem, v):
        if v > 0:
            self.em(eng, lambda e, s=self.sems[sem], v=v: e.wait_ge(s, v))

    def bump(self, sem, k=1):
        self.cnt[sem] = self.cnt.get(sem, 0) + k
        return self.sems[sem], self.cnt[sem]

    def val(self, sem):
        return self.cnt.get(sem, 0)

    def emi(self, eng, sem, fn, k=1):
        h, _ = self.bump(sem, k)
        self.em(eng, lambda e, h=h, k=k, fn=fn: fn(e).then_inc(h, k))
        return self.val(sem)

    def semi(self, eng, sem, fn):
        self.wait(eng, sem, self.val(sem))
        return self.emi(eng, sem, fn)


def build_program():
    ctx = contextlib.ExitStack()
    nc = bass.Bass("TRN2", target_bir_lowering=False)
    kb = KB(nc, ctx)

    x_in = nc.dram_tensor("x", [TOK, D], F32, kind="ExternalInput").ap()
    wqkv = nc.dram_tensor("wqkv", [D, 3 * D], F32, kind="ExternalInput").ap()
    outw = nc.dram_tensor("outw", [D, D], F32, kind="ExternalInput").ap()
    upT = nc.dram_tensor("upT", [NE, D, 2816], F32, kind="ExternalInput").ap()
    downT = nc.dram_tensor("downT", [NE, 1408, D], F32, kind="ExternalInput").ap()
    gwT = nc.dram_tensor("gwT", [D, 16], F32, kind="ExternalInput").ap()
    qg_in = nc.dram_tensor("qg", [128, NCH], F32, kind="ExternalInput").ap()
    kg_in = nc.dram_tensor("kg", [128, NCH], F32, kind="ExternalInput").ap()
    idb_in = nc.dram_tensor("idb", [128, 128], BF16, kind="ExternalInput").ap()
    idf_in = nc.dram_tensor("idf", [128, 128], F32, kind="ExternalInput").ap()
    amask_in = nc.dram_tensor("amask", [128, 4 * 512], F32, kind="ExternalInput").ap()
    aidx_in = nc.dram_tensor("aidx", [128, 48], I32, kind="ExternalInput").ap()
    eidx_in = nc.dram_tensor("eidx", [128, 64], I32, kind="ExternalInput").ap()

    out_t = nc.dram_tensor("out", [TOK, D], F32, kind="ExternalOutput").ap()

    contrib = nc.dram_tensor("contrib", [3 * D, TOK], BF16).ap()
    gath = nc.dram_tensor("gath", [4 * 3 * D, TOK], BF16).ap()
    ctxc = nc.dram_tensor("ctxc", [TOK, TOK], BF16).ap()
    gath2 = nc.dram_tensor("gath2", [4 * TOK, TOK], BF16).ap()
    qraw = nc.dram_tensor("qraw", [D, TOK], BF16).ap()
    kraw = nc.dram_tensor("kraw", [D, TOK], BF16).ap()
    rowsAB = nc.dram_tensor("rowsAB", [4, TOK], F32).ap()
    cwT_d = nc.dram_tensor("cwT_d", [NE, TOK], F32).ap()
    ffnacc = nc.dram_tensor("ffnacc", [TOK, D], F32).ap()
    dbg = {}
    for name in DEBUG_TAPS:
        if name in ("xn_sb", "ab_sb"):
            continue
        src = {"contrib": contrib, "gath": gath, "ctxc": ctxc, "gath2": gath2,
               "qraw": qraw, "kraw": kraw, "rowsAB": rowsAB, "cwT_d": cwT_d,
               "ffnacc": ffnacc}[name]
        dbg[name] = (nc.dram_tensor("dbg_" + name, list(src.shape), src.dtype,
                                    kind="ExternalOutput").ap(), src)

    A = lambda *a, **k2: ctx.enter_context(nc.sbuf_tensor(*a, side="left", **k2))
    xn = A("xn", [128, NCH * TOK], BF16)
    idb = A("idb_s", [128, 128], BF16)
    idf = A("idf_s", [128, 128], F32)
    ones_b = A("ones_b", [128, 1], BF16)
    ones_f = A("ones_f", [1, 128], F32)
    ones_fb = A("ones_fb", [1, 128], BF16)
    ones_row = A("ones_row", [1, TOK], F32)
    qg = A("qg_s", [128, NCH], F32)
    kg = A("kg_s", [128, NCH], F32)
    amask = A("amask_s", [128, 4 * 512], F32)
    aidx = A("aidx_s", [128, 48], I32)
    eidx = A("eidx_s", [128, 64], I32)
    gwT_s = A("gwT_s", [128, NCH * 16], BF16)
    eps5 = A("eps5", [128, 1], F32)
    eps6 = A("eps6", [128, 1], F32)
    cwT_s = A("cwT_s", [16, TOK], F32)

    em, wait, emi, val = kb.em, kb.wait, kb.emi, kb.val

    with nc.Block() as block:
        kb.phase_begin()
        for dst, src in [(idb, idb_in), (idf, idf_in), (qg, qg_in), (kg, kg_in),
                         (amask, amask_in), (aidx, aidx_in), (eidx, eidx_in)]:
            emi("sy", "LD", lambda e, d=dst, s=src: e.dma_start(out=d[:], in_=s[:]), 16)
        ld_consts = val("LD")
        wait("ve", "LD", ld_consts)
        em("ve", lambda e: e.memset(ones_b[:], 1.0))
        em("ve", lambda e: e.memset(ones_f[:], 1.0))
        em("ve", lambda e: e.memset(ones_fb[:], 1.0))
        em("ve", lambda e: e.memset(eps5[:], 1.0e-5))
        em("ve", lambda e: e.memset(eps6[:], 1.0e-6))
        ve_ones = emi("ve", "VE", lambda e: e.memset(ones_row[:], 1.0))
        wait("sy", "VE", ve_ones)
        for r in (16, 17):
            emi("sy", "LD", lambda e, r=r: e.dma_start(
                out=cwT_d[r:r + 1, :], in_=ones_row[:]), 16)
        kb.phase_flush(block)

        kb.phase_begin()
        with contextlib.ExitStack() as ph:
            Ar = lambda *a: ph.enter_context(nc.sbuf_tensor(*a, side="right"))
            xbuf = [Ar(f"xb{i}", [128, D], F32) for i in range(2)]
            xnt = [Ar(f"xnt{i}", [128, D], BF16) for i in range(2)]
            sq = [Ar(f"sq1a{i}", [128, D], F32) for i in range(2)]
            ssum = Ar("ssum", [128, 2], F32)
            mean = Ar("mean", [128, 2], F32)
            msq = Ar("msq", [128, 2], F32)
            var = Ar("var", [128, 2], F32)
            sd = Ar("sd", [128, 2], F32)
            nb_ = Ar("nb", [128, 2], F32)
            tps = [ph.enter_context(nc.psum_tensor(f"tp{i}", [128, 512], BF16))
                   for i in range(2)]

            ac_done, pe_done = {}, {}
            ve_copy = {}
            wait("pe", "LD", ld_consts)
            wait("ve", "LD", ld_consts)

            def tile_units(tc):
                buf, xb = xbuf[tc % 2], xnt[tc % 2]
                par = tc % 2
                st8 = {}
                units = []

                def u_load():
                    if tc >= 2:
                        wait("sy", "VE", ac_done[tc - 2])
                    st8["ld"] = emi("sy", "LD", lambda e, b=buf, tc=tc: e.dma_start(
                        out=b[:], in_=x_in[tc * 128:(tc + 1) * 128, :]), 16)
                units.append(u_load)

                def u_sum():
                    wait("ve", "LD", st8["ld"])
                    kb.semi("ve", "VE", lambda e, b=buf: e.reduce_sum(
                        out=ssum[:, par:par + 1], in_=b[:], axis=AX.X))
                units.append(u_sum)

                def u_sq():
                    wait("ac", "LD", st8["ld"])
                    st8["sq"] = emi("ac", "AC", lambda e, b=buf: e.activation(
                        out=sq[par][:], in_=b[:], func=AF.Square))
                units.append(u_sq)

                def u_sum2():
                    wait("ve", "AC", st8["sq"])
                    st8["s1"] = kb.semi("ve", "VE", lambda e: e.reduce_sum(
                        out=msq[:, par:par + 1], in_=sq[par][:], axis=AX.X))
                units.append(u_sum2)

                def u_var():
                    kb.semi("ve", "VE", lambda e: e.tensor_scalar_mul(
                        out=mean[:, par:par + 1], in0=ssum[:, par:par + 1], scalar1=1.0 / D))
                    kb.semi("ve", "VE", lambda e: e.tensor_scalar_mul(
                        out=var[:, par:par + 1], in0=msq[:, par:par + 1], scalar1=1.0 / D))
                    kb.semi("ve", "VE", lambda e: e.scalar_tensor_tensor(
                        out=sd[:, par:par + 1], in0=mean[:, par:par + 1], scalar=-1.0,
                        in1=mean[:, par:par + 1], op0=ALU.mult, op1=ALU.mult))
                    st8["v1"] = kb.semi("ve", "VE", lambda e: e.tensor_tensor(
                        out=var[:, par:par + 1], in0=var[:, par:par + 1],
                        in1=sd[:, par:par + 1], op=ALU.add))
                units.append(u_var)

                def u_sqrt():
                    wait("ac", "VE", st8["v1"])
                    st8["sd"] = emi("ac", "AC", lambda e: e.activation(
                        out=sd[:, par:par + 1], in_=var[:, par:par + 1],
                        func=AF.Sqrt, bias=eps5[:]))
                units.append(u_sqrt)

                def u_norm():
                    wait("ve", "AC", st8["sd"])
                    kb.semi("ve", "VE", lambda e: e.reciprocal(
                        out=sd[:, par:par + 1], in_=sd[:, par:par + 1]))
                    kb.semi("ve", "VE", lambda e: e.scalar_tensor_tensor(
                        out=nb_[:, par:par + 1], in0=mean[:, par:par + 1], scalar=-1.0,
                        in1=sd[:, par:par + 1], op0=ALU.mult, op1=ALU.mult))
                    if tc >= 2:
                        wait("ve", "PE", pe_done[tc - 2])
                    ac_done[tc] = kb.semi("ve", "VE", lambda e, b=buf, xb=xb: e.tensor_scalar(
                        out=xb[:], in0=b[:], scalar1=sd[:, par:par + 1],
                        scalar2=nb_[:, par:par + 1], op0=ALU.mult, op1=ALU.add))
                units.append(u_norm)

                def u_trans():
                    wait("pe", "VE", ac_done[tc])
                    for g in range(4):
                        bank = tps[g % 2]
                        prev = (tc, g - 2) if g >= 2 else (tc - 1, g + 2)
                        if prev in ve_copy:
                            wait("pe", "VE", ve_copy[prev])
                        for d4 in range(4):
                            d = g * 4 + d4
                            fn = lambda e, d=d, d4=d4, bank=bank, xb=xb: e.transpose(
                                out=bank[:, d4 * 128:(d4 + 1) * 128],
                                in_=xb[:, d * 128:(d + 1) * 128], identity=idb[:])
                            if d4 < 3:
                                em("pe", fn)
                            else:
                                emi("pe", "PE", fn)
                        wait("ve", "PE", val("PE"))
                        ve_copy[(tc, g)] = emi("ve", "VE", lambda e, tc=tc, g=g, bank=bank: e.tensor_copy(
                            out=xn[:].rearrange("p (n t) -> p n t", t=TOK)[:, 4 * g:4 * g + 4, tc * 128:(tc + 1) * 128],
                            in_=bank[:].rearrange("p (n t) -> p n t", t=128)))
                    pe_done[tc] = val("PE")
                units.append(u_trans)
                return units

            pend = None
            for tc in range(0, TOK // 128, 2):
                ua = tile_units(tc)
                ub = tile_units(tc + 1)
                order = [ua[0], ub[0], ua[2], ua[1], ub[2], ua[3], ub[1], ua[4],
                         ub[3], ua[5], ub[4], ua[6], ub[5], ua[7], ub[6], ub[7]]
                for u in order:
                    u()
            xn_ready = ve_copy[(TOK // 128 - 1, 3)]
        kb.phase_flush(block)

        ld_contrib_done = 0
        ld_cwt = 0
        cc1 = 0
        if PHASE_LIMIT >= 2:
            kb.phase_begin()
            with contextlib.ExitStack() as ph:
                Ar = lambda *a: ph.enter_context(nc.sbuf_tensor(*a, side="right"))
                Pp = lambda n, s, dt=F32: ph.enter_context(nc.psum_tensor(n, s, dt))
                wbuf = [Ar(f"wb{i}", [128, NCH * 128], BF16) for i in range(2)]
                stg = [Ar(f"stg{i}", [128, TOK], BF16) for i in range(2)]
                sqb = Ar("sqb", [128, TOK], BF16)
                qk_ps = [Pp(f"qk{i}", [128, 512]) for i in range(2)]
                sum_q, ssq_q = Pp("sum_q", [128, 512]), Pp("ssq_q", [128, 512])
                sum_k, ssq_k = Pp("sum_k", [128, 512]), Pp("ssq_k", [128, 512])

                pe_qk, ac_qk, ld_sp = {}, {}, {}
                for f in range(48):
                    isv = f >= 32
                    wb, st = wbuf[f % 2], stg[f % 2]
                    if f >= 2:
                        wait("gp", "PE", pe_qk[(f - 2, 3)])
                    else:
                        wait("gp", "VE", xn_ready)
                    gd_w = emi("gp", "GD", lambda e, f=f, wb=wb: e.dma_start(
                        out=wb[:].rearrange("p (n c) -> p n c", c=128),
                        in_=wqkv[:, f * 128:(f + 1) * 128].rearrange("(n p) c -> p n c", p=128)), 16)
                    for ts in range(4):
                        bank = qk_ps[ts % 2]
                        wait("pe", "GD", gd_w)
                        if f == 0 and ts == 0:
                            wait("pe", "VE", xn_ready)
                        prev = (f, ts - 2) if ts >= 2 else ((f - 1, ts + 2) if f >= 1 else None)
                        if prev in ac_qk:
                            wait("pe", "AC", ac_qk[prev])
                        for d in range(NCH):
                            em("pe", lambda e, wb=wb, d=d: e.ldweights(wb[:, d * 128:(d + 1) * 128]))
                            fn = lambda e, wb=wb, d=d, ts=ts, bank=bank: e.matmul(
                                out=bank[:], lhsT=wb[:, d * 128:(d + 1) * 128],
                                rhs=xn[:, d * TOK + ts * 512:d * TOK + (ts + 1) * 512],
                                start=(d == 0), stop=(d == NCH - 1))
                            if d < NCH - 1:
                                em("pe", fn)
                            else:
                                emi("pe", "PE", fn)
                        mm_done = val("PE")
                        wait("ac", "PE", mm_done)
                        if f >= 2 and ts == 0:
                            wait("ac", "LD", ld_sp[f - 2])
                            wait("ac", "PE", pe_qk[(f - 2, 3)])
                        ac1 = emi("ac", "AC", lambda e, st=st, ts=ts, bank=bank: e.activation(
                            out=st[:, ts * 512:(ts + 1) * 512], in_=bank[:], func=AF.Copy))
                        if not isv:
                            ac1 = emi("ac", "AC", lambda e, ts=ts, bank=bank: e.activation(
                                out=sqb[:, ts * 512:(ts + 1) * 512], in_=bank[:], func=AF.Square))
                        ac_qk[(f, ts)] = ac1
                        if not isv:
                            sm = sum_q if f < 16 else sum_k
                            ss = ssq_q if f < 16 else ssq_k
                            first, last = (f % 16 == 0), (f % 16 == 15)
                            wait("pe", "AC", ac1)
                            em("pe", lambda e: e.ldweights(ones_b[:]))
                            em("pe", lambda e, st=st, ts=ts, sm=sm, first=first, last=last: e.matmul(
                                out=sm[ts * 32:ts * 32 + 1, :], lhsT=ones_b[:],
                                rhs=st[:, ts * 512:(ts + 1) * 512],
                                start=first, stop=last, tile_position=(0, ts * 32),
                                skip_group_check=True))
                            emi("pe", "PE", lambda e, ts=ts, ss=ss, first=first, last=last: e.matmul(
                                out=ss[ts * 32:ts * 32 + 1, :], lhsT=ones_b[:],
                                rhs=sqb[:, ts * 512:(ts + 1) * 512],
                                start=first, stop=last, tile_position=(0, ts * 32),
                                skip_group_check=True))
                        pe_qk[(f, ts)] = val("PE")
                    wait("sy", "AC", ac_qk[(f, 3)])
                    if not isv:
                        wait("sy", "PE", pe_qk[(f, 3)])
                        tgt, r0 = (qraw, f * 128) if f < 16 else (kraw, (f - 16) * 128)
                        ld_sp[f] = emi("sy", "LD", lambda e, st=st, tgt=tgt, r0=r0: e.dma_start(
                            out=tgt[r0:r0 + 128, :], in_=st[:]), 16)
                    else:
                        r0 = 2 * D + (f - 32) * 128
                        ld_sp[f] = emi("sy", "LD", lambda e, st=st, r0=r0: e.dma_start(
                            out=contrib[r0:r0 + 128, :], in_=st[:]), 16)

                fmean = Ar("fmean", [128, 512], F32)
                fvar = Ar("fvar", [128, 512], F32)
                fA = Ar("fA", [128, 512], F32)
                fB = Ar("fB", [128, 512], F32)
                for which, (sm, ss, f_last) in enumerate(
                        [(sum_q, ssq_q, 15), (sum_k, ssq_k, 31)]):
                    wait("ve", "PE", pe_qk[(f_last, 3)])
                    for q4 in range(4):
                        r = q4 * 32
                        kb.semi("ve", "VE", lambda e, sm=sm, r=r: e.tensor_scalar_mul(
                            out=fmean[r:r + 1, :], in0=sm[r:r + 1, :], scalar1=1.0 / D))
                        kb.semi("ve", "VE", lambda e, ss=ss, r=r: e.tensor_scalar_mul(
                            out=fvar[r:r + 1, :], in0=ss[r:r + 1, :], scalar1=1.0 / D))
                        kb.semi("ve", "VE", lambda e, r=r: e.tensor_tensor(
                            out=fA[r:r + 1, :], in0=fmean[r:r + 1, :],
                            in1=fmean[r:r + 1, :], op=ALU.mult))
                        ve1 = kb.semi("ve", "VE", lambda e, r=r: e.tensor_tensor(
                            out=fvar[r:r + 1, :], in0=fvar[r:r + 1, :],
                            in1=fA[r:r + 1, :], op=ALU.subtract))
                        wait("ac", "VE", ve1)
                        ac1 = emi("ac", "AC", lambda e, r=r: e.activation(
                            out=fvar[r:r + 1, :], in_=fvar[r:r + 1, :], func=AF.Sqrt,
                            bias=eps6[r:r + 1, :]))
                        wait("ve", "AC", ac1)
                        kb.semi("ve", "VE", lambda e, r=r: e.reciprocal(out=fA[r:r + 1, :], in_=fvar[r:r + 1, :]))
                        ve2 = kb.semi("ve", "VE", lambda e, r=r: e.scalar_tensor_tensor(
                            out=fB[r:r + 1, :], in0=fmean[r:r + 1, :], scalar=-1.0,
                            in1=fA[r:r + 1, :], op0=ALU.mult, op1=ALU.mult))
                        wait("sy", "VE", ve2)
                        emi("sy", "LD", lambda e, which=which, q4=q4, r=r: e.dma_start(
                            out=rowsAB[2 * which:2 * which + 1, q4 * 512:(q4 + 1) * 512],
                            in_=fA[r:r + 1, :]), 16)
                        emi("sy", "LD", lambda e, which=which, q4=q4, r=r: e.dma_start(
                            out=rowsAB[2 * which + 1:2 * which + 2, q4 * 512:(q4 + 1) * 512],
                            in_=fB[r:r + 1, :]), 16)
                ld_rows = val("LD")

                Abb = [Ar(f"Ab{i}", [128, TOK], F32) for i in range(2)]
                Bbb = [Ar(f"Bb{i}", [128, TOK], F32) for i in range(2)]
                raws = [Ar(f"raw{i}", [128, TOK], BF16) for i in range(2)]
                tmp1 = Ar("tmp1", [128, TOK], F32)
                nstg = [Ar(f"nstg{i}", [128, TOK], BF16) for i in range(2)]
                wait("gp", "LD", ld_rows)
                for which in range(2):
                    emi("gp", "GD", lambda e, which=which: e.dma_start(
                        out=Abb[which][:],
                        in_=rowsAB[2 * which:2 * which + 1, :].partition_broadcast(128)), 16)
                    emi("gp", "GD", lambda e, which=which: e.dma_start(
                        out=Bbb[which][:],
                        in_=rowsAB[2 * which + 1:2 * which + 2, :].partition_broadcast(128)), 16)
                gd_ab = val("GD")
                ld_raw, ve_n, ac_n, ld_c = {}, {}, {}, {}
                for j in range(32):
                    which, f = (0, j) if j < 16 else (1, j - 16)
                    src = qraw if which == 0 else kraw
                    gam = qg if which == 0 else kg
                    rb, ns = raws[j % 2], nstg[j % 2]
                    if j >= 2:
                        wait("sy", "VE", ve_n[j - 2])
                    ld_raw[j] = emi("sy", "LD", lambda e, rb=rb, src=src, f=f: e.dma_start(
                        out=rb[:], in_=src[f * 128:(f + 1) * 128, :]), 16)
                    wait("ve", "LD", ld_raw[j])
                    wait("ve", "GD", gd_ab)
                    if j >= 2:
                        wait("ve", "LD", ld_c[j - 2])
                    kb.semi("ve", "VE", lambda e, rb=rb, which=which: e.tensor_tensor(
                        out=tmp1[:], in0=rb[:], in1=Abb[which][:], op=ALU.mult))
                    kb.semi("ve", "VE", lambda e, which=which: e.tensor_tensor(
                        out=tmp1[:], in0=tmp1[:], in1=Bbb[which][:], op=ALU.add))
                    ve_n[j] = kb.semi("ve", "VE", lambda e, ns=ns, gam=gam, f=f: e.tensor_scalar_mul(
                        out=ns[:], in0=tmp1[:], scalar1=gam[:, f:f + 1]))
                    ac_n[j] = ve_n[j]
                    wait("sy", "VE", ve_n[j])
                    r0 = which * D + f * 128
                    ld_c[j] = emi("sy", "LD", lambda e, ns=ns, r0=r0: e.dma_start(
                        out=contrib[r0:r0 + 128, :], in_=ns[:]), 16)
                ld_contrib_done = val("LD")
            kb.phase_flush(block)

            kb.phase_begin()
            with contextlib.ExitStack() as ph:
                Ar = lambda *a: ph.enter_context(nc.sbuf_tensor(*a, side="right"))
                Pp = lambda n, s: ph.enter_context(nc.psum_tensor(n, s, F32))
                g_ps = [Pp(f"g_ps{i}", [128, 16]) for i in range(2)]
                cwt_ps = [Pp(f"cwt{i}", [16, 128]) for i in range(2)]
                lg = Ar("lg", [128, 16], F32)
                m1 = Ar("m1", [128, 1], F32)
                pr = Ar("pr", [128, 16], F32)
                gs = Ar("gs", [128, 4], F32)
                gm = Ar("gm", [128, 4], F32)
                mp = Ar("mp", [128, 16], F32)
                mpw = Ar("mpw", [128, 16], F32)
                cw = Ar("cw", [128, 16], F32)
                s1 = Ar("s1", [128, 1], F32)

                ve_g, pe_gmm = {}, {}
                wait("gp", "VE", xn_ready)
                gd_gwt = emi("gp", "GD", lambda e: e.dma_start(
                    out=gwT_s[:].rearrange("p (n c) -> p n c", c=16),
                    in_=gwT.rearrange("(n p) c -> p n c", p=128)), 16)
                wait("pe", "GD", gd_gwt)
                for tc in range(16):
                    bank = g_ps[tc % 2]
                    if tc >= 2:
                        wait("pe", "VE", ve_g[tc - 2])
                    for d in range(NCH):
                        em("pe", lambda e, d=d, tc=tc: e.ldweights(
                            xn[:, d * TOK + tc * 128:d * TOK + (tc + 1) * 128]))
                        fn = lambda e, d=d, tc=tc, bank=bank: e.matmul(
                            out=bank[:], lhsT=xn[:, d * TOK + tc * 128:d * TOK + (tc + 1) * 128],
                            rhs=gwT_s[:, d * 16:(d + 1) * 16],
                            start=(d == 0), stop=(d == NCH - 1))
                        if d < NCH - 1:
                            em("pe", fn)
                        else:
                            emi("pe", "PE", fn)
                    wait("ve", "PE", val("PE"))
                    if tc >= 1:
                        wait("ve", "PE", pe_gmm[tc - 1])
                    SV = lambda fn: kb.semi("ve", "VE", fn)
                    SV(lambda e, bank=bank: e.reduce_max(out=m1[:], in_=bank[:], axis=AX.X))
                    SV(lambda e: e.tensor_scalar_mul(out=m1[:], in0=m1[:], scalar1=-1.0))
                    ve1 = SV(lambda e, bank=bank: e.tensor_copy(out=lg[:], in_=bank[:]))
                    wait("ac", "VE", ve1)
                    ac1 = emi("ac", "AC", lambda e: e.activation(
                        out=pr[:], in_=lg[:], func=AF.Exp, bias=m1[:]))
                    wait("ve", "AC", ac1)
                    SV(lambda e: e.reduce_sum(out=s1[:], in_=pr[:], axis=AX.X))
                    SV(lambda e: e.reciprocal(out=s1[:], in_=s1[:]))
                    SV(lambda e: e.tensor_scalar_mul(out=pr[:], in0=pr[:], scalar1=s1[:]))
                    SV(lambda e: e.tensor_tensor(
                        out=gs[:], in0=pr[:, 0::4], in1=pr[:, 1::4], op=ALU.max))
                    SV(lambda e: e.tensor_tensor(
                        out=gs[:], in0=gs[:], in1=pr[:, 2::4], op=ALU.max))
                    SV(lambda e: e.tensor_tensor(
                        out=gs[:], in0=gs[:], in1=pr[:, 3::4], op=ALU.max))
                    SV(lambda e: e.reduce_max(out=m1[:], in_=gs[:], axis=AX.X))
                    SV(lambda e: e.tensor_scalar(
                        out=gm[:], in0=gs[:], scalar1=m1[:], scalar2=None, op0=ALU.is_equal))
                    SV(lambda e: e.scalar_tensor_tensor(
                        out=gm[:], in0=gm[:], scalar=NEGBIG, in1=gs[:], op0=ALU.mult, op1=ALU.add))
                    SV(lambda e: e.reduce_max(out=m1[:], in_=gm[:], axis=AX.X))
                    SV(lambda e: e.tensor_scalar(
                        out=gm[:], in0=gs[:], scalar1=m1[:], scalar2=None, op0=ALU.is_ge))
                    for k4 in range(4):
                        SV(lambda e, k4=k4: e.tensor_tensor(
                            out=mp[:, k4::4], in0=pr[:, k4::4], in1=gm[:], op=ALU.mult))
                    SV(lambda e: e.tensor_copy(out=mpw[:], in_=mp[:]))
                    for _ in range(3):
                        SV(lambda e: e.reduce_max(out=m1[:], in_=mpw[:], axis=AX.X))
                        SV(lambda e: e.tensor_scalar(
                            out=cw[:], in0=mpw[:], scalar1=m1[:], scalar2=None, op0=ALU.is_equal))
                        SV(lambda e: e.scalar_tensor_tensor(
                            out=mpw[:], in0=cw[:], scalar=NEGBIG, in1=mpw[:],
                            op0=ALU.mult, op1=ALU.add))
                    SV(lambda e: e.reduce_max(out=m1[:], in_=mpw[:], axis=AX.X))
                    SV(lambda e: e.tensor_scalar(
                        out=cw[:], in0=mp[:], scalar1=m1[:], scalar2=None, op0=ALU.is_ge))
                    SV(lambda e: e.tensor_tensor(out=cw[:], in0=cw[:], in1=mp[:], op=ALU.mult))
                    SV(lambda e: e.reduce_sum(out=s1[:], in_=cw[:], axis=AX.X))
                    SV(lambda e: e.tensor_scalar_add(out=s1[:], in0=s1[:], scalar1=1.0e-20))
                    SV(lambda e: e.reciprocal(out=s1[:], in_=s1[:]))
                    ve_cw = SV(lambda e: e.tensor_scalar_mul(
                        out=cw[:], in0=cw[:], scalar1=s1[:]))
                    cb = cwt_ps[tc % 2]
                    wait("pe", "VE", ve_cw)
                    pe_gmm[tc] = emi("pe", "PE", lambda e, cb=cb: e.matmul(
                        out=cb[:], lhsT=cw[:], rhs=idf[:], start=True, stop=True))
                    wait("ve", "PE", pe_gmm[tc])
                    ve_g[tc] = emi("ve", "VE", lambda e, cb=cb, tc=tc: e.tensor_copy(
                        out=cwT_s[:, tc * 128:(tc + 1) * 128], in_=cb[:]))
                wait("sy", "VE", ve_g[15])
                ld_cwt = emi("sy", "LD", lambda e: e.dma_start(
                    out=cwT_d[0:16, :], in_=cwT_s[:]), 16)
            kb.phase_flush(block)

            kb.phase_begin()
            wait("gp", "LD", ld_contrib_done)
            for s_ in ("GD", "UP", "DN", "AA", "CW"):
                wait("gp", s_, val(s_))
            cc1 = emi("gp", "CC", lambda e: e.collective_compute(
                "AllGather", ALU.bypass,
                replica_groups=[[0, 1, 2, 3], [4, 5, 6, 7]],
                ins=[contrib[:, :].opt()], outs=[gath[:, :].opt()]))
            kb.phase_flush(block)

        st_ffn = {"piece_free": {}, "down_free": {}, "down_ac": {}, "cwb_free": {},
                  "up_prev_ve": {}, "down_strip_free": {}, "acc_cell": {},
                  "gd_piece": {}, "cwb_mm": {}, "cwb_ac_prev": {}}

        def make_piece_emitter(bufs):
            (upg_sl, dwn, ab_, cwb, t1, fstage, g_ps, u_ps, d_banks, cwrow, cw_ps) = bufs
            nsl = len(upg_sl)

            def emit_piece_dma(e_idx, i):
                if (e_idx, i) in st_ffn["gd_piece"]:
                    return
                sl = upg_sl[i % nsl]
                jprev = ICH - 1 - ((ICH - 1 - (i % nsl)) % nsl)
                prev = st_ffn["piece_free"].get((e_idx, i - nsl)) or \
                    st_ffn["piece_free"].get((e_idx - 1, jprev))
                if prev:
                    wait("gp", "PE", prev)
                emi("gp", "UP", lambda e, sl=sl, i=i, e_idx=e_idx: e.dma_start(
                    out=sl[:, 0:NCH * 128].rearrange("p (n c) -> p n c", c=128),
                    in_=upT[e_idx, :, i * 128:(i + 1) * 128].rearrange("(n p) c -> p n c", p=128)), 16)
                st_ffn["gd_piece"][(e_idx, i)] = emi("gp", "UP", lambda e, sl=sl, i=i, e_idx=e_idx: e.dma_start(
                    out=sl[:, NCH * 128:].rearrange("p (n c) -> p n c", c=128),
                    in_=upT[e_idx, :, 1408 + i * 128:1408 + (i + 1) * 128].rearrange("(n p) c -> p n c", p=128)), 16)
            return emit_piece_dma

        def ffn_expert(e_idx, first_expert, bufs, emit_piece_dma):
            (upg_sl, dwn, ab_, cwb, t1, fstage, g_ps, u_ps, d_banks, cwrow, cw_ps) = bufs
            nsl = len(upg_sl)
            if e_idx == first_expert:
                wait("gp", "CC", val("CC"))
            wait("sy", "LD", ld_cwt)
            prev_mm = st_ffn["cwb_mm"].get(e_idx - 1, 0)
            if prev_mm:
                wait("sy", "PE", prev_mm)
            ld_row = emi("sy", "LD", lambda e, e_idx=e_idx: e.dma_start(
                out=cwrow[:], in_=cwT_d[e_idx:e_idx + 1, :]), 16)
            prev_free = st_ffn["cwb_free"].get(e_idx - 1, 0)
            ac_cwb = 0
            for q4 in range(4):
                wait("pe", "LD", ld_row)
                if q4 == 0 and st_ffn["cwb_ac_prev"].get(e_idx - 1):
                    wait("pe", "AC", st_ffn["cwb_ac_prev"][e_idx - 1])
                pe_b = emi("pe", "PE", lambda e, q4=q4: e.matmul(
                    out=cw_ps[:], lhsT=ones_f[:], rhs=cwrow[0:1, q4 * 512:(q4 + 1) * 512],
                    start=True, stop=True))
                wait("ac", "PE", pe_b)
                if q4 == 0 and prev_free:
                    wait("ac", "VE", prev_free)
                ac_cwb = emi("ac", "AC", lambda e, q4=q4: e.activation(
                    out=cwb[:, q4 * 512:(q4 + 1) * 512], in_=cw_ps[:], func=AF.Copy))
                if q4 < 3:
                    wait("pe", "AC", ac_cwb)
            st_ffn["cwb_mm"][e_idx] = val("PE")
            st_ffn["cwb_ac_prev"][e_idx] = ac_cwb
            gd_cwb = ac_cwb
            for i in range(nsl):
                emit_piece_dma(e_idx, i)
            ve_a = {}
            first_up_wait = max(st_ffn["down_free"].get(e_idx - 1, 0), 0)
            ac_prev_expert = st_ffn["down_ac"].get(e_idx - 1, 0)
            for i in range(ICH):
                sl = upg_sl[i % nsl]
                for ts in range(4):
                    par = (i * 4 + ts) % 2
                    gb, ub = g_ps[par], u_ps[par]
                    wait("pe", "UP", st_ffn["gd_piece"][(e_idx, i)])
                    if i == 0 and ts == 0:
                        if e_idx == first_expert and first_expert == 0:
                            wait("pe", "VE", xn_ready)
                        if ac_prev_expert:
                            wait("pe", "AC", ac_prev_expert)
                    prev = ve_a.get((i, ts - 2)) or ve_a.get((i - 1, ts + 2)) or \
                        st_ffn["up_prev_ve"].get(e_idx, 0)
                    if prev:
                        wait("pe", "VE", prev)
                    for half, bank in ((0, gb), (1, ub)):
                        off = half * NCH * 128
                        for d in range(NCH):
                            em("pe", lambda e, sl=sl, off=off, d=d: e.ldweights(
                                sl[:, off + d * 128:off + (d + 1) * 128]))
                            fn = lambda e, sl=sl, off=off, d=d, ts=ts, bank=bank: e.matmul(
                                out=bank[:], lhsT=sl[:, off + d * 128:off + (d + 1) * 128],
                                rhs=xn[:, d * TOK + ts * 512:d * TOK + (ts + 1) * 512],
                                start=(d == 0), stop=(d == NCH - 1))
                            if d < NCH - 1:
                                em("pe", fn)
                            else:
                                emi("pe", "PE", fn)
                    pe_ui = val("PE")
                    wait("ac", "PE", pe_ui)
                    if (i - 1, ts) in ve_a:
                        wait("ac", "VE", ve_a[(i - 1, ts)])
                    ac_s = emi("ac", "AC", lambda e, gb=gb, ts=ts: e.activation(
                        out=t1[:, ts * 512:(ts + 1) * 512], in_=gb[:], func=AF.Silu))
                    wait("ve", "AC", ac_s)
                    wait("ve", "AC", gd_cwb)
                    if i == 0 and ts == 0:
                        dpf = st_ffn["down_free"].get(e_idx - 1, 0)
                        if dpf:
                            wait("ve", "PE", dpf)
                    kb.semi("ve", "VE", lambda e, ub=ub, ts=ts: e.tensor_tensor(
                        out=t1[:, ts * 512:(ts + 1) * 512],
                        in0=t1[:, ts * 512:(ts + 1) * 512], in1=ub[:], op=ALU.mult))
                    ve_a[(i, ts)] = kb.semi("ve", "VE", lambda e, i=i, ts=ts: e.tensor_tensor(
                        out=ab_[:, i * TOK + ts * 512:i * TOK + (ts + 1) * 512],
                        in0=t1[:, ts * 512:(ts + 1) * 512],
                        in1=cwb[:, ts * 512:(ts + 1) * 512], op=ALU.mult))
                st_ffn["piece_free"][(e_idx, i)] = val("PE")
                if i + nsl < ICH:
                    emit_piece_dma(e_idx, i + nsl)
            st_ffn["up_prev_ve"][e_idx + 1] = val("VE")
            st_ffn["cwb_free"][e_idx] = val("VE")
            ve_a_done = val("VE")
            ac_ev, gd_acc = {}, {}
            gd_down = {}

            def emit_down_dma(dsi, guard_pe):
                dw = dwn[dsi % 2]
                if guard_pe:
                    wait("gp", "PE", guard_pe)
                for i in range(ICH):
                    gd_down[dsi] = emi("gp", "DN", lambda e, i=i, dsi=dsi, dw=dw, e_idx=e_idx: e.dma_start(
                        out=dw[:, i * 512:(i + 1) * 512],
                        in_=downT[e_idx, i * 128:(i + 1) * 128, dsi * 512:(dsi + 1) * 512]), 16)

            for dsi in (0, 1):
                emit_down_dma(dsi, st_ffn["down_strip_free"].get((e_idx - 1, dsi + 2), 0))
            for dsi in range(4):
                dw = dwn[dsi % 2]
                if dsi == 2 and e_idx + 1 < NE and not st_ffn.get("no_prefetch"):
                    emit_piece_dma(e_idx + 1, 0)
                    emit_piece_dma(e_idx + 1, 1)
                wait("pe", "DN", gd_down[dsi])
                if dsi == 0:
                    wait("pe", "VE", ve_a_done)
                for sub in range(16):
                    par = sub % 2
                    bank = d_banks[par]
                    prev = ac_ev.get((dsi, sub - 2)) or ac_ev.get((dsi - 1, sub + 14))
                    if prev:
                        wait("pe", "AC", prev)
                    for i in range(ICH):
                        em("pe", lambda e, i=i, sub=sub: e.ldweights(
                            ab_[:, i * TOK + sub * 128:i * TOK + (sub + 1) * 128]))
                        fn = lambda e, i=i, sub=sub, dsi=dsi, bank=bank, dw=dw: e.matmul(
                            out=bank[:], lhsT=ab_[:, i * TOK + sub * 128:i * TOK + (sub + 1) * 128],
                            rhs=dw[:, i * 512:(i + 1) * 512],
                            start=(i == 0), stop=(i == ICH - 1))
                        if i < ICH - 1:
                            em("pe", fn)
                        else:
                            emi("pe", "PE", fn)
                    pe_sub = val("PE")
                    fs = fstage[par]
                    wait("ac", "PE", pe_sub)
                    prevg = gd_acc.get((dsi, sub - 2)) or gd_acc.get((dsi - 1, sub + 14))
                    if prevg:
                        wait("ac", "AA", prevg)
                    ac_ev[(dsi, sub)] = emi("ac", "AC", lambda e, fs=fs, bank=bank: e.activation(
                        out=fs[:], in_=bank[:], func=AF.Copy))
                    wait("gp", "AC", ac_ev[(dsi, sub)])
                    prev_cell = st_ffn["acc_cell"].get((dsi, sub), 0)
                    if prev_cell:
                        wait("gp", "AA", prev_cell)
                    op = ALU.bypass if (e_idx == 0 and first_expert == 0) else ALU.add
                    gd_acc[(dsi, sub)] = emi("gp", "AA", lambda e, fs=fs, sub=sub, dsi=dsi, op=op: e.dma_start(
                        out=ffnacc[sub * 128:(sub + 1) * 128, dsi * 512:(dsi + 1) * 512],
                        in_=fs[:], accum_op=op), 16)
                    st_ffn["acc_cell"][(dsi, sub)] = gd_acc[(dsi, sub)]
                st_ffn["down_strip_free"][(e_idx, dsi)] = val("PE")
                if dsi + 2 < 4:
                    emit_down_dma(dsi + 2, st_ffn["down_strip_free"][(e_idx, dsi)])
            st_ffn["down_free"][e_idx] = val("PE")
            st_ffn["down_ac"][e_idx] = val("AC")
            return val("AA")

        def ffn_phase_bufs(ph, tag):
            Ar = lambda *a: ph.enter_context(nc.sbuf_tensor(*a, side="right"))
            Pp = lambda n, s: ph.enter_context(nc.psum_tensor(n, s, F32))
            upg_sl = [Ar(f"upsl{i}{tag}", [128, 2 * NCH * 128], BF16) for i in range(2)]
            dwn = [Ar(f"dwn{i}{tag}", [128, ICH * 512], BF16) for i in range(2)]
            ab_ = Ar(f"ab{tag}", [128, ICH * TOK], BF16)
            cwb = Ar(f"cwb{tag}", [128, TOK], F32)
            cwrow = Ar(f"cwrow{tag}", [1, TOK], F32)
            t1 = Ar(f"t1{tag}", [128, TOK], F32)
            fstage = [Ar(f"fst{i}{tag}", [128, 512], F32) for i in range(2)]
            g_ps = [Pp(f"gps{i}{tag}", [128, 512]) for i in range(2)]
            u_ps = [Pp(f"ups{i}{tag}", [128, 512]) for i in range(2)]
            d_banks = [Pp(f"dps{i}{tag}", [128, 512]) for i in range(2)]
            cw_ps = Pp(f"cwps{tag}", [128, 512])
            return (upg_sl, dwn, ab_, cwb, t1, fstage, g_ps, u_ps, d_banks, cwrow, cw_ps)

        gd_ffn_done = 0
        if PHASE_LIMIT >= 3:
            kb.phase_begin()
            with contextlib.ExitStack() as ph:
                bufs = ffn_phase_bufs(ph, "a")
                epd = make_piece_emitter(bufs)
                st_ffn["no_prefetch"] = False
                for e_i in range(E_SPLIT):
                    st_ffn["no_prefetch"] = (e_i == E_SPLIT - 1)
                    gd_ffn_done = ffn_expert(e_i, 0, bufs, epd)
                if "ab_sb" in DEBUG_TAPS:
                    abdbg = nc.dram_tensor("dbg_ab", [128, ICH * TOK], BF16,
                                           kind="ExternalOutput").ap()
                    wait("sy", "VE", val("VE"))
                    emi("sy", "OUT", lambda e: e.dma_start(
                        out=abdbg[:, :], in_=bufs[2][:]), 16)
            kb.phase_flush(block)

        ld_ctx_done = 0
        cc2 = 0
        if PHASE_LIMIT >= 4:
            kb.phase_begin()
            with contextlib.ExitStack() as ph:
                Ar = lambda *a: ph.enter_context(nc.sbuf_tensor(*a, side="right"))
                Pp = lambda n, s, dt=F32: ph.enter_context(nc.psum_tensor(n, s, dt))
                qh = Ar("qh", [128, S], BF16)
                kh = Ar("kh", [128, S], BF16)
                vf = Ar("vf", [128, S], BF16)
                vt = Ar("vt", [128, S], BF16)
                ssb = Ar("ssb", [128, 512], F32)
                ssb2 = Ar("ssb2", [128, 512], F32)
                mq = Ar("mq", [128, 4], F32)
                msl = Ar("msl", [128, 16], F32)
                nrow = Ar("nrow", [1, 512], F32)
                nmb = Ar("nmb", [128, 512], F32)
                rlrow = Ar("rlrow", [1, 512], F32)
                rlb = Ar("rlb", [128, 512], F32)
                pt = [Ar(f"pt{i}", [128, 512], BF16) for i in range(2)]
                ctxo = Ar("ctxo", [128, 512], BF16)
                sps = [Pp(f"sps{i}", [128, 512]) for i in range(2)]
                stp = [Pp(f"stp{i}", [128, 512]) for i in range(2)]
                ctx_ps = Pp("ctx_ps", [128, 512])
                br_ps = Pp("br_ps", [128, 512])
                vt_ps = [Pp(f"vtp{i}", [128, 512], BF16) for i in range(2)]

                kb.last_ctxo_ve = 0
                kb.last_ctx_ld = 0
                kb.last_rlb_ac = 0
                kb.last_pe_tr = 0
                kb.passa_prev_red = 0
                kb.passa_prev_combine = 0
                wait("ve", "LD", ld_consts)
                wait("gp", "LD", ld_consts)
                pair_pe, pair_ve, pair_ac, pair_pl = 0, 0, 0, 0
                for pair in range(4):
                    wait("gp", "CC", cc1)
                    if pair > 0:
                        wait("gp", "PE", pair_pe)
                        wait("gp", "VE", pair_ve)
                        wait("gp", "AC", pair_ac)
                    for t_i, dst in ((0, qh), (1, kh), (2, vf)):
                        for si in range(4):
                            col = (pair * 3 + t_i) * 4 + si
                            emi("gp", "GD", lambda e, dst=dst, si=si, col=col: e.indirect_dma_start(
                                out=dst[:, si * TOK:(si + 1) * TOK], out_offset=None,
                                in_=gath[:, :],
                                in_offset=IndirectOffsetOnAxis(aidx[:, col:col + 1], 0)), 16)
                    gd_qkv = val("GD")
                    wait("pe", "GD", gd_qkv)
                    ve_vt = {}
                    for g in range(16):
                        bank = vt_ps[g % 2]
                        if g >= 2:
                            wait("pe", "VE", ve_vt[g - 2])
                        for b4 in range(4):
                            kk = g * 4 + b4
                            fn = lambda e, kk=kk, b4=b4, bank=bank: e.transpose(
                                out=bank[:, b4 * 128:(b4 + 1) * 128],
                                in_=vf[:, kk * 128:(kk + 1) * 128], identity=idb[:])
                            if b4 < 3:
                                em("pe", fn)
                            else:
                                emi("pe", "PE", fn)
                        wait("ve", "PE", val("PE"))
                        ve_vt[g] = emi("ve", "VE", lambda e, g=g, bank=bank: e.tensor_copy(
                            out=vt[:, g * 512:(g + 1) * 512], in_=bank[:]))
                    ve_vt_done = val("VE")

                    def passA_units(st):
                        units = []
                        state = {"prev_combine": kb.passa_prev_combine}
                        for b in range(4):
                            qt = st * 4 + b
                            nkc = qt // 4 + 1
                            ve_red = {}
                            for kc in range(nkc):
                                def unit(b=b, qt=qt, kc=kc, nkc=nkc, ve_red=ve_red, state=state):
                                    bank = sps[kc % 2]
                                    if kc >= 2:
                                        wait("pe", "VE", ve_red[kc - 2])
                                    elif kb.passa_prev_red:
                                        wait("pe", "VE", kb.passa_prev_red)
                                    em("pe", lambda e, qt=qt: e.ldweights(
                                        qh[:, qt * 128:(qt + 1) * 128]))
                                    pe_s = emi("pe", "PE", lambda e, qt=qt, kc=kc, bank=bank: e.matmul(
                                        out=bank[:], lhsT=qh[:, qt * 128:(qt + 1) * 128],
                                        rhs=kh[:, kc * 512:(kc + 1) * 512], start=True, stop=True))
                                    wait("ve", "PE", pe_s)
                                    if kc == 0 and state["prev_combine"]:
                                        wait("ve", "VE", state["prev_combine"])
                                    if kc == nkc - 1:
                                        emi("ve", "VE", lambda e, bank=bank, b=b: e.scalar_tensor_tensor(
                                            out=ssb2[:], in0=bank[:], scalar=1.0,
                                            in1=amask[:, b * 512:(b + 1) * 512],
                                            op0=ALU.mult, op1=ALU.add))
                                        wait("ve", "VE", val("VE"))
                                        ve_red[kc] = emi("ve", "VE", lambda e, kc=kc: e.reduce_max(
                                            out=msl[:, kc:kc + 1], in_=ssb2[:], axis=AX.X))
                                        kb.passa_prev_red = ve_red[kc]
                                        wait("ve", "VE", val("VE"))
                                        if kb.last_pe_tr:
                                            wait("ve", "PE", kb.last_pe_tr)
                                        state["prev_combine"] = emi("ve", "VE",
                                            lambda e, b=b, nkc=nkc: e.reduce_max(
                                                out=mq[:, b:b + 1], in_=msl[:, 0:nkc], axis=AX.X))
                                        kb.passa_prev_combine = state["prev_combine"]
                                    else:
                                        ve_red[kc] = emi("ve", "VE", lambda e, bank=bank, kc=kc: e.reduce_max(
                                            out=msl[:, kc:kc + 1], in_=bank[:], axis=AX.X))
                                        kb.passa_prev_red = ve_red[kc]
                                units.append(unit)
                        return units

                    def emit_negm(st):
                        wait("pe", "VE", kb.passa_prev_combine)
                        if kb.last_rlb_ac:
                            wait("pe", "AC", kb.last_rlb_ac)
                        pe_tr = 0
                        for b in range(4):
                            fn = lambda e, b=b: e.transpose(
                                out=br_ps[0:1, b * 128:(b + 1) * 128],
                                in_=mq[:, b:b + 1], identity=idf[:])
                            if b < 3:
                                em("pe", fn)
                            else:
                                pe_tr = emi("pe", "PE", fn)
                        kb.last_pe_tr = pe_tr
                        wait("ve", "PE", pe_tr)
                        ve_nrow = emi("ve", "VE", lambda e: e.tensor_scalar_mul(
                            out=nrow[:], in0=br_ps[0:1, 0:512], scalar1=-1.0))
                        wait("pe", "VE", ve_nrow)
                        pe_nb = emi("pe", "PE", lambda e: e.matmul(
                            out=br_ps[:], lhsT=ones_f[:], rhs=nrow[:], start=True, stop=True))
                        wait("ac", "PE", pe_nb)
                        return emi("ac", "AC", lambda e: e.activation(
                            out=nmb[:], in_=br_ps[:], func=AF.Copy))

                    def passB_units(st, ac_nmb, shared):
                        nkc8 = 4 * (st + 1)
                        ve_es, ac_pt, ctx_pe, pl_pt = {}, {}, {}, {}
                        shared["ctx_pe"] = ctx_pe
                        shared["nkc8"] = nkc8

                        def emit_ctx_l(kc):
                            p = pt[kc % 2]
                            if kc >= 4 * st:
                                wait("pe", "PL", pl_pt[kc])
                            else:
                                wait("pe", "AC", ac_pt[kc])
                            em("pe", lambda e, kc=kc: e.ldweights(vt[:, kc * 128:(kc + 1) * 128]))
                            em("pe", lambda e, kc=kc, p=p, nkc8=nkc8: e.matmul(
                                out=ctx_ps[:], lhsT=vt[:, kc * 128:(kc + 1) * 128], rhs=p[:],
                                start=(kc == 0), stop=(kc == nkc8 - 1),
                                skip_group_check=True))
                            em("pe", lambda e: e.ldweights(ones_b[:]))
                            ctx_pe[kc] = emi("pe", "PE", lambda e, kc=kc, p=p, nkc8=nkc8: e.matmul(
                                out=br_ps[0:1, :], lhsT=ones_b[:], rhs=p[:],
                                start=(kc == 0), stop=(kc == nkc8 - 1),
                                skip_group_check=True))

                        units = []
                        for kc in range(nkc8):
                            def unit(kc=kc, st=st, ac_nmb=ac_nmb):
                                bank = stp[kc % 2]
                                if kc == 0:
                                    wait("pe", "VE", max(ve_vt_done, kb.last_ctxo_ve))
                                prevv = ve_es.get(kc - 2)
                                if prevv:
                                    wait("pe", "VE", prevv)
                                em("pe", lambda e, kc=kc: e.ldweights(kh[:, kc * 128:(kc + 1) * 128]))
                                pe_st = emi("pe", "PE", lambda e, kc=kc, st=st, bank=bank: e.matmul(
                                    out=bank[:], lhsT=kh[:, kc * 128:(kc + 1) * 128],
                                    rhs=qh[:, st * 512:(st + 1) * 512], start=True, stop=True))
                                wait("ve", "PE", pe_st)
                                wait("ve", "AC", ac_nmb)
                                if kc > 0:
                                    wait("ve", "AC", ac_pt[kc - 1])
                                ve_es[kc] = emi("ve", "VE", lambda e, bank=bank: e.scalar_tensor_tensor(
                                    out=ssb[:], in0=bank[:], scalar=1.0, in1=nmb[:],
                                    op0=ALU.mult, op1=ALU.add))
                                wait("ac", "VE", ve_es[kc])
                                prev_pe = ctx_pe.get(kc - 2)
                                if prev_pe:
                                    wait("ac", "PE", prev_pe)
                                p = pt[kc % 2]
                                ac_pt[kc] = emi("ac", "AC", lambda e, p=p: e.activation(
                                    out=p[:], in_=ssb[:], func=AF.Exp))
                                if kc >= 4 * st:
                                    wait("gp", "AC", ac_pt[kc])
                                    pl_pt[kc] = emi("gp", "PL", lambda e, p=p, st=st, kc=kc: e.affine_select(
                                        out=p[:], in_=p[:], pattern=[[1, 512]],
                                        base=512 * st - 128 * kc, channel_multiplier=-1,
                                        compare_op=ALU.is_ge, fill=0.0))
                                if kc >= 1:
                                    emit_ctx_l(kc - 1)
                                if kc == nkc8 - 1:
                                    emit_ctx_l(kc)
                            units.append(unit)
                        return units

                    def finish_strip(st, shared):
                        ctx_pe = shared["ctx_pe"]
                        nkc8 = shared["nkc8"]
                        wait("ve", "PE", ctx_pe[nkc8 - 1])
                        ve_rl = emi("ve", "VE", lambda e: e.reciprocal(
                            out=rlrow[:], in_=br_ps[0:1, :]))
                        wait("pe", "VE", ve_rl)
                        pe_rlb = emi("pe", "PE", lambda e: e.matmul(
                            out=br_ps[:], lhsT=ones_f[:], rhs=rlrow[:], start=True, stop=True))
                        wait("ac", "PE", pe_rlb)
                        kb.last_rlb_ac = emi("ac", "AC", lambda e: e.activation(
                            out=rlb[:], in_=br_ps[:], func=AF.Copy))
                        wait("ve", "AC", kb.last_rlb_ac)
                        if kb.last_ctx_ld:
                            wait("ve", "LD", kb.last_ctx_ld)
                        kb.last_ctxo_ve = emi("ve", "VE", lambda e: e.tensor_tensor(
                            out=ctxo[:], in0=ctx_ps[:], in1=rlb[:], op=ALU.mult))
                        wait("sy", "VE", kb.last_ctxo_ve)
                        rj, stl = st // 4, st % 4
                        r0 = 512 * rj + 128 * pair
                        kb.last_ctx_ld = emi("sy", "LD", lambda e, r0=r0, stl=stl: e.dma_start(
                            out=ctxc[r0:r0 + 128, stl * 512:(stl + 1) * 512], in_=ctxo[:]), 16)

                    for u in passA_units(0):
                        u()
                    for st_i in range(16):
                        ac_nmb = emit_negm(st_i)
                        ua = passA_units(st_i + 1) if st_i < 15 else []
                        shared = {}
                        ub = passB_units(st_i, ac_nmb, shared)
                        ia, ib = 0, 0
                        while ia < len(ua) or ib < len(ub):
                            if ib < len(ub):
                                ub[ib]()
                                ib += 1
                            if ia < len(ua):
                                ua[ia]()
                                ia += 1
                        finish_strip(st_i, shared)
                    pair_pe, pair_ve, pair_ac = val("PE"), val("VE"), val("AC")
                ld_ctx_done = val("LD")
            kb.phase_flush(block)

            kb.phase_begin()
            wait("gp", "LD", ld_ctx_done)
            for s_ in ("GD", "UP", "DN", "AA", "CW"):
                wait("gp", s_, val(s_))
            cc2 = emi("gp", "CC", lambda e: e.collective_compute(
                "AllGather", ALU.bypass,
                replica_groups=[[0, 1, 2, 3], [4, 5, 6, 7]],
                ins=[ctxc[:, :].opt()], outs=[gath2[:, :].opt()]))
            kb.phase_flush(block)

        if PHASE_LIMIT >= 5:
            kb.phase_begin()
            with contextlib.ExitStack() as ph:
                bufs = ffn_phase_bufs(ph, "b")
                st_ffn["piece_free"] = {}
                st_ffn["down_free"] = {}
                st_ffn["down_ac"] = {}
                st_ffn["cwb_free"] = {}
                st_ffn["up_prev_ve"] = {}
                st_ffn["down_strip_free"] = {}
                st_ffn["gd_piece"] = {}
                epd = make_piece_emitter(bufs)
                st_ffn["no_prefetch"] = False
                for e_i in range(E_SPLIT, NE):
                    st_ffn["no_prefetch"] = (e_i == NE - 1)
                    gd_ffn_done = ffn_expert(e_i, E_SPLIT, bufs, epd)
            kb.phase_flush(block)

        if PHASE_LIMIT >= 6:
            kb.phase_begin()
            with contextlib.ExitStack() as ph:
                Ar = lambda *a: ph.enter_context(nc.sbuf_tensor(*a, side="right"))
                Pp = lambda n, s: ph.enter_context(nc.psum_tensor(n, s, F32))
                oww = Ar("oww", [128, NCH * D], BF16)
                octx = Ar("octx", [128, NCH * 512], BF16)
                ffn_sb = [Ar(f"ffn_sb{i}", [128, D], F32) for i in range(2)]
                x_sb = [Ar(f"x_sb{i}", [128, D], F32) for i in range(2)]
                o_ps = [Pp(f"ops{i}", [128, 512]) for i in range(8)]

                wait("gp", "AA", gd_ffn_done)
                wait("sy", "AA", gd_ffn_done)
                for chn in range(NCH):
                    emi("gp", "GD", lambda e, chn=chn: e.dma_start(
                        out=oww[:, chn * D:(chn + 1) * D],
                        in_=outw[chn * 128:(chn + 1) * 128, :]), 16)
                gd_oww = val("GD")
                ve_fin = {}
                out_cnt = {}
                pe_strip = {}
                for st_i in range(4):
                    wait("gp", "CC", cc2)
                    if st_i >= 1:
                        wait("gp", "PE", pe_strip[st_i - 1])
                    for chn in range(NCH):
                        col = st_i * 16 + chn
                        emi("gp", "GD", lambda e, chn=chn, col=col: e.indirect_dma_start(
                            out=octx[:, chn * 512:(chn + 1) * 512], out_offset=None,
                            in_=gath2.rearrange("r (s c) -> (r s) c", c=512),
                            in_offset=IndirectOffsetOnAxis(eidx[:, col:col + 1], 0)), 16)
                    gd_ctx = val("GD")
                    for sub in range(4):
                        tok0 = st_i * 512 + sub * 128
                        par = sub % 2
                        prevv = ve_fin.get((st_i, sub - 2)) or ve_fin.get((st_i - 1, sub + 2))
                        if prevv:
                            wait("sy", "VE", prevv)
                        emi("sy", "LD", lambda e, par=par, tok0=tok0: e.dma_start(
                            out=ffn_sb[par][:], in_=ffnacc[tok0:tok0 + 128, :]), 16)
                        ld_f = emi("sy", "LD", lambda e, par=par, tok0=tok0: e.dma_start(
                            out=x_sb[par][:], in_=x_in[tok0:tok0 + 128, :]), 16)
                        wait("pe", "GD", gd_ctx)
                        wait("pe", "GD", gd_oww)
                        if prevv:
                            wait("pe", "VE", prevv)
                        for chn in range(NCH):
                            em("pe", lambda e, chn=chn, sub=sub: e.ldweights(
                                octx[:, chn * 512 + sub * 128:chn * 512 + (sub + 1) * 128]))
                            for dsi in range(4):
                                bank = o_ps[par * 4 + dsi]
                                fn = lambda e, chn=chn, sub=sub, dsi=dsi, bank=bank: e.matmul(
                                    out=bank[:],
                                    lhsT=octx[:, chn * 512 + sub * 128:chn * 512 + (sub + 1) * 128],
                                    rhs=oww[:, chn * D + dsi * 512:chn * D + (dsi + 1) * 512],
                                    start=(chn == 0), stop=(chn == NCH - 1),
                                    skip_group_check=True)
                                if not (chn == NCH - 1 and dsi == 3):
                                    em("pe", fn)
                                else:
                                    emi("pe", "PE", fn)
                        pe_sub = val("PE")
                        pe_strip[st_i] = pe_sub
                        wait("ve", "PE", pe_sub)
                        wait("ve", "LD", ld_f)
                        prev_out = out_cnt.get((st_i, sub - 2)) or out_cnt.get((st_i - 1, sub + 2))
                        if prev_out:
                            wait("ve", "OUT", prev_out)
                        for dsi in range(4):
                            bank = o_ps[par * 4 + dsi]
                            fn = lambda e, par=par, dsi=dsi, bank=bank: e.tensor_tensor(
                                out=ffn_sb[par][:, dsi * 512:(dsi + 1) * 512], in0=bank[:],
                                in1=ffn_sb[par][:, dsi * 512:(dsi + 1) * 512], op=ALU.add)
                            if dsi < 3:
                                em("ve", fn)
                            else:
                                emi("ve", "VE", fn)
                        ve_fin[(st_i, sub)] = kb.semi("ve", "VE", lambda e, par=par: e.tensor_tensor(
                            out=x_sb[par][:], in0=ffn_sb[par][:], in1=x_sb[par][:], op=ALU.add))
                        wait("sy", "VE", ve_fin[(st_i, sub)])
                        out_cnt[(st_i, sub)] = emi("sy", "OUT", lambda e, par=par, tok0=tok0: e.dma_start(
                            out=out_t[tok0:tok0 + 128, :], in_=x_sb[par][:]), 16)
            kb.phase_flush(block)

        kb.phase_begin()
        if "xn_sb" in DEBUG_TAPS:
            xndbg = nc.dram_tensor("dbg_xn", [128, NCH * TOK], BF16,
                                   kind="ExternalOutput").ap()
            for s in ("PE", "VE", "AC"):
                wait("sy", s, val(s))
            emi("sy", "OUT", lambda e: e.dma_start(out=xndbg[:, :], in_=xn[:]), 16)
        for name, (dst, src) in dbg.items():
            for s in ("GD", "LD", "CC", "PE", "VE", "AC", "PL"):
                wait("sy", s, val(s))
            emi("sy", "OUT", lambda e, dst=dst, src=src: e.dma_start(
                out=dst[:, :], in_=src[:, :]), 16)
        for s in ("GD", "LD", "CC", "PE", "VE", "AC", "PL", "OUT",
                  "CW", "UP", "DN", "AA"):
            wait("sy", s, val(s))
        for eng in ("pe", "ve", "ac", "gp"):
            wait(eng, "OUT", val("OUT"))
            wait(eng, "LD", val("LD"))
            wait(eng, "GD", val("GD"))
        kb.phase_flush(block)

    ctx.close()
    return nc


def _host_inputs(inputs):
    x = np.asarray(inputs["L0_attn_x"])
    wqkv = np.ascontiguousarray(np.asarray(inputs["L0_attn_Wqkv"], dtype=np.float32))
    outw = np.ascontiguousarray(np.asarray(inputs["L0_attn_out_W"], dtype=np.float32))
    qls = np.asarray(inputs["L0_attn_q_ln_scale"], dtype=np.float32)
    kls = np.asarray(inputs["L0_attn_k_ln_scale"], dtype=np.float32)
    gate = np.asarray(inputs["L0_ffn_gate_w"], dtype=np.float32)
    eup = np.asarray(inputs["L0_ffn_expert_up_w"], dtype=np.float32)
    edn = np.asarray(inputs["L0_ffn_expert_down_w"], dtype=np.float32)
    sup = np.asarray(inputs["L0_ffn_shared_up_w"], dtype=np.float32)
    sdn = np.asarray(inputs["L0_ffn_shared_down_w"], dtype=np.float32)

    upT = np.empty((NE, D, 2816), np.float32)
    upT[:16] = eup.transpose(0, 2, 1)
    supT = sup.T
    upT[16] = np.concatenate([supT[:, 0:1408], supT[:, 2816:4224]], axis=1)
    upT[17] = np.concatenate([supT[:, 1408:2816], supT[:, 4224:5632]], axis=1)
    downT = np.empty((NE, 1408, D), np.float32)
    downT[:16] = edn.transpose(0, 2, 1)
    sdnT = sdn.T
    downT[16] = sdnT[0:1408]
    downT[17] = sdnT[1408:2816]
    upT = np.ascontiguousarray(upT)
    downT = np.ascontiguousarray(downT)
    gwT = np.ascontiguousarray(gate.T)
    qgc = np.ascontiguousarray((qls / np.sqrt(128.0)).reshape(NCH, 128).T)
    kgc = np.ascontiguousarray(kls.reshape(NCH, 128).T)
    idb = np.eye(128, dtype=BF)
    idf = np.eye(128, dtype=np.float32)
    am = np.empty((128, 4 * 512), np.float32)
    ii = np.arange(128)[:, None]
    jj = np.arange(512)[None, :]
    for b in range(4):
        am[:, b * 512:(b + 1) * 512] = np.where((jj - ii) <= 128 * b, 0.0, NEGBIG)

    per_core = []
    for c in range(8):
        bb, rr = c // 4, c % 4
        xs = np.ascontiguousarray(x[bb, rr * TOK:(rr + 1) * TOK, :], dtype=np.float32)
        a_idx = np.empty((128, 48), np.int32)
        for p in range(4):
            for t_i in range(3):
                for si in range(4):
                    col = (p * 3 + t_i) * 4 + si
                    base = 6144 * si + 2048 * t_i + 512 * rr + 128 * p
                    a_idx[:, col] = base + np.arange(128)
        e_idx = np.empty((128, 64), np.int32)
        for st_i in range(4):
            for chn in range(16):
                si, j = divmod(chn, 4)
                base = 2048 * si + 512 * rr + 128 * j
                e_idx[:, st_i * 16 + chn] = 4 * (base + np.arange(128)) + st_i
        per_core.append({
            "x": xs, "wqkv": wqkv, "outw": outw, "upT": upT, "downT": downT,
            "gwT": gwT, "qg": qgc, "kg": kgc, "idb": idb, "idf": idf,
            "amask": am, "aidx": a_idx, "eidx": e_idx,
        })
    return per_core


def kernel(**inputs):
    per_core = _host_inputs(inputs)
    nc = build_program()
    res = run_bass_kernel_spmd(nc, per_core, core_ids=list(range(8)))
    y = np.empty((NB, S, D), np.float32)
    for c in range(8):
        bb, rr = c // 4, c % 4
        y[bb, rr * TOK:(rr + 1) * TOK, :] = res.results[c]["out"]
    return y


# version 12
# speedup vs baseline: 1.0765x; 1.0213x over previous
import contextlib
import numpy as np
import ml_dtypes
import concourse.bass as bass
import concourse.mybir as mybir
from concourse.bass_core_rust import IndirectOffsetOnAxis
from concourse.bass_utils import run_bass_kernel_spmd

F32 = mybir.dt.float32
BF16 = mybir.dt.bfloat16
I32 = mybir.dt.int32
AF = mybir.ActivationFunctionType
ALU = mybir.AluOpType
AX = mybir.AxisListType

D = 2048
S = 8192
NB = 2
TOK = 2048
NCH = D // 128
NE = 18
ICH = 11
NEGBIG = -1.0e30
E_SPLIT = 6
BF = ml_dtypes.bfloat16

DEBUG_TAPS = []
PHASE_LIMIT = 99


class KB:
    def __init__(self, nc, ctx):
        self.nc = nc
        self.ctx = ctx
        self.cnt = {}
        self.sems = {n: ctx.enter_context(nc.semaphore(n))
                     for n in ("LD", "GD", "PE", "VE", "AC", "PL", "CC", "OUT",
                               "CW", "UP", "DN", "AA")}
        self.ops = None

    def phase_begin(self):
        self.ops = {k: [] for k in ("pe", "ve", "ac", "gp", "sy")}

    def phase_flush(self, block):
        def runner(fns):
            return lambda e: [f(e) for f in fns]
        if self.ops["pe"]:
            block.tensor(runner(self.ops["pe"]))
        if self.ops["ve"]:
            block.vector(runner(self.ops["ve"]))
        if self.ops["ac"]:
            block.scalar(runner(self.ops["ac"]))
        if self.ops["gp"]:
            block.gpsimd(runner(self.ops["gp"]))
        if self.ops["sy"]:
            block.sync(runner(self.ops["sy"]))
        self.ops = None

    def em(self, eng, fn):
        self.ops[eng].append(fn)

    def wait(self, eng, sem, v):
        if v > 0:
            self.em(eng, lambda e, s=self.sems[sem], v=v: e.wait_ge(s, v))

    def bump(self, sem, k=1):
        self.cnt[sem] = self.cnt.get(sem, 0) + k
        return self.sems[sem], self.cnt[sem]

    def val(self, sem):
        return self.cnt.get(sem, 0)

    def emi(self, eng, sem, fn, k=1):
        h, _ = self.bump(sem, k)
        self.em(eng, lambda e, h=h, k=k, fn=fn: fn(e).then_inc(h, k))
        return self.val(sem)

    def semi(self, eng, sem, fn):
        self.wait(eng, sem, self.val(sem))
        return self.emi(eng, sem, fn)


def build_program():
    ctx = contextlib.ExitStack()
    nc = bass.Bass("TRN2", target_bir_lowering=False)
    kb = KB(nc, ctx)

    x_in = nc.dram_tensor("x", [TOK, D], F32, kind="ExternalInput").ap()
    wqkv = nc.dram_tensor("wqkv", [D, 3 * D], F32, kind="ExternalInput").ap()
    outw = nc.dram_tensor("outw", [D, D], F32, kind="ExternalInput").ap()
    upT = nc.dram_tensor("upT", [NE, D, 2816], F32, kind="ExternalInput").ap()
    downT = nc.dram_tensor("downT", [NE, 1408, D], F32, kind="ExternalInput").ap()
    gwT = nc.dram_tensor("gwT", [D, 16], F32, kind="ExternalInput").ap()
    qg_in = nc.dram_tensor("qg", [128, NCH], F32, kind="ExternalInput").ap()
    kg_in = nc.dram_tensor("kg", [128, NCH], F32, kind="ExternalInput").ap()
    idb_in = nc.dram_tensor("idb", [128, 128], BF16, kind="ExternalInput").ap()
    idf_in = nc.dram_tensor("idf", [128, 128], F32, kind="ExternalInput").ap()
    amask_in = nc.dram_tensor("amask", [128, 4 * 512], F32, kind="ExternalInput").ap()
    aidx_in = nc.dram_tensor("aidx", [128, 48], I32, kind="ExternalInput").ap()
    eidx_in = nc.dram_tensor("eidx", [128, 64], I32, kind="ExternalInput").ap()

    out_t = nc.dram_tensor("out", [TOK, D], F32, kind="ExternalOutput").ap()

    contrib = nc.dram_tensor("contrib", [3 * D, TOK], BF16).ap()
    gath = nc.dram_tensor("gath", [4 * 3 * D, TOK], BF16).ap()
    ctxc = nc.dram_tensor("ctxc", [TOK, TOK], BF16).ap()
    gath2 = nc.dram_tensor("gath2", [4 * TOK, TOK], BF16).ap()
    qraw = nc.dram_tensor("qraw", [D, TOK], BF16).ap()
    kraw = nc.dram_tensor("kraw", [D, TOK], BF16).ap()
    rowsAB = nc.dram_tensor("rowsAB", [4, TOK], F32).ap()
    cwT_d = nc.dram_tensor("cwT_d", [NE, TOK], F32).ap()
    ffnacc = nc.dram_tensor("ffnacc", [TOK, D], F32).ap()
    dbg = {}
    for name in DEBUG_TAPS:
        if name in ("xn_sb", "ab_sb"):
            continue
        src = {"contrib": contrib, "gath": gath, "ctxc": ctxc, "gath2": gath2,
               "qraw": qraw, "kraw": kraw, "rowsAB": rowsAB, "cwT_d": cwT_d,
               "ffnacc": ffnacc}[name]
        dbg[name] = (nc.dram_tensor("dbg_" + name, list(src.shape), src.dtype,
                                    kind="ExternalOutput").ap(), src)

    A = lambda *a, **k2: ctx.enter_context(nc.sbuf_tensor(*a, side="left", **k2))
    xn = A("xn", [128, NCH * TOK], BF16)
    idb = A("idb_s", [128, 128], BF16)
    idf = A("idf_s", [128, 128], F32)
    ones_b = A("ones_b", [128, 1], BF16)
    ones_f = A("ones_f", [1, 128], F32)
    ones_fb = A("ones_fb", [1, 128], BF16)
    ones_row = A("ones_row", [1, TOK], F32)
    qg = A("qg_s", [128, NCH], F32)
    kg = A("kg_s", [128, NCH], F32)
    amask = A("amask_s", [128, 4 * 512], F32)
    aidx = A("aidx_s", [128, 48], I32)
    eidx = A("eidx_s", [128, 64], I32)
    gwT_s = A("gwT_s", [128, NCH * 16], BF16)
    eps5 = A("eps5", [128, 1], F32)
    eps6 = A("eps6", [128, 1], F32)
    cwT_s = A("cwT_s", [16, TOK], F32)

    em, wait, emi, val = kb.em, kb.wait, kb.emi, kb.val

    with nc.Block() as block:
        kb.phase_begin()
        for dst, src in [(idb, idb_in), (idf, idf_in), (qg, qg_in), (kg, kg_in),
                         (amask, amask_in), (aidx, aidx_in), (eidx, eidx_in)]:
            emi("sy", "LD", lambda e, d=dst, s=src: e.dma_start(out=d[:], in_=s[:]), 16)
        ld_consts = val("LD")
        wait("ve", "LD", ld_consts)
        em("ve", lambda e: e.memset(ones_b[:], 1.0))
        em("ve", lambda e: e.memset(ones_f[:], 1.0))
        em("ve", lambda e: e.memset(ones_fb[:], 1.0))
        em("ve", lambda e: e.memset(eps5[:], 1.0e-5))
        em("ve", lambda e: e.memset(eps6[:], 1.0e-6))
        ve_ones = emi("ve", "VE", lambda e: e.memset(ones_row[:], 1.0))
        wait("sy", "VE", ve_ones)
        for r in (16, 17):
            emi("sy", "LD", lambda e, r=r: e.dma_start(
                out=cwT_d[r:r + 1, :], in_=ones_row[:]), 16)
        kb.phase_flush(block)

        kb.phase_begin()
        with contextlib.ExitStack() as ph:
            Ar = lambda *a: ph.enter_context(nc.sbuf_tensor(*a, side="right"))
            xbuf = [Ar(f"xb{i}", [128, D], F32) for i in range(2)]
            xnt = [Ar(f"xnt{i}", [128, D], BF16) for i in range(2)]
            sq = [Ar(f"sq1a{i}", [128, D], F32) for i in range(2)]
            ssum = Ar("ssum", [128, 2], F32)
            mean = Ar("mean", [128, 2], F32)
            msq = Ar("msq", [128, 2], F32)
            var = Ar("var", [128, 2], F32)
            sd = Ar("sd", [128, 2], F32)
            nb_ = Ar("nb", [128, 2], F32)
            tps = [ph.enter_context(nc.psum_tensor(f"tp{i}", [128, 512], BF16))
                   for i in range(2)]

            ac_done, pe_done = {}, {}
            ve_copy = {}
            wait("pe", "LD", ld_consts)
            wait("ve", "LD", ld_consts)

            def tile_units(tc):
                buf, xb = xbuf[tc % 2], xnt[tc % 2]
                par = tc % 2
                st8 = {}
                units = []

                def u_load():
                    if tc >= 2:
                        wait("sy", "VE", ac_done[tc - 2])
                    st8["ld"] = emi("sy", "LD", lambda e, b=buf, tc=tc: e.dma_start(
                        out=b[:], in_=x_in[tc * 128:(tc + 1) * 128, :]), 16)
                units.append(u_load)

                def u_sum():
                    wait("ve", "LD", st8["ld"])
                    kb.semi("ve", "VE", lambda e, b=buf: e.reduce_sum(
                        out=ssum[:, par:par + 1], in_=b[:], axis=AX.X))
                units.append(u_sum)

                def u_sq():
                    wait("ac", "LD", st8["ld"])
                    st8["sq"] = emi("ac", "AC", lambda e, b=buf: e.activation(
                        out=sq[par][:], in_=b[:], func=AF.Square))
                units.append(u_sq)

                def u_sum2():
                    wait("ve", "AC", st8["sq"])
                    st8["s1"] = kb.semi("ve", "VE", lambda e: e.reduce_sum(
                        out=msq[:, par:par + 1], in_=sq[par][:], axis=AX.X))
                units.append(u_sum2)

                def u_var():
                    kb.semi("ve", "VE", lambda e: e.tensor_scalar_mul(
                        out=mean[:, par:par + 1], in0=ssum[:, par:par + 1], scalar1=1.0 / D))
                    kb.semi("ve", "VE", lambda e: e.tensor_scalar_mul(
                        out=var[:, par:par + 1], in0=msq[:, par:par + 1], scalar1=1.0 / D))
                    kb.semi("ve", "VE", lambda e: e.scalar_tensor_tensor(
                        out=sd[:, par:par + 1], in0=mean[:, par:par + 1], scalar=-1.0,
                        in1=mean[:, par:par + 1], op0=ALU.mult, op1=ALU.mult))
                    st8["v1"] = kb.semi("ve", "VE", lambda e: e.tensor_tensor(
                        out=var[:, par:par + 1], in0=var[:, par:par + 1],
                        in1=sd[:, par:par + 1], op=ALU.add))
                units.append(u_var)

                def u_sqrt():
                    wait("ac", "VE", st8["v1"])
                    st8["sd"] = emi("ac", "AC", lambda e: e.activation(
                        out=sd[:, par:par + 1], in_=var[:, par:par + 1],
                        func=AF.Sqrt, bias=eps5[:]))
                units.append(u_sqrt)

                def u_norm():
                    wait("ve", "AC", st8["sd"])
                    kb.semi("ve", "VE", lambda e: e.reciprocal(
                        out=sd[:, par:par + 1], in_=sd[:, par:par + 1]))
                    kb.semi("ve", "VE", lambda e: e.scalar_tensor_tensor(
                        out=nb_[:, par:par + 1], in0=mean[:, par:par + 1], scalar=-1.0,
                        in1=sd[:, par:par + 1], op0=ALU.mult, op1=ALU.mult))
                    if tc >= 2:
                        wait("ve", "PE", pe_done[tc - 2])
                    ac_done[tc] = kb.semi("ve", "VE", lambda e, b=buf, xb=xb: e.tensor_scalar(
                        out=xb[:], in0=b[:], scalar1=sd[:, par:par + 1],
                        scalar2=nb_[:, par:par + 1], op0=ALU.mult, op1=ALU.add))
                units.append(u_norm)

                def u_trans():
                    wait("pe", "VE", ac_done[tc])
                    for g in range(4):
                        bank = tps[g % 2]
                        prev = (tc, g - 2) if g >= 2 else (tc - 1, g + 2)
                        if prev in ve_copy:
                            wait("pe", "VE", ve_copy[prev])
                        for d4 in range(4):
                            d = g * 4 + d4
                            fn = lambda e, d=d, d4=d4, bank=bank, xb=xb: e.transpose(
                                out=bank[:, d4 * 128:(d4 + 1) * 128],
                                in_=xb[:, d * 128:(d + 1) * 128], identity=idb[:])
                            if d4 < 3:
                                em("pe", fn)
                            else:
                                emi("pe", "PE", fn)
                        wait("ve", "PE", val("PE"))
                        ve_copy[(tc, g)] = emi("ve", "VE", lambda e, tc=tc, g=g, bank=bank: e.tensor_copy(
                            out=xn[:].rearrange("p (n t) -> p n t", t=TOK)[:, 4 * g:4 * g + 4, tc * 128:(tc + 1) * 128],
                            in_=bank[:].rearrange("p (n t) -> p n t", t=128)))
                    pe_done[tc] = val("PE")
                units.append(u_trans)
                return units

            pend = None
            for tc in range(0, TOK // 128, 2):
                ua = tile_units(tc)
                ub = tile_units(tc + 1)
                order = [ua[0], ub[0], ua[2], ua[1], ub[2], ua[3], ub[1], ua[4],
                         ub[3], ua[5], ub[4], ua[6], ub[5], ua[7], ub[6], ub[7]]
                for u in order:
                    u()
            xn_ready = ve_copy[(TOK // 128 - 1, 3)]
        kb.phase_flush(block)

        ld_contrib_done = 0
        ld_cwt = 0
        cc1 = 0
        if PHASE_LIMIT >= 2:
            kb.phase_begin()
            with contextlib.ExitStack() as ph:
                Ar = lambda *a: ph.enter_context(nc.sbuf_tensor(*a, side="right"))
                Pp = lambda n, s, dt=F32: ph.enter_context(nc.psum_tensor(n, s, dt))
                wbuf = [Ar(f"wb{i}", [128, NCH * 128], BF16) for i in range(2)]
                stg = [Ar(f"stg{i}", [128, TOK], BF16) for i in range(2)]
                sqb = Ar("sqb", [128, TOK], BF16)
                qk_ps = [Pp(f"qk{i}", [128, 512]) for i in range(2)]
                sum_q, ssq_q = Pp("sum_q", [128, 512]), Pp("ssq_q", [128, 512])
                sum_k, ssq_k = Pp("sum_k", [128, 512]), Pp("ssq_k", [128, 512])

                pe_qk, ac_qk, ld_sp = {}, {}, {}
                for f in range(48):
                    isv = f >= 32
                    wb, st = wbuf[f % 2], stg[f % 2]
                    if f >= 2:
                        wait("gp", "PE", pe_qk[(f - 2, 3)])
                    else:
                        wait("gp", "VE", xn_ready)
                    gd_w = emi("gp", "GD", lambda e, f=f, wb=wb: e.dma_start(
                        out=wb[:].rearrange("p (n c) -> p n c", c=128),
                        in_=wqkv[:, f * 128:(f + 1) * 128].rearrange("(n p) c -> p n c", p=128)), 16)
                    for ts in range(4):
                        bank = qk_ps[ts % 2]
                        wait("pe", "GD", gd_w)
                        if f == 0 and ts == 0:
                            wait("pe", "VE", xn_ready)
                        prev = (f, ts - 2) if ts >= 2 else ((f - 1, ts + 2) if f >= 1 else None)
                        if prev in ac_qk:
                            wait("pe", "AC", ac_qk[prev])
                        for d in range(NCH):
                            em("pe", lambda e, wb=wb, d=d: e.ldweights(wb[:, d * 128:(d + 1) * 128]))
                            fn = lambda e, wb=wb, d=d, ts=ts, bank=bank: e.matmul(
                                out=bank[:], lhsT=wb[:, d * 128:(d + 1) * 128],
                                rhs=xn[:, d * TOK + ts * 512:d * TOK + (ts + 1) * 512],
                                start=(d == 0), stop=(d == NCH - 1))
                            if d < NCH - 1:
                                em("pe", fn)
                            else:
                                emi("pe", "PE", fn)
                        mm_done = val("PE")
                        wait("ac", "PE", mm_done)
                        if f >= 2 and ts == 0:
                            wait("ac", "LD", ld_sp[f - 2])
                            wait("ac", "PE", pe_qk[(f - 2, 3)])
                        ac1 = emi("ac", "AC", lambda e, st=st, ts=ts, bank=bank: e.activation(
                            out=st[:, ts * 512:(ts + 1) * 512], in_=bank[:], func=AF.Copy))
                        if not isv:
                            ac1 = emi("ac", "AC", lambda e, ts=ts, bank=bank: e.activation(
                                out=sqb[:, ts * 512:(ts + 1) * 512], in_=bank[:], func=AF.Square))
                        ac_qk[(f, ts)] = ac1
                        if not isv:
                            sm = sum_q if f < 16 else sum_k
                            ss = ssq_q if f < 16 else ssq_k
                            first, last = (f % 16 == 0), (f % 16 == 15)
                            wait("pe", "AC", ac1)
                            em("pe", lambda e: e.ldweights(ones_b[:]))
                            em("pe", lambda e, st=st, ts=ts, sm=sm, first=first, last=last: e.matmul(
                                out=sm[ts * 32:ts * 32 + 1, :], lhsT=ones_b[:],
                                rhs=st[:, ts * 512:(ts + 1) * 512],
                                start=first, stop=last, tile_position=(0, ts * 32),
                                skip_group_check=True))
                            emi("pe", "PE", lambda e, ts=ts, ss=ss, first=first, last=last: e.matmul(
                                out=ss[ts * 32:ts * 32 + 1, :], lhsT=ones_b[:],
                                rhs=sqb[:, ts * 512:(ts + 1) * 512],
                                start=first, stop=last, tile_position=(0, ts * 32),
                                skip_group_check=True))
                        pe_qk[(f, ts)] = val("PE")
                    wait("sy", "AC", ac_qk[(f, 3)])
                    if not isv:
                        wait("sy", "PE", pe_qk[(f, 3)])
                        tgt, r0 = (qraw, f * 128) if f < 16 else (kraw, (f - 16) * 128)
                        ld_sp[f] = emi("sy", "LD", lambda e, st=st, tgt=tgt, r0=r0: e.dma_start(
                            out=tgt[r0:r0 + 128, :], in_=st[:]), 16)
                    else:
                        r0 = 2 * D + (f - 32) * 128
                        ld_sp[f] = emi("sy", "LD", lambda e, st=st, r0=r0: e.dma_start(
                            out=contrib[r0:r0 + 128, :], in_=st[:]), 16)

                fmean = Ar("fmean", [128, 512], F32)
                fvar = Ar("fvar", [128, 512], F32)
                fA = Ar("fA", [128, 512], F32)
                fB = Ar("fB", [128, 512], F32)
                for which, (sm, ss, f_last) in enumerate(
                        [(sum_q, ssq_q, 15), (sum_k, ssq_k, 31)]):
                    wait("ve", "PE", pe_qk[(f_last, 3)])
                    for q4 in range(4):
                        r = q4 * 32
                        kb.semi("ve", "VE", lambda e, sm=sm, r=r: e.tensor_scalar_mul(
                            out=fmean[r:r + 1, :], in0=sm[r:r + 1, :], scalar1=1.0 / D))
                        kb.semi("ve", "VE", lambda e, ss=ss, r=r: e.tensor_scalar_mul(
                            out=fvar[r:r + 1, :], in0=ss[r:r + 1, :], scalar1=1.0 / D))
                        kb.semi("ve", "VE", lambda e, r=r: e.tensor_tensor(
                            out=fA[r:r + 1, :], in0=fmean[r:r + 1, :],
                            in1=fmean[r:r + 1, :], op=ALU.mult))
                        ve1 = kb.semi("ve", "VE", lambda e, r=r: e.tensor_tensor(
                            out=fvar[r:r + 1, :], in0=fvar[r:r + 1, :],
                            in1=fA[r:r + 1, :], op=ALU.subtract))
                        wait("ac", "VE", ve1)
                        ac1 = emi("ac", "AC", lambda e, r=r: e.activation(
                            out=fvar[r:r + 1, :], in_=fvar[r:r + 1, :], func=AF.Sqrt,
                            bias=eps6[r:r + 1, :]))
                        wait("ve", "AC", ac1)
                        kb.semi("ve", "VE", lambda e, r=r: e.reciprocal(out=fA[r:r + 1, :], in_=fvar[r:r + 1, :]))
                        ve2 = kb.semi("ve", "VE", lambda e, r=r: e.scalar_tensor_tensor(
                            out=fB[r:r + 1, :], in0=fmean[r:r + 1, :], scalar=-1.0,
                            in1=fA[r:r + 1, :], op0=ALU.mult, op1=ALU.mult))
                        wait("sy", "VE", ve2)
                        emi("sy", "LD", lambda e, which=which, q4=q4, r=r: e.dma_start(
                            out=rowsAB[2 * which:2 * which + 1, q4 * 512:(q4 + 1) * 512],
                            in_=fA[r:r + 1, :]), 16)
                        emi("sy", "LD", lambda e, which=which, q4=q4, r=r: e.dma_start(
                            out=rowsAB[2 * which + 1:2 * which + 2, q4 * 512:(q4 + 1) * 512],
                            in_=fB[r:r + 1, :]), 16)
                ld_rows = val("LD")

                Abb = [Ar(f"Ab{i}", [128, TOK], F32) for i in range(2)]
                Bbb = [Ar(f"Bb{i}", [128, TOK], F32) for i in range(2)]
                raws = [Ar(f"raw{i}", [128, TOK], BF16) for i in range(2)]
                tmp1 = Ar("tmp1", [128, TOK], F32)
                nstg = [Ar(f"nstg{i}", [128, TOK], BF16) for i in range(2)]
                wait("gp", "LD", ld_rows)
                for which in range(2):
                    emi("gp", "GD", lambda e, which=which: e.dma_start(
                        out=Abb[which][:],
                        in_=rowsAB[2 * which:2 * which + 1, :].partition_broadcast(128)), 16)
                    emi("gp", "GD", lambda e, which=which: e.dma_start(
                        out=Bbb[which][:],
                        in_=rowsAB[2 * which + 1:2 * which + 2, :].partition_broadcast(128)), 16)
                gd_ab = val("GD")
                ld_raw, ve_n, ac_n, ld_c = {}, {}, {}, {}
                for j in range(32):
                    which, f = (0, j) if j < 16 else (1, j - 16)
                    src = qraw if which == 0 else kraw
                    gam = qg if which == 0 else kg
                    rb, ns = raws[j % 2], nstg[j % 2]
                    if j >= 2:
                        wait("sy", "VE", ve_n[j - 2])
                    ld_raw[j] = emi("sy", "LD", lambda e, rb=rb, src=src, f=f: e.dma_start(
                        out=rb[:], in_=src[f * 128:(f + 1) * 128, :]), 16)
                    wait("ve", "LD", ld_raw[j])
                    wait("ve", "GD", gd_ab)
                    if j >= 2:
                        wait("ve", "LD", ld_c[j - 2])
                    kb.semi("ve", "VE", lambda e, rb=rb, which=which: e.tensor_tensor(
                        out=tmp1[:], in0=rb[:], in1=Abb[which][:], op=ALU.mult))
                    kb.semi("ve", "VE", lambda e, which=which: e.tensor_tensor(
                        out=tmp1[:], in0=tmp1[:], in1=Bbb[which][:], op=ALU.add))
                    ve_n[j] = kb.semi("ve", "VE", lambda e, ns=ns, gam=gam, f=f: e.tensor_scalar_mul(
                        out=ns[:], in0=tmp1[:], scalar1=gam[:, f:f + 1]))
                    ac_n[j] = ve_n[j]
                    wait("sy", "VE", ve_n[j])
                    r0 = which * D + f * 128
                    ld_c[j] = emi("sy", "LD", lambda e, ns=ns, r0=r0: e.dma_start(
                        out=contrib[r0:r0 + 128, :], in_=ns[:]), 16)
                ld_contrib_done = val("LD")
            kb.phase_flush(block)

            kb.phase_begin()
            with contextlib.ExitStack() as ph:
                Ar = lambda *a: ph.enter_context(nc.sbuf_tensor(*a, side="right"))
                Pp = lambda n, s: ph.enter_context(nc.psum_tensor(n, s, F32))
                g_ps = [Pp(f"g_ps{i}", [128, 16]) for i in range(2)]
                cwt_ps = [Pp(f"cwt{i}", [16, 128]) for i in range(2)]
                lg = Ar("lg", [128, 16], F32)
                m1 = Ar("m1", [128, 1], F32)
                pr = Ar("pr", [128, 16], F32)
                gs = Ar("gs", [128, 4], F32)
                gm = Ar("gm", [128, 4], F32)
                mp = Ar("mp", [128, 16], F32)
                mpw = Ar("mpw", [128, 16], F32)
                cw = Ar("cw", [128, 16], F32)
                s1 = Ar("s1", [128, 1], F32)

                ve_g, pe_gmm = {}, {}
                wait("gp", "VE", xn_ready)
                gd_gwt = emi("gp", "GD", lambda e: e.dma_start(
                    out=gwT_s[:].rearrange("p (n c) -> p n c", c=16),
                    in_=gwT.rearrange("(n p) c -> p n c", p=128)), 16)
                wait("pe", "GD", gd_gwt)
                for tc in range(16):
                    bank = g_ps[tc % 2]
                    if tc >= 2:
                        wait("pe", "VE", ve_g[tc - 2])
                    for d in range(NCH):
                        em("pe", lambda e, d=d, tc=tc: e.ldweights(
                            xn[:, d * TOK + tc * 128:d * TOK + (tc + 1) * 128]))
                        fn = lambda e, d=d, tc=tc, bank=bank: e.matmul(
                            out=bank[:], lhsT=xn[:, d * TOK + tc * 128:d * TOK + (tc + 1) * 128],
                            rhs=gwT_s[:, d * 16:(d + 1) * 16],
                            start=(d == 0), stop=(d == NCH - 1))
                        if d < NCH - 1:
                            em("pe", fn)
                        else:
                            emi("pe", "PE", fn)
                    wait("ve", "PE", val("PE"))
                    if tc >= 1:
                        wait("ve", "PE", pe_gmm[tc - 1])
                    SV = lambda fn: kb.semi("ve", "VE", fn)
                    SV(lambda e, bank=bank: e.reduce_max(out=m1[:], in_=bank[:], axis=AX.X))
                    SV(lambda e: e.tensor_scalar_mul(out=m1[:], in0=m1[:], scalar1=-1.0))
                    ve1 = SV(lambda e, bank=bank: e.tensor_copy(out=lg[:], in_=bank[:]))
                    wait("ac", "VE", ve1)
                    ac1 = emi("ac", "AC", lambda e: e.activation(
                        out=pr[:], in_=lg[:], func=AF.Exp, bias=m1[:]))
                    wait("ve", "AC", ac1)
                    SV(lambda e: e.reduce_sum(out=s1[:], in_=pr[:], axis=AX.X))
                    SV(lambda e: e.reciprocal(out=s1[:], in_=s1[:]))
                    SV(lambda e: e.tensor_scalar_mul(out=pr[:], in0=pr[:], scalar1=s1[:]))
                    SV(lambda e: e.tensor_tensor(
                        out=gs[:], in0=pr[:, 0::4], in1=pr[:, 1::4], op=ALU.max))
                    SV(lambda e: e.tensor_tensor(
                        out=gs[:], in0=gs[:], in1=pr[:, 2::4], op=ALU.max))
                    SV(lambda e: e.tensor_tensor(
                        out=gs[:], in0=gs[:], in1=pr[:, 3::4], op=ALU.max))
                    SV(lambda e: e.reduce_max(out=m1[:], in_=gs[:], axis=AX.X))
                    SV(lambda e: e.tensor_scalar(
                        out=gm[:], in0=gs[:], scalar1=m1[:], scalar2=None, op0=ALU.is_equal))
                    SV(lambda e: e.scalar_tensor_tensor(
                        out=gm[:], in0=gm[:], scalar=NEGBIG, in1=gs[:], op0=ALU.mult, op1=ALU.add))
                    SV(lambda e: e.reduce_max(out=m1[:], in_=gm[:], axis=AX.X))
                    SV(lambda e: e.tensor_scalar(
                        out=gm[:], in0=gs[:], scalar1=m1[:], scalar2=None, op0=ALU.is_ge))
                    for k4 in range(4):
                        SV(lambda e, k4=k4: e.tensor_tensor(
                            out=mp[:, k4::4], in0=pr[:, k4::4], in1=gm[:], op=ALU.mult))
                    SV(lambda e: e.tensor_copy(out=mpw[:], in_=mp[:]))
                    for _ in range(3):
                        SV(lambda e: e.reduce_max(out=m1[:], in_=mpw[:], axis=AX.X))
                        SV(lambda e: e.tensor_scalar(
                            out=cw[:], in0=mpw[:], scalar1=m1[:], scalar2=None, op0=ALU.is_equal))
                        SV(lambda e: e.scalar_tensor_tensor(
                            out=mpw[:], in0=cw[:], scalar=NEGBIG, in1=mpw[:],
                            op0=ALU.mult, op1=ALU.add))
                    SV(lambda e: e.reduce_max(out=m1[:], in_=mpw[:], axis=AX.X))
                    SV(lambda e: e.tensor_scalar(
                        out=cw[:], in0=mp[:], scalar1=m1[:], scalar2=None, op0=ALU.is_ge))
                    SV(lambda e: e.tensor_tensor(out=cw[:], in0=cw[:], in1=mp[:], op=ALU.mult))
                    SV(lambda e: e.reduce_sum(out=s1[:], in_=cw[:], axis=AX.X))
                    SV(lambda e: e.tensor_scalar_add(out=s1[:], in0=s1[:], scalar1=1.0e-20))
                    SV(lambda e: e.reciprocal(out=s1[:], in_=s1[:]))
                    ve_cw = SV(lambda e: e.tensor_scalar_mul(
                        out=cw[:], in0=cw[:], scalar1=s1[:]))
                    cb = cwt_ps[tc % 2]
                    wait("pe", "VE", ve_cw)
                    pe_gmm[tc] = emi("pe", "PE", lambda e, cb=cb: e.matmul(
                        out=cb[:], lhsT=cw[:], rhs=idf[:], start=True, stop=True))
                    wait("ve", "PE", pe_gmm[tc])
                    ve_g[tc] = emi("ve", "VE", lambda e, cb=cb, tc=tc: e.tensor_copy(
                        out=cwT_s[:, tc * 128:(tc + 1) * 128], in_=cb[:]))
                wait("sy", "VE", ve_g[15])
                ld_cwt = emi("sy", "LD", lambda e: e.dma_start(
                    out=cwT_d[0:16, :], in_=cwT_s[:]), 16)
            kb.phase_flush(block)

            kb.phase_begin()
            wait("gp", "LD", ld_contrib_done)
            for s_ in ("GD", "UP", "DN", "AA", "CW"):
                wait("gp", s_, val(s_))
            cc1 = emi("gp", "CC", lambda e: e.collective_compute(
                "AllGather", ALU.bypass,
                replica_groups=[[0, 1, 2, 3], [4, 5, 6, 7]],
                ins=[contrib[:, :].opt()], outs=[gath[:, :].opt()]))
            kb.phase_flush(block)

        st_ffn = {"piece_free": {}, "down_free": {}, "down_ac": {}, "cwb_free": {},
                  "up_prev_ve": {}, "down_strip_free": {}, "acc_cell": {},
                  "gd_piece": {}, "cwb_mm": {}, "cwb_ac_prev": {}}

        def make_piece_emitter(bufs):
            (upg_sl, dwn, ab_, cwb, t1, fstage, g_ps, u_ps, d_banks, cwrow,
             cw_ps, stg32) = bufs
            nsl = len(upg_sl)

            def emit_piece_dma(e_idx, i, safe=False):
                if (e_idx, i) in st_ffn["gd_piece"]:
                    return
                sl = upg_sl[i % nsl]
                jprev = ICH - 1 - ((ICH - 1 - (i % nsl)) % nsl)
                prev = st_ffn["piece_free"].get((e_idx, i - nsl)) or \
                    st_ffn["piece_free"].get((e_idx - 1, jprev))
                if not safe:
                    if prev:
                        wait("gp", "PE", prev)
                    emi("gp", "UP", lambda e, sl=sl, i=i, e_idx=e_idx: e.dma_start(
                        out=sl[:, 0:NCH * 128].rearrange("p (n c) -> p n c", c=128),
                        in_=upT[e_idx, :, i * 128:(i + 1) * 128].rearrange("(n p) c -> p n c", p=128)), 16)
                    st_ffn["gd_piece"][(e_idx, i)] = ("UP", emi("gp", "UP",
                        lambda e, sl=sl, i=i, e_idx=e_idx: e.dma_start(
                            out=sl[:, NCH * 128:].rearrange("p (n c) -> p n c", c=128),
                            in_=upT[e_idx, :, 1408 + i * 128:1408 + (i + 1) * 128].rearrange("(n p) c -> p n c", p=128)), 16))
                else:
                    if prev:
                        wait("sy", "PE", prev)
                        wait("gp", "PE", prev)
                    for half in range(2):
                        f0 = i * 128 if half == 0 else 1408 + i * 128
                        off = half * NCH * 128
                        prev_cast = st_ffn.get("safe_cast", 0)
                        if prev_cast:
                            wait("sy", "PL", prev_cast)
                        ldh = emi("sy", "LD", lambda e, f0=f0: e.dma_start(
                            out=stg32[:].rearrange("p (n c) -> p n c", c=128),
                            in_=upT[e_idx, :, f0:f0 + 128].rearrange("(n p) c -> p n c", p=128)), 16)
                        wait("gp", "LD", ldh)
                        st_ffn["safe_cast"] = emi("gp", "PL", lambda e, sl=sl, off=off: e.tensor_copy(
                            out=sl[:, off:off + NCH * 128], in_=stg32[:]))
                    st_ffn["gd_piece"][(e_idx, i)] = ("PL", st_ffn["safe_cast"])
            return emit_piece_dma

        def ffn_expert(e_idx, first_expert, bufs, emit_piece_dma):
            (upg_sl, dwn, ab_, cwb, t1, fstage, g_ps, u_ps, d_banks, cwrow,
             cw_ps, stg32) = bufs
            nsl = len(upg_sl)
            safe = (e_idx == first_expert)
            wait("sy", "LD", ld_cwt)
            prev_mm = st_ffn["cwb_mm"].get(e_idx - 1, 0)
            if prev_mm:
                wait("sy", "PE", prev_mm)
            ld_row = emi("sy", "LD", lambda e, e_idx=e_idx: e.dma_start(
                out=cwrow[:], in_=cwT_d[e_idx:e_idx + 1, :]), 16)
            prev_free = st_ffn["cwb_free"].get(e_idx - 1, 0)
            ac_cwb = 0
            for q4 in range(4):
                wait("pe", "LD", ld_row)
                if q4 == 0 and st_ffn["cwb_ac_prev"].get(e_idx - 1):
                    wait("pe", "AC", st_ffn["cwb_ac_prev"][e_idx - 1])
                pe_b = emi("pe", "PE", lambda e, q4=q4: e.matmul(
                    out=cw_ps[:], lhsT=ones_f[:], rhs=cwrow[0:1, q4 * 512:(q4 + 1) * 512],
                    start=True, stop=True))
                wait("ac", "PE", pe_b)
                if q4 == 0 and prev_free:
                    wait("ac", "VE", prev_free)
                ac_cwb = emi("ac", "AC", lambda e, q4=q4: e.activation(
                    out=cwb[:, q4 * 512:(q4 + 1) * 512], in_=cw_ps[:], func=AF.Copy))
                if q4 < 3:
                    wait("pe", "AC", ac_cwb)
            st_ffn["cwb_mm"][e_idx] = val("PE")
            st_ffn["cwb_ac_prev"][e_idx] = ac_cwb
            gd_cwb = ac_cwb
            for i in range(nsl):
                emit_piece_dma(e_idx, i, safe=safe)
            ve_a = {}
            first_up_wait = max(st_ffn["down_free"].get(e_idx - 1, 0), 0)
            ac_prev_expert = st_ffn["down_ac"].get(e_idx - 1, 0)
            for i in range(ICH):
                sl = upg_sl[i % nsl]
                for ts in range(4):
                    par = (i * 4 + ts) % 2
                    gb, ub = g_ps[par], u_ps[par]
                    _sem, _v = st_ffn["gd_piece"][(e_idx, i)]
                    wait("pe", _sem, _v)
                    if i == 0 and ts == 0:
                        if e_idx == first_expert and first_expert == 0:
                            wait("pe", "VE", xn_ready)
                        if ac_prev_expert:
                            wait("pe", "AC", ac_prev_expert)
                    prev = ve_a.get((i, ts - 2)) or ve_a.get((i - 1, ts + 2)) or \
                        st_ffn["up_prev_ve"].get(e_idx, 0)
                    if prev:
                        wait("pe", "VE", prev)
                    for half, bank in ((0, gb), (1, ub)):
                        off = half * NCH * 128
                        for d in range(NCH):
                            em("pe", lambda e, sl=sl, off=off, d=d: e.ldweights(
                                sl[:, off + d * 128:off + (d + 1) * 128]))
                            fn = lambda e, sl=sl, off=off, d=d, ts=ts, bank=bank: e.matmul(
                                out=bank[:], lhsT=sl[:, off + d * 128:off + (d + 1) * 128],
                                rhs=xn[:, d * TOK + ts * 512:d * TOK + (ts + 1) * 512],
                                start=(d == 0), stop=(d == NCH - 1))
                            if d < NCH - 1:
                                em("pe", fn)
                            else:
                                emi("pe", "PE", fn)
                    pe_ui = val("PE")
                    wait("ac", "PE", pe_ui)
                    tprev = ve_a.get((i, ts - 2)) or ve_a.get((i - 1, ts + 2))
                    if tprev:
                        wait("ac", "VE", tprev)
                    ac_s = emi("ac", "AC", lambda e, gb=gb, ts=ts: e.activation(
                        out=t1[:, (ts % 2) * 512:(ts % 2 + 1) * 512], in_=gb[:], func=AF.Silu))
                    wait("ve", "AC", ac_s)
                    wait("ve", "AC", gd_cwb)
                    if i == 0 and ts == 0:
                        dpf = st_ffn["down_free"].get(e_idx - 1, 0)
                        if dpf:
                            wait("ve", "PE", dpf)
                    kb.semi("ve", "VE", lambda e, ub=ub, ts=ts: e.tensor_tensor(
                        out=t1[:, (ts % 2) * 512:(ts % 2 + 1) * 512],
                        in0=t1[:, (ts % 2) * 512:(ts % 2 + 1) * 512], in1=ub[:], op=ALU.mult))
                    ve_a[(i, ts)] = kb.semi("ve", "VE", lambda e, i=i, ts=ts: e.tensor_tensor(
                        out=ab_[:, i * TOK + ts * 512:i * TOK + (ts + 1) * 512],
                        in0=t1[:, (ts % 2) * 512:(ts % 2 + 1) * 512],
                        in1=cwb[:, ts * 512:(ts + 1) * 512], op=ALU.mult))
                st_ffn["piece_free"][(e_idx, i)] = val("PE")
                if i + nsl < ICH:
                    emit_piece_dma(e_idx, i + nsl, safe=safe)
            st_ffn["up_prev_ve"][e_idx + 1] = val("VE")
            st_ffn["cwb_free"][e_idx] = val("VE")
            ve_a_done = val("VE")
            ac_ev, gd_acc = {}, {}
            gd_down = {}

            def emit_down_dma(dsi, guard_pe):
                dw = dwn[dsi % 2]
                if guard_pe:
                    wait("gp", "PE", guard_pe)
                for i in range(ICH):
                    gd_down[dsi] = emi("gp", "DN", lambda e, i=i, dsi=dsi, dw=dw, e_idx=e_idx: e.dma_start(
                        out=dw[:, i * 512:(i + 1) * 512],
                        in_=downT[e_idx, i * 128:(i + 1) * 128, dsi * 512:(dsi + 1) * 512]), 16)

            if safe:
                wait("gp", "CC", val("CC"))
            for dsi in (0, 1):
                emit_down_dma(dsi, st_ffn["down_strip_free"].get((e_idx - 1, dsi + 2), 0))
            for dsi in range(4):
                dw = dwn[dsi % 2]
                if dsi == 2 and e_idx + 1 < NE and not st_ffn.get("no_prefetch"):
                    emit_piece_dma(e_idx + 1, 0)
                    emit_piece_dma(e_idx + 1, 1)
                wait("pe", "DN", gd_down[dsi])
                if dsi == 0:
                    wait("pe", "VE", ve_a_done)
                for sub in range(16):
                    par = sub % 2
                    bank = d_banks[par]
                    prev = ac_ev.get((dsi, sub - 2)) or ac_ev.get((dsi - 1, sub + 14))
                    if prev:
                        wait("pe", "AC", prev)
                    for i in range(ICH):
                        em("pe", lambda e, i=i, sub=sub: e.ldweights(
                            ab_[:, i * TOK + sub * 128:i * TOK + (sub + 1) * 128]))
                        fn = lambda e, i=i, sub=sub, dsi=dsi, bank=bank, dw=dw: e.matmul(
                            out=bank[:], lhsT=ab_[:, i * TOK + sub * 128:i * TOK + (sub + 1) * 128],
                            rhs=dw[:, i * 512:(i + 1) * 512],
                            start=(i == 0), stop=(i == ICH - 1))
                        if i < ICH - 1:
                            em("pe", fn)
                        else:
                            emi("pe", "PE", fn)
                    pe_sub = val("PE")
                    fs = fstage[par]
                    wait("ac", "PE", pe_sub)
                    prevg = gd_acc.get((dsi, sub - 2)) or gd_acc.get((dsi - 1, sub + 14))
                    if prevg:
                        wait("ac", "AA", prevg)
                    ac_ev[(dsi, sub)] = emi("ac", "AC", lambda e, fs=fs, bank=bank: e.activation(
                        out=fs[:], in_=bank[:], func=AF.Copy))
                    wait("gp", "AC", ac_ev[(dsi, sub)])
                    prev_cell = st_ffn["acc_cell"].get((dsi, sub), 0)
                    if prev_cell:
                        wait("gp", "AA", prev_cell)
                    op = ALU.bypass if (e_idx == 0 and first_expert == 0) else ALU.add
                    gd_acc[(dsi, sub)] = emi("gp", "AA", lambda e, fs=fs, sub=sub, dsi=dsi, op=op: e.dma_start(
                        out=ffnacc[sub * 128:(sub + 1) * 128, dsi * 512:(dsi + 1) * 512],
                        in_=fs[:], accum_op=op), 16)
                    st_ffn["acc_cell"][(dsi, sub)] = gd_acc[(dsi, sub)]
                st_ffn["down_strip_free"][(e_idx, dsi)] = val("PE")
                if dsi + 2 < 4:
                    emit_down_dma(dsi + 2, st_ffn["down_strip_free"][(e_idx, dsi)])
            st_ffn["down_free"][e_idx] = val("PE")
            st_ffn["down_ac"][e_idx] = val("AC")
            return val("AA")

        def ffn_phase_bufs(ph, tag):
            Ar = lambda *a: ph.enter_context(nc.sbuf_tensor(*a, side="right"))
            Pp = lambda n, s: ph.enter_context(nc.psum_tensor(n, s, F32))
            upg_sl = [Ar(f"upsl{i}{tag}", [128, 2 * NCH * 128], BF16) for i in range(2)]
            dwn = [Ar(f"dwn{i}{tag}", [128, ICH * 512], BF16) for i in range(2)]
            ab_ = Ar(f"ab{tag}", [128, ICH * TOK], BF16)
            cwb = Ar(f"cwb{tag}", [128, TOK], F32)
            cwrow = Ar(f"cwrow{tag}", [1, TOK], F32)
            stg32 = Ar(f"stg32{tag}", [128, NCH * 128], F32)
            t1 = Ar(f"t1{tag}", [128, 1024], F32)
            fstage = [Ar(f"fst{i}{tag}", [128, 512], F32) for i in range(2)]
            g_ps = [Pp(f"gps{i}{tag}", [128, 512]) for i in range(2)]
            u_ps = [Pp(f"ups{i}{tag}", [128, 512]) for i in range(2)]
            d_banks = [Pp(f"dps{i}{tag}", [128, 512]) for i in range(2)]
            cw_ps = Pp(f"cwps{tag}", [128, 512])
            return (upg_sl, dwn, ab_, cwb, t1, fstage, g_ps, u_ps, d_banks, cwrow,
                    cw_ps, stg32)

        gd_ffn_done = 0
        if PHASE_LIMIT >= 3:
            kb.phase_begin()
            with contextlib.ExitStack() as ph:
                bufs = ffn_phase_bufs(ph, "a")
                epd = make_piece_emitter(bufs)
                st_ffn["no_prefetch"] = False
                for e_i in range(E_SPLIT):
                    st_ffn["no_prefetch"] = (e_i == E_SPLIT - 1)
                    gd_ffn_done = ffn_expert(e_i, 0, bufs, epd)
                if "ab_sb" in DEBUG_TAPS:
                    abdbg = nc.dram_tensor("dbg_ab", [128, ICH * TOK], BF16,
                                           kind="ExternalOutput").ap()
                    wait("sy", "VE", val("VE"))
                    emi("sy", "OUT", lambda e: e.dma_start(
                        out=abdbg[:, :], in_=bufs[2][:]), 16)
            kb.phase_flush(block)

        ld_ctx_done = 0
        cc2 = 0
        if PHASE_LIMIT >= 4:
            kb.phase_begin()
            with contextlib.ExitStack() as ph:
                Ar = lambda *a: ph.enter_context(nc.sbuf_tensor(*a, side="right"))
                Pp = lambda n, s, dt=F32: ph.enter_context(nc.psum_tensor(n, s, dt))
                qh = Ar("qh", [128, S], BF16)
                kh = Ar("kh", [128, S], BF16)
                vf = Ar("vf", [128, S], BF16)
                vt = Ar("vt", [128, S], BF16)
                ssb = Ar("ssb", [128, 512], F32)
                ssb2 = Ar("ssb2", [128, 512], F32)
                mq = Ar("mq", [128, 4], F32)
                msl = Ar("msl", [128, 16], F32)
                nrow = Ar("nrow", [1, 512], F32)
                nmb = Ar("nmb", [128, 512], F32)
                rlrow = Ar("rlrow", [1, 512], F32)
                rlb = Ar("rlb", [128, 512], F32)
                pt = [Ar(f"pt{i}", [128, 512], BF16) for i in range(2)]
                ctxo = Ar("ctxo", [128, 512], BF16)
                sps = [Pp(f"sps{i}", [128, 512]) for i in range(2)]
                stp = [Pp(f"stp{i}", [128, 512]) for i in range(2)]
                ctx_ps = Pp("ctx_ps", [128, 512])
                br_ps = Pp("br_ps", [128, 512])
                vt_ps = [Pp(f"vtp{i}", [128, 512], BF16) for i in range(2)]

                kb.last_ctxo_ve = 0
                kb.last_ctx_ld = 0
                kb.last_rlb_ac = 0
                kb.last_pe_tr = 0
                kb.passa_prev_red = 0
                kb.passa_prev_combine = 0
                wait("ve", "LD", ld_consts)
                wait("gp", "LD", ld_consts)
                pair_pe, pair_ve, pair_ac, pair_pl = 0, 0, 0, 0
                for pair in range(4):
                    wait("gp", "CC", cc1)
                    if pair > 0:
                        wait("gp", "PE", pair_pe)
                        wait("gp", "VE", pair_ve)
                        wait("gp", "AC", pair_ac)
                    for t_i, dst in ((0, qh), (1, kh), (2, vf)):
                        for si in range(4):
                            col = (pair * 3 + t_i) * 4 + si
                            emi("gp", "GD", lambda e, dst=dst, si=si, col=col: e.indirect_dma_start(
                                out=dst[:, si * TOK:(si + 1) * TOK], out_offset=None,
                                in_=gath[:, :],
                                in_offset=IndirectOffsetOnAxis(aidx[:, col:col + 1], 0)), 16)
                    gd_qkv = val("GD")
                    wait("pe", "GD", gd_qkv)
                    ve_vt = {}
                    for g in range(16):
                        bank = vt_ps[g % 2]
                        if g >= 2:
                            wait("pe", "VE", ve_vt[g - 2])
                        for b4 in range(4):
                            kk = g * 4 + b4
                            fn = lambda e, kk=kk, b4=b4, bank=bank: e.transpose(
                                out=bank[:, b4 * 128:(b4 + 1) * 128],
                                in_=vf[:, kk * 128:(kk + 1) * 128], identity=idb[:])
                            if b4 < 3:
                                em("pe", fn)
                            else:
                                emi("pe", "PE", fn)
                        wait("ve", "PE", val("PE"))
                        ve_vt[g] = emi("ve", "VE", lambda e, g=g, bank=bank: e.tensor_copy(
                            out=vt[:, g * 512:(g + 1) * 512], in_=bank[:]))
                    ve_vt_done = val("VE")

                    def passA_units(st):
                        units = []
                        state = {"prev_combine": kb.passa_prev_combine}
                        for b in range(4):
                            qt = st * 4 + b
                            nkc = qt // 4 + 1
                            ve_red = {}
                            for kc in range(nkc):
                                def unit(b=b, qt=qt, kc=kc, nkc=nkc, ve_red=ve_red, state=state):
                                    bank = sps[kc % 2]
                                    if kc >= 2:
                                        wait("pe", "VE", ve_red[kc - 2])
                                    elif kb.passa_prev_red:
                                        wait("pe", "VE", kb.passa_prev_red)
                                    em("pe", lambda e, qt=qt: e.ldweights(
                                        qh[:, qt * 128:(qt + 1) * 128]))
                                    pe_s = emi("pe", "PE", lambda e, qt=qt, kc=kc, bank=bank: e.matmul(
                                        out=bank[:], lhsT=qh[:, qt * 128:(qt + 1) * 128],
                                        rhs=kh[:, kc * 512:(kc + 1) * 512], start=True, stop=True))
                                    wait("ve", "PE", pe_s)
                                    if kc == 0 and state["prev_combine"]:
                                        wait("ve", "VE", state["prev_combine"])
                                    if kc == nkc - 1:
                                        emi("ve", "VE", lambda e, bank=bank, b=b: e.scalar_tensor_tensor(
                                            out=ssb2[:], in0=bank[:], scalar=1.0,
                                            in1=amask[:, b * 512:(b + 1) * 512],
                                            op0=ALU.mult, op1=ALU.add))
                                        wait("ve", "VE", val("VE"))
                                        ve_red[kc] = emi("ve", "VE", lambda e, kc=kc: e.reduce_max(
                                            out=msl[:, kc:kc + 1], in_=ssb2[:], axis=AX.X))
                                        kb.passa_prev_red = ve_red[kc]
                                        wait("ve", "VE", val("VE"))
                                        if kb.last_pe_tr:
                                            wait("ve", "PE", kb.last_pe_tr)
                                        state["prev_combine"] = emi("ve", "VE",
                                            lambda e, b=b, nkc=nkc: e.reduce_max(
                                                out=mq[:, b:b + 1], in_=msl[:, 0:nkc], axis=AX.X))
                                        kb.passa_prev_combine = state["prev_combine"]
                                    else:
                                        ve_red[kc] = emi("ve", "VE", lambda e, bank=bank, kc=kc: e.reduce_max(
                                            out=msl[:, kc:kc + 1], in_=bank[:], axis=AX.X))
                                        kb.passa_prev_red = ve_red[kc]
                                units.append(unit)
                        return units

                    def emit_negm(st):
                        wait("pe", "VE", kb.passa_prev_combine)
                        if kb.last_rlb_ac:
                            wait("pe", "AC", kb.last_rlb_ac)
                        pe_tr = 0
                        for b in range(4):
                            fn = lambda e, b=b: e.transpose(
                                out=br_ps[0:1, b * 128:(b + 1) * 128],
                                in_=mq[:, b:b + 1], identity=idf[:])
                            if b < 3:
                                em("pe", fn)
                            else:
                                pe_tr = emi("pe", "PE", fn)
                        kb.last_pe_tr = pe_tr
                        wait("ve", "PE", pe_tr)
                        ve_nrow = emi("ve", "VE", lambda e: e.tensor_scalar_mul(
                            out=nrow[:], in0=br_ps[0:1, 0:512], scalar1=-1.0))
                        wait("pe", "VE", ve_nrow)
                        pe_nb = emi("pe", "PE", lambda e: e.matmul(
                            out=br_ps[:], lhsT=ones_f[:], rhs=nrow[:], start=True, stop=True))
                        wait("ac", "PE", pe_nb)
                        return emi("ac", "AC", lambda e: e.activation(
                            out=nmb[:], in_=br_ps[:], func=AF.Copy))

                    def passB_units(st, ac_nmb, shared):
                        nkc8 = 4 * (st + 1)
                        ve_es, ac_pt, ctx_pe, pl_pt = {}, {}, {}, {}
                        shared["ctx_pe"] = ctx_pe
                        shared["nkc8"] = nkc8

                        def emit_ctx_l(kc):
                            p = pt[kc % 2]
                            if kc >= 4 * st:
                                wait("pe", "PL", pl_pt[kc])
                            else:
                                wait("pe", "AC", ac_pt[kc])
                            em("pe", lambda e, kc=kc: e.ldweights(vt[:, kc * 128:(kc + 1) * 128]))
                            em("pe", lambda e, kc=kc, p=p, nkc8=nkc8: e.matmul(
                                out=ctx_ps[:], lhsT=vt[:, kc * 128:(kc + 1) * 128], rhs=p[:],
                                start=(kc == 0), stop=(kc == nkc8 - 1),
                                skip_group_check=True))
                            em("pe", lambda e: e.ldweights(ones_b[:]))
                            ctx_pe[kc] = emi("pe", "PE", lambda e, kc=kc, p=p, nkc8=nkc8: e.matmul(
                                out=br_ps[0:1, :], lhsT=ones_b[:], rhs=p[:],
                                start=(kc == 0), stop=(kc == nkc8 - 1),
                                skip_group_check=True))

                        units = []
                        for kc in range(nkc8):
                            def unit(kc=kc, st=st, ac_nmb=ac_nmb):
                                bank = stp[kc % 2]
                                if kc == 0:
                                    wait("pe", "VE", max(ve_vt_done, kb.last_ctxo_ve))
                                prevv = ve_es.get(kc - 2)
                                if prevv:
                                    wait("pe", "VE", prevv)
                                em("pe", lambda e, kc=kc: e.ldweights(kh[:, kc * 128:(kc + 1) * 128]))
                                pe_st = emi("pe", "PE", lambda e, kc=kc, st=st, bank=bank: e.matmul(
                                    out=bank[:], lhsT=kh[:, kc * 128:(kc + 1) * 128],
                                    rhs=qh[:, st * 512:(st + 1) * 512], start=True, stop=True))
                                wait("ve", "PE", pe_st)
                                wait("ve", "AC", ac_nmb)
                                if kc > 0:
                                    wait("ve", "AC", ac_pt[kc - 1])
                                ve_es[kc] = emi("ve", "VE", lambda e, bank=bank: e.scalar_tensor_tensor(
                                    out=ssb[:], in0=bank[:], scalar=1.0, in1=nmb[:],
                                    op0=ALU.mult, op1=ALU.add))
                                wait("ac", "VE", ve_es[kc])
                                prev_pe = ctx_pe.get(kc - 2)
                                if prev_pe:
                                    wait("ac", "PE", prev_pe)
                                p = pt[kc % 2]
                                ac_pt[kc] = emi("ac", "AC", lambda e, p=p: e.activation(
                                    out=p[:], in_=ssb[:], func=AF.Exp))
                                if kc >= 4 * st:
                                    wait("gp", "AC", ac_pt[kc])
                                    pl_pt[kc] = emi("gp", "PL", lambda e, p=p, st=st, kc=kc: e.affine_select(
                                        out=p[:], in_=p[:], pattern=[[1, 512]],
                                        base=512 * st - 128 * kc, channel_multiplier=-1,
                                        compare_op=ALU.is_ge, fill=0.0))
                                if kc >= 1:
                                    emit_ctx_l(kc - 1)
                                if kc == nkc8 - 1:
                                    emit_ctx_l(kc)
                            units.append(unit)
                        return units

                    def finish_strip(st, shared):
                        ctx_pe = shared["ctx_pe"]
                        nkc8 = shared["nkc8"]
                        wait("ve", "PE", ctx_pe[nkc8 - 1])
                        ve_rl = emi("ve", "VE", lambda e: e.reciprocal(
                            out=rlrow[:], in_=br_ps[0:1, :]))
                        wait("pe", "VE", ve_rl)
                        pe_rlb = emi("pe", "PE", lambda e: e.matmul(
                            out=br_ps[:], lhsT=ones_f[:], rhs=rlrow[:], start=True, stop=True))
                        wait("ac", "PE", pe_rlb)
                        kb.last_rlb_ac = emi("ac", "AC", lambda e: e.activation(
                            out=rlb[:], in_=br_ps[:], func=AF.Copy))
                        wait("ve", "AC", kb.last_rlb_ac)
                        if kb.last_ctx_ld:
                            wait("ve", "LD", kb.last_ctx_ld)
                        kb.last_ctxo_ve = emi("ve", "VE", lambda e: e.tensor_tensor(
                            out=ctxo[:], in0=ctx_ps[:], in1=rlb[:], op=ALU.mult))
                        wait("sy", "VE", kb.last_ctxo_ve)
                        rj, stl = st // 4, st % 4
                        r0 = 512 * rj + 128 * pair
                        kb.last_ctx_ld = emi("sy", "LD", lambda e, r0=r0, stl=stl: e.dma_start(
                            out=ctxc[r0:r0 + 128, stl * 512:(stl + 1) * 512], in_=ctxo[:]), 16)

                    for u in passA_units(0):
                        u()
                    for st_i in range(16):
                        ac_nmb = emit_negm(st_i)
                        ua = passA_units(st_i + 1) if st_i < 15 else []
                        shared = {}
                        ub = passB_units(st_i, ac_nmb, shared)
                        ia, ib = 0, 0
                        while ia < len(ua) or ib < len(ub):
                            if ib < len(ub):
                                ub[ib]()
                                ib += 1
                            if ia < len(ua):
                                ua[ia]()
                                ia += 1
                        finish_strip(st_i, shared)
                    pair_pe, pair_ve, pair_ac = val("PE"), val("VE"), val("AC")
                ld_ctx_done = val("LD")
            kb.phase_flush(block)

            kb.phase_begin()
            wait("gp", "LD", ld_ctx_done)
            for s_ in ("GD", "UP", "DN", "AA", "CW"):
                wait("gp", s_, val(s_))
            cc2 = emi("gp", "CC", lambda e: e.collective_compute(
                "AllGather", ALU.bypass,
                replica_groups=[[0, 1, 2, 3], [4, 5, 6, 7]],
                ins=[ctxc[:, :].opt()], outs=[gath2[:, :].opt()]))
            kb.phase_flush(block)

        if PHASE_LIMIT >= 5:
            kb.phase_begin()
            with contextlib.ExitStack() as ph:
                bufs = ffn_phase_bufs(ph, "b")
                st_ffn["piece_free"] = {}
                st_ffn["down_free"] = {}
                st_ffn["down_ac"] = {}
                st_ffn["cwb_free"] = {}
                st_ffn["up_prev_ve"] = {}
                st_ffn["down_strip_free"] = {}
                st_ffn["gd_piece"] = {}
                epd = make_piece_emitter(bufs)
                st_ffn["no_prefetch"] = False
                for e_i in range(E_SPLIT, NE):
                    st_ffn["no_prefetch"] = (e_i == NE - 1)
                    gd_ffn_done = ffn_expert(e_i, E_SPLIT, bufs, epd)
            kb.phase_flush(block)

        if PHASE_LIMIT >= 6:
            kb.phase_begin()
            with contextlib.ExitStack() as ph:
                Ar = lambda *a: ph.enter_context(nc.sbuf_tensor(*a, side="right"))
                Pp = lambda n, s: ph.enter_context(nc.psum_tensor(n, s, F32))
                oww = Ar("oww", [128, NCH * D], BF16)
                octx = Ar("octx", [128, NCH * 512], BF16)
                ffn_sb = [Ar(f"ffn_sb{i}", [128, D], F32) for i in range(2)]
                x_sb = [Ar(f"x_sb{i}", [128, D], F32) for i in range(2)]
                o_ps = [Pp(f"ops{i}", [128, 512]) for i in range(8)]

                wait("gp", "AA", gd_ffn_done)
                wait("sy", "AA", gd_ffn_done)
                for chn in range(NCH):
                    emi("gp", "GD", lambda e, chn=chn: e.dma_start(
                        out=oww[:, chn * D:(chn + 1) * D],
                        in_=outw[chn * 128:(chn + 1) * 128, :]), 16)
                gd_oww = val("GD")
                ve_fin = {}
                out_cnt = {}
                pe_strip = {}
                for st_i in range(4):
                    wait("gp", "CC", cc2)
                    if st_i >= 1:
                        wait("gp", "PE", pe_strip[st_i - 1])
                    for chn in range(NCH):
                        col = st_i * 16 + chn
                        emi("gp", "GD", lambda e, chn=chn, col=col: e.indirect_dma_start(
                            out=octx[:, chn * 512:(chn + 1) * 512], out_offset=None,
                            in_=gath2.rearrange("r (s c) -> (r s) c", c=512),
                            in_offset=IndirectOffsetOnAxis(eidx[:, col:col + 1], 0)), 16)
                    gd_ctx = val("GD")
                    for sub in range(4):
                        tok0 = st_i * 512 + sub * 128
                        par = sub % 2
                        prevv = ve_fin.get((st_i, sub - 2)) or ve_fin.get((st_i - 1, sub + 2))
                        if prevv:
                            wait("sy", "VE", prevv)
                        emi("sy", "LD", lambda e, par=par, tok0=tok0: e.dma_start(
                            out=ffn_sb[par][:], in_=ffnacc[tok0:tok0 + 128, :]), 16)
                        ld_f = emi("sy", "LD", lambda e, par=par, tok0=tok0: e.dma_start(
                            out=x_sb[par][:], in_=x_in[tok0:tok0 + 128, :]), 16)
                        wait("pe", "GD", gd_ctx)
                        wait("pe", "GD", gd_oww)
                        if prevv:
                            wait("pe", "VE", prevv)
                        for chn in range(NCH):
                            em("pe", lambda e, chn=chn, sub=sub: e.ldweights(
                                octx[:, chn * 512 + sub * 128:chn * 512 + (sub + 1) * 128]))
                            for dsi in range(4):
                                bank = o_ps[par * 4 + dsi]
                                fn = lambda e, chn=chn, sub=sub, dsi=dsi, bank=bank: e.matmul(
                                    out=bank[:],
                                    lhsT=octx[:, chn * 512 + sub * 128:chn * 512 + (sub + 1) * 128],
                                    rhs=oww[:, chn * D + dsi * 512:chn * D + (dsi + 1) * 512],
                                    start=(chn == 0), stop=(chn == NCH - 1),
                                    skip_group_check=True)
                                if not (chn == NCH - 1 and dsi == 3):
                                    em("pe", fn)
                                else:
                                    emi("pe", "PE", fn)
                        pe_sub = val("PE")
                        pe_strip[st_i] = pe_sub
                        wait("ve", "PE", pe_sub)
                        wait("ve", "LD", ld_f)
                        prev_out = out_cnt.get((st_i, sub - 2)) or out_cnt.get((st_i - 1, sub + 2))
                        if prev_out:
                            wait("ve", "OUT", prev_out)
                        for dsi in range(4):
                            bank = o_ps[par * 4 + dsi]
                            fn = lambda e, par=par, dsi=dsi, bank=bank: e.tensor_tensor(
                                out=ffn_sb[par][:, dsi * 512:(dsi + 1) * 512], in0=bank[:],
                                in1=ffn_sb[par][:, dsi * 512:(dsi + 1) * 512], op=ALU.add)
                            if dsi < 3:
                                em("ve", fn)
                            else:
                                emi("ve", "VE", fn)
                        ve_fin[(st_i, sub)] = kb.semi("ve", "VE", lambda e, par=par: e.tensor_tensor(
                            out=x_sb[par][:], in0=ffn_sb[par][:], in1=x_sb[par][:], op=ALU.add))
                        wait("sy", "VE", ve_fin[(st_i, sub)])
                        out_cnt[(st_i, sub)] = emi("sy", "OUT", lambda e, par=par, tok0=tok0: e.dma_start(
                            out=out_t[tok0:tok0 + 128, :], in_=x_sb[par][:]), 16)
            kb.phase_flush(block)

        kb.phase_begin()
        if "xn_sb" in DEBUG_TAPS:
            xndbg = nc.dram_tensor("dbg_xn", [128, NCH * TOK], BF16,
                                   kind="ExternalOutput").ap()
            for s in ("PE", "VE", "AC"):
                wait("sy", s, val(s))
            emi("sy", "OUT", lambda e: e.dma_start(out=xndbg[:, :], in_=xn[:]), 16)
        for name, (dst, src) in dbg.items():
            for s in ("GD", "LD", "CC", "PE", "VE", "AC", "PL"):
                wait("sy", s, val(s))
            emi("sy", "OUT", lambda e, dst=dst, src=src: e.dma_start(
                out=dst[:, :], in_=src[:, :]), 16)
        for s in ("GD", "LD", "CC", "PE", "VE", "AC", "PL", "OUT",
                  "CW", "UP", "DN", "AA"):
            wait("sy", s, val(s))
        for eng in ("pe", "ve", "ac", "gp"):
            wait(eng, "OUT", val("OUT"))
            wait(eng, "LD", val("LD"))
            wait(eng, "GD", val("GD"))
        kb.phase_flush(block)

    ctx.close()
    return nc


def _host_inputs(inputs):
    x = np.asarray(inputs["L0_attn_x"])
    wqkv = np.ascontiguousarray(np.asarray(inputs["L0_attn_Wqkv"], dtype=np.float32))
    outw = np.ascontiguousarray(np.asarray(inputs["L0_attn_out_W"], dtype=np.float32))
    qls = np.asarray(inputs["L0_attn_q_ln_scale"], dtype=np.float32)
    kls = np.asarray(inputs["L0_attn_k_ln_scale"], dtype=np.float32)
    gate = np.asarray(inputs["L0_ffn_gate_w"], dtype=np.float32)
    eup = np.asarray(inputs["L0_ffn_expert_up_w"], dtype=np.float32)
    edn = np.asarray(inputs["L0_ffn_expert_down_w"], dtype=np.float32)
    sup = np.asarray(inputs["L0_ffn_shared_up_w"], dtype=np.float32)
    sdn = np.asarray(inputs["L0_ffn_shared_down_w"], dtype=np.float32)

    upT = np.empty((NE, D, 2816), np.float32)
    upT[:16] = eup.transpose(0, 2, 1)
    supT = sup.T
    upT[16] = np.concatenate([supT[:, 0:1408], supT[:, 2816:4224]], axis=1)
    upT[17] = np.concatenate([supT[:, 1408:2816], supT[:, 4224:5632]], axis=1)
    downT = np.empty((NE, 1408, D), np.float32)
    downT[:16] = edn.transpose(0, 2, 1)
    sdnT = sdn.T
    downT[16] = sdnT[0:1408]
    downT[17] = sdnT[1408:2816]
    upT = np.ascontiguousarray(upT)
    downT = np.ascontiguousarray(downT)
    gwT = np.ascontiguousarray(gate.T)
    qgc = np.ascontiguousarray((qls / np.sqrt(128.0)).reshape(NCH, 128).T)
    kgc = np.ascontiguousarray(kls.reshape(NCH, 128).T)
    idb = np.eye(128, dtype=BF)
    idf = np.eye(128, dtype=np.float32)
    am = np.empty((128, 4 * 512), np.float32)
    ii = np.arange(128)[:, None]
    jj = np.arange(512)[None, :]
    for b in range(4):
        am[:, b * 512:(b + 1) * 512] = np.where((jj - ii) <= 128 * b, 0.0, NEGBIG)

    per_core = []
    for c in range(8):
        bb, rr = c // 4, c % 4
        xs = np.ascontiguousarray(x[bb, rr * TOK:(rr + 1) * TOK, :], dtype=np.float32)
        a_idx = np.empty((128, 48), np.int32)
        for p in range(4):
            for t_i in range(3):
                for si in range(4):
                    col = (p * 3 + t_i) * 4 + si
                    base = 6144 * si + 2048 * t_i + 512 * rr + 128 * p
                    a_idx[:, col] = base + np.arange(128)
        e_idx = np.empty((128, 64), np.int32)
        for st_i in range(4):
            for chn in range(16):
                si, j = divmod(chn, 4)
                base = 2048 * si + 512 * rr + 128 * j
                e_idx[:, st_i * 16 + chn] = 4 * (base + np.arange(128)) + st_i
        per_core.append({
            "x": xs, "wqkv": wqkv, "outw": outw, "upT": upT, "downT": downT,
            "gwT": gwT, "qg": qgc, "kg": kgc, "idb": idb, "idf": idf,
            "amask": am, "aidx": a_idx, "eidx": e_idx,
        })
    return per_core


def kernel(**inputs):
    per_core = _host_inputs(inputs)
    nc = build_program()
    res = run_bass_kernel_spmd(nc, per_core, core_ids=list(range(8)))
    y = np.empty((NB, S, D), np.float32)
    for c in range(8):
        bb, rr = c // 4, c % 4
        y[bb, rr * TOK:(rr + 1) * TOK, :] = res.results[c]["out"]
    return y


# version 13 (final)
# speedup vs baseline: 1.0993x; 1.0212x over previous
import contextlib
import numpy as np
import ml_dtypes
import concourse.bass as bass
import concourse.mybir as mybir
from concourse.bass_core_rust import IndirectOffsetOnAxis
from concourse.bass_utils import run_bass_kernel_spmd

F32 = mybir.dt.float32
BF16 = mybir.dt.bfloat16
I32 = mybir.dt.int32
AF = mybir.ActivationFunctionType
ALU = mybir.AluOpType
AX = mybir.AxisListType

D = 2048
S = 8192
NB = 2
TOK = 2048
NCH = D // 128
NE = 18
ICH = 11
NEGBIG = -1.0e30
E_SPLIT = 6
BF = ml_dtypes.bfloat16

DEBUG_TAPS = []
PHASE_LIMIT = 99


class KB:
    def __init__(self, nc, ctx):
        self.nc = nc
        self.ctx = ctx
        self.cnt = {}
        self.sems = {n: ctx.enter_context(nc.semaphore(n))
                     for n in ("LD", "GD", "PE", "VE", "AC", "PL", "CC", "OUT",
                               "CW", "UP", "DN", "AA")}
        self.ops = None

    def phase_begin(self):
        self.ops = {k: [] for k in ("pe", "ve", "ac", "gp", "sy")}

    def phase_flush(self, block):
        def runner(fns):
            return lambda e: [f(e) for f in fns]
        if self.ops["pe"]:
            block.tensor(runner(self.ops["pe"]))
        if self.ops["ve"]:
            block.vector(runner(self.ops["ve"]))
        if self.ops["ac"]:
            block.scalar(runner(self.ops["ac"]))
        if self.ops["gp"]:
            block.gpsimd(runner(self.ops["gp"]))
        if self.ops["sy"]:
            block.sync(runner(self.ops["sy"]))
        self.ops = None

    def em(self, eng, fn):
        self.ops[eng].append(fn)

    def wait(self, eng, sem, v):
        if v > 0:
            self.em(eng, lambda e, s=self.sems[sem], v=v: e.wait_ge(s, v))

    def bump(self, sem, k=1):
        self.cnt[sem] = self.cnt.get(sem, 0) + k
        return self.sems[sem], self.cnt[sem]

    def val(self, sem):
        return self.cnt.get(sem, 0)

    def emi(self, eng, sem, fn, k=1):
        h, _ = self.bump(sem, k)
        self.em(eng, lambda e, h=h, k=k, fn=fn: fn(e).then_inc(h, k))
        return self.val(sem)

    def semi(self, eng, sem, fn):
        self.wait(eng, sem, self.val(sem))
        return self.emi(eng, sem, fn)


def build_program():
    ctx = contextlib.ExitStack()
    nc = bass.Bass("TRN2", target_bir_lowering=False)
    kb = KB(nc, ctx)

    x_in = nc.dram_tensor("x", [TOK, D], F32, kind="ExternalInput").ap()
    wqkv = nc.dram_tensor("wqkv", [D, 3 * D], F32, kind="ExternalInput").ap()
    outw = nc.dram_tensor("outw", [D, D], F32, kind="ExternalInput").ap()
    upT = nc.dram_tensor("upT", [NE, D, 2816], F32, kind="ExternalInput").ap()
    downT = nc.dram_tensor("downT", [NE, 1408, D], F32, kind="ExternalInput").ap()
    gwT = nc.dram_tensor("gwT", [D, 16], F32, kind="ExternalInput").ap()
    qg_in = nc.dram_tensor("qg", [128, NCH], F32, kind="ExternalInput").ap()
    kg_in = nc.dram_tensor("kg", [128, NCH], F32, kind="ExternalInput").ap()
    idb_in = nc.dram_tensor("idb", [128, 128], BF16, kind="ExternalInput").ap()
    idf_in = nc.dram_tensor("idf", [128, 128], F32, kind="ExternalInput").ap()
    amask_in = nc.dram_tensor("amask", [128, 4 * 512], F32, kind="ExternalInput").ap()
    aidx_in = nc.dram_tensor("aidx", [128, 48], I32, kind="ExternalInput").ap()
    eidx_in = nc.dram_tensor("eidx", [128, 64], I32, kind="ExternalInput").ap()

    out_t = nc.dram_tensor("out", [TOK, D], F32, kind="ExternalOutput").ap()

    contrib = nc.dram_tensor("contrib", [3 * D, TOK], BF16).ap()
    gath = nc.dram_tensor("gath", [4 * 3 * D, TOK], BF16).ap()
    ctxc = nc.dram_tensor("ctxc", [TOK, TOK], BF16).ap()
    gath2 = nc.dram_tensor("gath2", [4 * TOK, TOK], BF16).ap()
    qraw = nc.dram_tensor("qraw", [D, TOK], BF16).ap()
    kraw = nc.dram_tensor("kraw", [D, TOK], BF16).ap()
    rowsAB = nc.dram_tensor("rowsAB", [4, TOK], F32).ap()
    cwT_d = nc.dram_tensor("cwT_d", [NE, TOK], F32).ap()
    ffnacc = nc.dram_tensor("ffnacc", [TOK, D], F32).ap()
    dbg = {}
    for name in DEBUG_TAPS:
        if name in ("xn_sb", "ab_sb"):
            continue
        src = {"contrib": contrib, "gath": gath, "ctxc": ctxc, "gath2": gath2,
               "qraw": qraw, "kraw": kraw, "rowsAB": rowsAB, "cwT_d": cwT_d,
               "ffnacc": ffnacc}[name]
        dbg[name] = (nc.dram_tensor("dbg_" + name, list(src.shape), src.dtype,
                                    kind="ExternalOutput").ap(), src)

    A = lambda *a, **k2: ctx.enter_context(nc.sbuf_tensor(*a, side="left", **k2))
    xn = A("xn", [128, NCH * TOK], BF16)
    idb = A("idb_s", [128, 128], BF16)
    idf = A("idf_s", [128, 128], F32)
    ones_b = A("ones_b", [128, 1], BF16)
    ones_f = A("ones_f", [1, 128], F32)
    ones_fb = A("ones_fb", [1, 128], BF16)
    ones_row = A("ones_row", [1, TOK], F32)
    qg = A("qg_s", [128, NCH], F32)
    kg = A("kg_s", [128, NCH], F32)
    amask = A("amask_s", [128, 4 * 512], F32)
    aidx = A("aidx_s", [128, 48], I32)
    eidx = A("eidx_s", [128, 64], I32)
    gwT_s = A("gwT_s", [128, NCH * 16], BF16)
    eps5 = A("eps5", [128, 1], F32)
    eps6 = A("eps6", [128, 1], F32)
    cwT_s = A("cwT_s", [16, TOK], F32)

    em, wait, emi, val = kb.em, kb.wait, kb.emi, kb.val

    with nc.Block() as block:
        kb.phase_begin()
        for dst, src in [(idb, idb_in), (idf, idf_in), (qg, qg_in), (kg, kg_in),
                         (amask, amask_in), (aidx, aidx_in), (eidx, eidx_in)]:
            emi("sy", "LD", lambda e, d=dst, s=src: e.dma_start(out=d[:], in_=s[:]), 16)
        ld_consts = val("LD")
        wait("ve", "LD", ld_consts)
        em("ve", lambda e: e.memset(ones_b[:], 1.0))
        em("ve", lambda e: e.memset(ones_f[:], 1.0))
        em("ve", lambda e: e.memset(ones_fb[:], 1.0))
        em("ve", lambda e: e.memset(eps5[:], 1.0e-5))
        em("ve", lambda e: e.memset(eps6[:], 1.0e-6))
        ve_ones = emi("ve", "VE", lambda e: e.memset(ones_row[:], 1.0))
        wait("sy", "VE", ve_ones)
        for r in (16, 17):
            emi("sy", "LD", lambda e, r=r: e.dma_start(
                out=cwT_d[r:r + 1, :], in_=ones_row[:]), 16)
        kb.phase_flush(block)

        kb.phase_begin()
        with contextlib.ExitStack() as ph:
            Ar = lambda *a: ph.enter_context(nc.sbuf_tensor(*a, side="right"))
            xbuf = [Ar(f"xb{i}", [128, D], F32) for i in range(2)]
            xnt = [Ar(f"xnt{i}", [128, D], BF16) for i in range(2)]
            sq = [Ar(f"sq1a{i}", [128, D], F32) for i in range(2)]
            ssum = Ar("ssum", [128, 2], F32)
            mean = Ar("mean", [128, 2], F32)
            msq = Ar("msq", [128, 2], F32)
            var = Ar("var", [128, 2], F32)
            sd = Ar("sd", [128, 2], F32)
            nb_ = Ar("nb", [128, 2], F32)
            tps = [ph.enter_context(nc.psum_tensor(f"tp{i}", [128, 512], BF16))
                   for i in range(2)]

            ac_done, pe_done = {}, {}
            ve_copy = {}
            wait("pe", "LD", ld_consts)
            wait("ve", "LD", ld_consts)

            def tile_units(tc):
                buf, xb = xbuf[tc % 2], xnt[tc % 2]
                par = tc % 2
                st8 = {}
                units = []

                def u_load():
                    if tc >= 2:
                        wait("sy", "VE", ac_done[tc - 2])
                    st8["ld"] = emi("sy", "LD", lambda e, b=buf, tc=tc: e.dma_start(
                        out=b[:], in_=x_in[tc * 128:(tc + 1) * 128, :]), 16)
                units.append(u_load)

                def u_sum():
                    wait("ve", "LD", st8["ld"])
                    kb.semi("ve", "VE", lambda e, b=buf: e.reduce_sum(
                        out=ssum[:, par:par + 1], in_=b[:], axis=AX.X))
                units.append(u_sum)

                def u_sq():
                    wait("ac", "LD", st8["ld"])
                    st8["sq"] = emi("ac", "AC", lambda e, b=buf: e.activation(
                        out=sq[par][:], in_=b[:], func=AF.Square))
                units.append(u_sq)

                def u_sum2():
                    wait("ve", "AC", st8["sq"])
                    st8["s1"] = kb.semi("ve", "VE", lambda e: e.reduce_sum(
                        out=msq[:, par:par + 1], in_=sq[par][:], axis=AX.X))
                units.append(u_sum2)

                def u_var():
                    kb.semi("ve", "VE", lambda e: e.tensor_scalar_mul(
                        out=mean[:, par:par + 1], in0=ssum[:, par:par + 1], scalar1=1.0 / D))
                    kb.semi("ve", "VE", lambda e: e.tensor_scalar_mul(
                        out=var[:, par:par + 1], in0=msq[:, par:par + 1], scalar1=1.0 / D))
                    kb.semi("ve", "VE", lambda e: e.scalar_tensor_tensor(
                        out=sd[:, par:par + 1], in0=mean[:, par:par + 1], scalar=-1.0,
                        in1=mean[:, par:par + 1], op0=ALU.mult, op1=ALU.mult))
                    st8["v1"] = kb.semi("ve", "VE", lambda e: e.tensor_tensor(
                        out=var[:, par:par + 1], in0=var[:, par:par + 1],
                        in1=sd[:, par:par + 1], op=ALU.add))
                units.append(u_var)

                def u_sqrt():
                    wait("ac", "VE", st8["v1"])
                    st8["sd"] = emi("ac", "AC", lambda e: e.activation(
                        out=sd[:, par:par + 1], in_=var[:, par:par + 1],
                        func=AF.Sqrt, bias=eps5[:]))
                units.append(u_sqrt)

                def u_norm():
                    wait("ve", "AC", st8["sd"])
                    kb.semi("ve", "VE", lambda e: e.reciprocal(
                        out=sd[:, par:par + 1], in_=sd[:, par:par + 1]))
                    kb.semi("ve", "VE", lambda e: e.scalar_tensor_tensor(
                        out=nb_[:, par:par + 1], in0=mean[:, par:par + 1], scalar=-1.0,
                        in1=sd[:, par:par + 1], op0=ALU.mult, op1=ALU.mult))
                    if tc >= 2:
                        wait("ve", "PE", pe_done[tc - 2])
                    ac_done[tc] = kb.semi("ve", "VE", lambda e, b=buf, xb=xb: e.tensor_scalar(
                        out=xb[:], in0=b[:], scalar1=sd[:, par:par + 1],
                        scalar2=nb_[:, par:par + 1], op0=ALU.mult, op1=ALU.add))
                units.append(u_norm)

                def u_trans():
                    wait("pe", "VE", ac_done[tc])
                    for g in range(4):
                        bank = tps[g % 2]
                        prev = (tc, g - 2) if g >= 2 else (tc - 1, g + 2)
                        if prev in ve_copy:
                            wait("pe", "VE", ve_copy[prev])
                        for d4 in range(4):
                            d = g * 4 + d4
                            fn = lambda e, d=d, d4=d4, bank=bank, xb=xb: e.transpose(
                                out=bank[:, d4 * 128:(d4 + 1) * 128],
                                in_=xb[:, d * 128:(d + 1) * 128], identity=idb[:])
                            if d4 < 3:
                                em("pe", fn)
                            else:
                                emi("pe", "PE", fn)
                        wait("ve", "PE", val("PE"))
                        ve_copy[(tc, g)] = emi("ve", "VE", lambda e, tc=tc, g=g, bank=bank: e.tensor_copy(
                            out=xn[:].rearrange("p (n t) -> p n t", t=TOK)[:, 4 * g:4 * g + 4, tc * 128:(tc + 1) * 128],
                            in_=bank[:].rearrange("p (n t) -> p n t", t=128)))
                    pe_done[tc] = val("PE")
                units.append(u_trans)
                return units

            pend = None
            for tc in range(0, TOK // 128, 2):
                ua = tile_units(tc)
                ub = tile_units(tc + 1)
                order = [ua[0], ub[0], ua[2], ua[1], ub[2], ua[3], ub[1], ua[4],
                         ub[3], ua[5], ub[4], ua[6], ub[5], ua[7], ub[6], ub[7]]
                for u in order:
                    u()
            xn_ready = ve_copy[(TOK // 128 - 1, 3)]
        kb.phase_flush(block)

        ld_contrib_done = 0
        ld_cwt = 0
        cc1 = 0
        if PHASE_LIMIT >= 2:
            kb.phase_begin()
            with contextlib.ExitStack() as ph:
                Ar = lambda *a: ph.enter_context(nc.sbuf_tensor(*a, side="right"))
                Pp = lambda n, s, dt=F32: ph.enter_context(nc.psum_tensor(n, s, dt))
                wbuf = [Ar(f"wb{i}", [128, NCH * 128], BF16) for i in range(2)]
                stg = [Ar(f"stg{i}", [128, TOK], BF16) for i in range(2)]
                sqb = Ar("sqb", [128, TOK], BF16)
                qk_ps = [Pp(f"qk{i}", [128, 512]) for i in range(2)]
                sum_q, ssq_q = Pp("sum_q", [128, 512]), Pp("ssq_q", [128, 512])
                sum_k, ssq_k = Pp("sum_k", [128, 512]), Pp("ssq_k", [128, 512])

                pe_qk, ac_qk, ld_sp = {}, {}, {}
                for f in range(48):
                    isv = f >= 32
                    wb, st = wbuf[f % 2], stg[f % 2]
                    if f >= 2:
                        wait("gp", "PE", pe_qk[(f - 2, 3)])
                    else:
                        wait("gp", "VE", xn_ready)
                    gd_w = emi("gp", "GD", lambda e, f=f, wb=wb: e.dma_start(
                        out=wb[:].rearrange("p (n c) -> p n c", c=128),
                        in_=wqkv[:, f * 128:(f + 1) * 128].rearrange("(n p) c -> p n c", p=128)), 16)
                    for ts in range(4):
                        bank = qk_ps[ts % 2]
                        wait("pe", "GD", gd_w)
                        if f == 0 and ts == 0:
                            wait("pe", "VE", xn_ready)
                        prev = (f, ts - 2) if ts >= 2 else ((f - 1, ts + 2) if f >= 1 else None)
                        if prev in ac_qk:
                            wait("pe", "AC", ac_qk[prev])
                        for d in range(NCH):
                            em("pe", lambda e, wb=wb, d=d: e.ldweights(wb[:, d * 128:(d + 1) * 128]))
                            fn = lambda e, wb=wb, d=d, ts=ts, bank=bank: e.matmul(
                                out=bank[:], lhsT=wb[:, d * 128:(d + 1) * 128],
                                rhs=xn[:, d * TOK + ts * 512:d * TOK + (ts + 1) * 512],
                                start=(d == 0), stop=(d == NCH - 1))
                            if d < NCH - 1:
                                em("pe", fn)
                            else:
                                emi("pe", "PE", fn)
                        mm_done = val("PE")
                        wait("ac", "PE", mm_done)
                        if f >= 2 and ts == 0:
                            wait("ac", "LD", ld_sp[f - 2])
                            wait("ac", "PE", pe_qk[(f - 2, 3)])
                        ac1 = emi("ac", "AC", lambda e, st=st, ts=ts, bank=bank: e.activation(
                            out=st[:, ts * 512:(ts + 1) * 512], in_=bank[:], func=AF.Copy))
                        if not isv:
                            ac1 = emi("ac", "AC", lambda e, ts=ts, bank=bank: e.activation(
                                out=sqb[:, ts * 512:(ts + 1) * 512], in_=bank[:], func=AF.Square))
                        ac_qk[(f, ts)] = ac1
                        if not isv:
                            sm = sum_q if f < 16 else sum_k
                            ss = ssq_q if f < 16 else ssq_k
                            first, last = (f % 16 == 0), (f % 16 == 15)
                            wait("pe", "AC", ac1)
                            em("pe", lambda e: e.ldweights(ones_b[:]))
                            em("pe", lambda e, st=st, ts=ts, sm=sm, first=first, last=last: e.matmul(
                                out=sm[ts * 32:ts * 32 + 1, :], lhsT=ones_b[:],
                                rhs=st[:, ts * 512:(ts + 1) * 512],
                                start=first, stop=last, tile_position=(0, ts * 32),
                                skip_group_check=True))
                            emi("pe", "PE", lambda e, ts=ts, ss=ss, first=first, last=last: e.matmul(
                                out=ss[ts * 32:ts * 32 + 1, :], lhsT=ones_b[:],
                                rhs=sqb[:, ts * 512:(ts + 1) * 512],
                                start=first, stop=last, tile_position=(0, ts * 32),
                                skip_group_check=True))
                        pe_qk[(f, ts)] = val("PE")
                    wait("sy", "AC", ac_qk[(f, 3)])
                    if not isv:
                        wait("sy", "PE", pe_qk[(f, 3)])
                        tgt, r0 = (qraw, f * 128) if f < 16 else (kraw, (f - 16) * 128)
                        ld_sp[f] = emi("sy", "LD", lambda e, st=st, tgt=tgt, r0=r0: e.dma_start(
                            out=tgt[r0:r0 + 128, :], in_=st[:]), 16)
                    else:
                        r0 = 2 * D + (f - 32) * 128
                        ld_sp[f] = emi("sy", "LD", lambda e, st=st, r0=r0: e.dma_start(
                            out=contrib[r0:r0 + 128, :], in_=st[:]), 16)

                fmean = Ar("fmean", [128, 512], F32)
                fvar = Ar("fvar", [128, 512], F32)
                fA = Ar("fA", [128, 512], F32)
                fB = Ar("fB", [128, 512], F32)
                for which, (sm, ss, f_last) in enumerate(
                        [(sum_q, ssq_q, 15), (sum_k, ssq_k, 31)]):
                    wait("ve", "PE", pe_qk[(f_last, 3)])
                    for q4 in range(4):
                        r = q4 * 32
                        kb.semi("ve", "VE", lambda e, sm=sm, r=r: e.tensor_scalar_mul(
                            out=fmean[r:r + 1, :], in0=sm[r:r + 1, :], scalar1=1.0 / D))
                        kb.semi("ve", "VE", lambda e, ss=ss, r=r: e.tensor_scalar_mul(
                            out=fvar[r:r + 1, :], in0=ss[r:r + 1, :], scalar1=1.0 / D))
                        kb.semi("ve", "VE", lambda e, r=r: e.tensor_tensor(
                            out=fA[r:r + 1, :], in0=fmean[r:r + 1, :],
                            in1=fmean[r:r + 1, :], op=ALU.mult))
                        ve1 = kb.semi("ve", "VE", lambda e, r=r: e.tensor_tensor(
                            out=fvar[r:r + 1, :], in0=fvar[r:r + 1, :],
                            in1=fA[r:r + 1, :], op=ALU.subtract))
                        wait("ac", "VE", ve1)
                        ac1 = emi("ac", "AC", lambda e, r=r: e.activation(
                            out=fvar[r:r + 1, :], in_=fvar[r:r + 1, :], func=AF.Sqrt,
                            bias=eps6[r:r + 1, :]))
                        wait("ve", "AC", ac1)
                        kb.semi("ve", "VE", lambda e, r=r: e.reciprocal(out=fA[r:r + 1, :], in_=fvar[r:r + 1, :]))
                        ve2 = kb.semi("ve", "VE", lambda e, r=r: e.scalar_tensor_tensor(
                            out=fB[r:r + 1, :], in0=fmean[r:r + 1, :], scalar=-1.0,
                            in1=fA[r:r + 1, :], op0=ALU.mult, op1=ALU.mult))
                        wait("sy", "VE", ve2)
                        emi("sy", "LD", lambda e, which=which, q4=q4, r=r: e.dma_start(
                            out=rowsAB[2 * which:2 * which + 1, q4 * 512:(q4 + 1) * 512],
                            in_=fA[r:r + 1, :]), 16)
                        emi("sy", "LD", lambda e, which=which, q4=q4, r=r: e.dma_start(
                            out=rowsAB[2 * which + 1:2 * which + 2, q4 * 512:(q4 + 1) * 512],
                            in_=fB[r:r + 1, :]), 16)
                ld_rows = val("LD")

                Abb = [Ar(f"Ab{i}", [128, TOK], F32) for i in range(2)]
                Bbb = [Ar(f"Bb{i}", [128, TOK], F32) for i in range(2)]
                raws = [Ar(f"raw{i}", [128, TOK], BF16) for i in range(2)]
                tmp1 = Ar("tmp1", [128, TOK], F32)
                nstg = [Ar(f"nstg{i}", [128, TOK], BF16) for i in range(2)]
                wait("gp", "LD", ld_rows)
                for which in range(2):
                    emi("gp", "GD", lambda e, which=which: e.dma_start(
                        out=Abb[which][:],
                        in_=rowsAB[2 * which:2 * which + 1, :].partition_broadcast(128)), 16)
                    emi("gp", "GD", lambda e, which=which: e.dma_start(
                        out=Bbb[which][:],
                        in_=rowsAB[2 * which + 1:2 * which + 2, :].partition_broadcast(128)), 16)
                gd_ab = val("GD")
                ld_raw, ve_n, ac_n, ld_c = {}, {}, {}, {}
                for j in range(32):
                    which, f = (0, j) if j < 16 else (1, j - 16)
                    src = qraw if which == 0 else kraw
                    gam = qg if which == 0 else kg
                    rb, ns = raws[j % 2], nstg[j % 2]
                    if j >= 2:
                        wait("sy", "VE", ve_n[j - 2])
                    ld_raw[j] = emi("sy", "LD", lambda e, rb=rb, src=src, f=f: e.dma_start(
                        out=rb[:], in_=src[f * 128:(f + 1) * 128, :]), 16)
                    wait("ve", "LD", ld_raw[j])
                    wait("ve", "GD", gd_ab)
                    if j >= 2:
                        wait("ve", "LD", ld_c[j - 2])
                    kb.semi("ve", "VE", lambda e, rb=rb, which=which: e.tensor_tensor(
                        out=tmp1[:], in0=rb[:], in1=Abb[which][:], op=ALU.mult))
                    kb.semi("ve", "VE", lambda e, which=which: e.tensor_tensor(
                        out=tmp1[:], in0=tmp1[:], in1=Bbb[which][:], op=ALU.add))
                    ve_n[j] = kb.semi("ve", "VE", lambda e, ns=ns, gam=gam, f=f: e.tensor_scalar_mul(
                        out=ns[:], in0=tmp1[:], scalar1=gam[:, f:f + 1]))
                    ac_n[j] = ve_n[j]
                    wait("sy", "VE", ve_n[j])
                    r0 = which * D + f * 128
                    ld_c[j] = emi("sy", "LD", lambda e, ns=ns, r0=r0: e.dma_start(
                        out=contrib[r0:r0 + 128, :], in_=ns[:]), 16)
                ld_contrib_done = val("LD")
            kb.phase_flush(block)

            kb.phase_begin()
            with contextlib.ExitStack() as ph:
                Ar = lambda *a: ph.enter_context(nc.sbuf_tensor(*a, side="right"))
                Pp = lambda n, s: ph.enter_context(nc.psum_tensor(n, s, F32))
                g_ps = [Pp(f"g_ps{i}", [128, 16]) for i in range(2)]
                cwt_ps = [Pp(f"cwt{i}", [16, 128]) for i in range(2)]
                lg = Ar("lg", [128, 16], F32)
                m1 = Ar("m1", [128, 1], F32)
                pr = Ar("pr", [128, 16], F32)
                gs = Ar("gs", [128, 4], F32)
                gm = Ar("gm", [128, 4], F32)
                mp = Ar("mp", [128, 16], F32)
                mpw = Ar("mpw", [128, 16], F32)
                cw = Ar("cw", [128, 16], F32)
                s1 = Ar("s1", [128, 1], F32)

                ve_g, pe_gmm = {}, {}
                wait("gp", "VE", xn_ready)
                gd_gwt = emi("gp", "GD", lambda e: e.dma_start(
                    out=gwT_s[:].rearrange("p (n c) -> p n c", c=16),
                    in_=gwT.rearrange("(n p) c -> p n c", p=128)), 16)
                wait("pe", "GD", gd_gwt)
                for tc in range(16):
                    bank = g_ps[tc % 2]
                    if tc >= 2:
                        wait("pe", "VE", ve_g[tc - 2])
                    for d in range(NCH):
                        em("pe", lambda e, d=d, tc=tc: e.ldweights(
                            xn[:, d * TOK + tc * 128:d * TOK + (tc + 1) * 128]))
                        fn = lambda e, d=d, tc=tc, bank=bank: e.matmul(
                            out=bank[:], lhsT=xn[:, d * TOK + tc * 128:d * TOK + (tc + 1) * 128],
                            rhs=gwT_s[:, d * 16:(d + 1) * 16],
                            start=(d == 0), stop=(d == NCH - 1))
                        if d < NCH - 1:
                            em("pe", fn)
                        else:
                            emi("pe", "PE", fn)
                    wait("ve", "PE", val("PE"))
                    if tc >= 1:
                        wait("ve", "PE", pe_gmm[tc - 1])
                    SV = lambda fn: kb.semi("ve", "VE", fn)
                    SV(lambda e, bank=bank: e.reduce_max(out=m1[:], in_=bank[:], axis=AX.X))
                    SV(lambda e: e.tensor_scalar_mul(out=m1[:], in0=m1[:], scalar1=-1.0))
                    ve1 = SV(lambda e, bank=bank: e.tensor_copy(out=lg[:], in_=bank[:]))
                    wait("ac", "VE", ve1)
                    ac1 = emi("ac", "AC", lambda e: e.activation(
                        out=pr[:], in_=lg[:], func=AF.Exp, bias=m1[:]))
                    wait("ve", "AC", ac1)
                    SV(lambda e: e.reduce_sum(out=s1[:], in_=pr[:], axis=AX.X))
                    SV(lambda e: e.reciprocal(out=s1[:], in_=s1[:]))
                    SV(lambda e: e.tensor_scalar_mul(out=pr[:], in0=pr[:], scalar1=s1[:]))
                    SV(lambda e: e.tensor_tensor(
                        out=gs[:], in0=pr[:, 0::4], in1=pr[:, 1::4], op=ALU.max))
                    SV(lambda e: e.tensor_tensor(
                        out=gs[:], in0=gs[:], in1=pr[:, 2::4], op=ALU.max))
                    SV(lambda e: e.tensor_tensor(
                        out=gs[:], in0=gs[:], in1=pr[:, 3::4], op=ALU.max))
                    SV(lambda e: e.reduce_max(out=m1[:], in_=gs[:], axis=AX.X))
                    SV(lambda e: e.tensor_scalar(
                        out=gm[:], in0=gs[:], scalar1=m1[:], scalar2=None, op0=ALU.is_equal))
                    SV(lambda e: e.scalar_tensor_tensor(
                        out=gm[:], in0=gm[:], scalar=NEGBIG, in1=gs[:], op0=ALU.mult, op1=ALU.add))
                    SV(lambda e: e.reduce_max(out=m1[:], in_=gm[:], axis=AX.X))
                    SV(lambda e: e.tensor_scalar(
                        out=gm[:], in0=gs[:], scalar1=m1[:], scalar2=None, op0=ALU.is_ge))
                    for k4 in range(4):
                        SV(lambda e, k4=k4: e.tensor_tensor(
                            out=mp[:, k4::4], in0=pr[:, k4::4], in1=gm[:], op=ALU.mult))
                    SV(lambda e: e.tensor_copy(out=mpw[:], in_=mp[:]))
                    for _ in range(3):
                        SV(lambda e: e.reduce_max(out=m1[:], in_=mpw[:], axis=AX.X))
                        SV(lambda e: e.tensor_scalar(
                            out=cw[:], in0=mpw[:], scalar1=m1[:], scalar2=None, op0=ALU.is_equal))
                        SV(lambda e: e.scalar_tensor_tensor(
                            out=mpw[:], in0=cw[:], scalar=NEGBIG, in1=mpw[:],
                            op0=ALU.mult, op1=ALU.add))
                    SV(lambda e: e.reduce_max(out=m1[:], in_=mpw[:], axis=AX.X))
                    SV(lambda e: e.tensor_scalar(
                        out=cw[:], in0=mp[:], scalar1=m1[:], scalar2=None, op0=ALU.is_ge))
                    SV(lambda e: e.tensor_tensor(out=cw[:], in0=cw[:], in1=mp[:], op=ALU.mult))
                    SV(lambda e: e.reduce_sum(out=s1[:], in_=cw[:], axis=AX.X))
                    SV(lambda e: e.tensor_scalar_add(out=s1[:], in0=s1[:], scalar1=1.0e-20))
                    SV(lambda e: e.reciprocal(out=s1[:], in_=s1[:]))
                    ve_cw = SV(lambda e: e.tensor_scalar_mul(
                        out=cw[:], in0=cw[:], scalar1=s1[:]))
                    cb = cwt_ps[tc % 2]
                    wait("pe", "VE", ve_cw)
                    pe_gmm[tc] = emi("pe", "PE", lambda e, cb=cb: e.matmul(
                        out=cb[:], lhsT=cw[:], rhs=idf[:], start=True, stop=True))
                    wait("ve", "PE", pe_gmm[tc])
                    ve_g[tc] = emi("ve", "VE", lambda e, cb=cb, tc=tc: e.tensor_copy(
                        out=cwT_s[:, tc * 128:(tc + 1) * 128], in_=cb[:]))
                wait("sy", "VE", ve_g[15])
                ld_cwt = emi("sy", "LD", lambda e: e.dma_start(
                    out=cwT_d[0:16, :], in_=cwT_s[:]), 16)
            kb.phase_flush(block)

            kb.phase_begin()
            wait("gp", "LD", ld_contrib_done)
            for s_ in ("GD", "UP", "DN", "AA", "CW"):
                wait("gp", s_, val(s_))
            cc1 = emi("gp", "CC", lambda e: e.collective_compute(
                "AllGather", ALU.bypass,
                replica_groups=[[0, 1, 2, 3], [4, 5, 6, 7]],
                ins=[contrib[:, :].opt()], outs=[gath[:, :].opt()]))
            kb.phase_flush(block)

        st_ffn = {"piece_free": {}, "down_free": {}, "down_ac": {}, "cwb_free": {},
                  "up_prev_ve": {}, "down_strip_free": {}, "acc_cell": {},
                  "gd_piece": {}, "cwb_mm": {}, "cwb_ac_prev": {}}

        def make_piece_emitter(bufs):
            (upg_sl, dwn, ab_, cwb, t1, fstage, g_ps, u_ps, d_banks, cwrow,
             cw_ps, stg32) = bufs
            nsl = len(upg_sl)

            def emit_piece_dma(e_idx, i, safe=False):
                if (e_idx, i) in st_ffn["gd_piece"]:
                    return
                sl = upg_sl[i % nsl]
                jprev = ICH - 1 - ((ICH - 1 - (i % nsl)) % nsl)
                prev = st_ffn["piece_free"].get((e_idx, i - nsl)) or \
                    st_ffn["piece_free"].get((e_idx - 1, jprev))
                if not safe:
                    if prev:
                        wait("gp", "PE", prev)
                    emi("gp", "UP", lambda e, sl=sl, i=i, e_idx=e_idx: e.dma_start(
                        out=sl[:, 0:NCH * 128].rearrange("p (n c) -> p n c", c=128),
                        in_=upT[e_idx, :, i * 128:(i + 1) * 128].rearrange("(n p) c -> p n c", p=128)), 16)
                    st_ffn["gd_piece"][(e_idx, i)] = ("UP", emi("gp", "UP",
                        lambda e, sl=sl, i=i, e_idx=e_idx: e.dma_start(
                            out=sl[:, NCH * 128:].rearrange("p (n c) -> p n c", c=128),
                            in_=upT[e_idx, :, 1408 + i * 128:1408 + (i + 1) * 128].rearrange("(n p) c -> p n c", p=128)), 16))
                else:
                    if prev:
                        wait("sy", "PE", prev)
                        wait("gp", "PE", prev)
                    for half in range(2):
                        f0 = i * 128 if half == 0 else 1408 + i * 128
                        off = half * NCH * 128
                        prev_cast = st_ffn.get("safe_cast", 0)
                        if prev_cast:
                            wait("sy", "PL", prev_cast)
                        ldh = emi("sy", "LD", lambda e, f0=f0: e.dma_start(
                            out=stg32[:].rearrange("p (n c) -> p n c", c=128),
                            in_=upT[e_idx, :, f0:f0 + 128].rearrange("(n p) c -> p n c", p=128)), 16)
                        wait("gp", "LD", ldh)
                        st_ffn["safe_cast"] = emi("gp", "PL", lambda e, sl=sl, off=off: e.tensor_copy(
                            out=sl[:, off:off + NCH * 128], in_=stg32[:]))
                    st_ffn["gd_piece"][(e_idx, i)] = ("PL", st_ffn["safe_cast"])
            return emit_piece_dma

        def ffn_expert(e_idx, first_expert, bufs, emit_piece_dma):
            (upg_sl, dwn, ab_, cwb, t1, fstage, g_ps, u_ps, d_banks, cwrow,
             cw_ps, stg32) = bufs
            nsl = len(upg_sl)
            safe = (e_idx == first_expert)
            wait("sy", "LD", ld_cwt)
            prev_mm = st_ffn["cwb_mm"].get(e_idx - 1, 0)
            if prev_mm:
                wait("sy", "PE", prev_mm)
            ld_row = emi("sy", "LD", lambda e, e_idx=e_idx: e.dma_start(
                out=cwrow[:], in_=cwT_d[e_idx:e_idx + 1, :]), 16)
            prev_free = st_ffn["cwb_free"].get(e_idx - 1, 0)
            ac_cwb = 0
            for q4 in range(4):
                wait("pe", "LD", ld_row)
                if q4 == 0 and st_ffn["cwb_ac_prev"].get(e_idx - 1):
                    wait("pe", "AC", st_ffn["cwb_ac_prev"][e_idx - 1])
                pe_b = emi("pe", "PE", lambda e, q4=q4: e.matmul(
                    out=cw_ps[:], lhsT=ones_f[:], rhs=cwrow[0:1, q4 * 512:(q4 + 1) * 512],
                    start=True, stop=True))
                wait("ac", "PE", pe_b)
                if q4 == 0 and prev_free:
                    wait("ac", "VE", prev_free)
                ac_cwb = emi("ac", "AC", lambda e, q4=q4: e.activation(
                    out=cwb[:, q4 * 512:(q4 + 1) * 512], in_=cw_ps[:], func=AF.Copy))
                if q4 < 3:
                    wait("pe", "AC", ac_cwb)
            st_ffn["cwb_mm"][e_idx] = val("PE")
            st_ffn["cwb_ac_prev"][e_idx] = ac_cwb
            gd_cwb = ac_cwb
            for i in range(nsl):
                emit_piece_dma(e_idx, i, safe=safe)
            ve_a = {}
            first_up_wait = max(st_ffn["down_free"].get(e_idx - 1, 0), 0)
            ac_prev_expert = st_ffn["down_ac"].get(e_idx - 1, 0)
            for i in range(ICH):
                sl = upg_sl[i % nsl]
                for ts in range(4):
                    par = (i * 4 + ts) % 2
                    gb, ub = g_ps[par], u_ps[par]
                    _sem, _v = st_ffn["gd_piece"][(e_idx, i)]
                    wait("pe", _sem, _v)
                    if i == 0 and ts == 0:
                        if e_idx == first_expert and first_expert == 0:
                            wait("pe", "VE", xn_ready)
                        if ac_prev_expert:
                            wait("pe", "AC", ac_prev_expert)
                    prev = ve_a.get((i, ts - 2)) or ve_a.get((i - 1, ts + 2)) or \
                        st_ffn["up_prev_ve"].get(e_idx, 0)
                    if prev:
                        wait("pe", "VE", prev)
                    for half, bank in ((0, gb), (1, ub)):
                        off = half * NCH * 128
                        for d in range(NCH):
                            em("pe", lambda e, sl=sl, off=off, d=d: e.ldweights(
                                sl[:, off + d * 128:off + (d + 1) * 128]))
                            fn = lambda e, sl=sl, off=off, d=d, ts=ts, bank=bank: e.matmul(
                                out=bank[:], lhsT=sl[:, off + d * 128:off + (d + 1) * 128],
                                rhs=xn[:, d * TOK + ts * 512:d * TOK + (ts + 1) * 512],
                                start=(d == 0), stop=(d == NCH - 1))
                            if d < NCH - 1:
                                em("pe", fn)
                            else:
                                emi("pe", "PE", fn)
                    pe_ui = val("PE")
                    wait("ac", "PE", pe_ui)
                    tprev = ve_a.get((i, ts - 2)) or ve_a.get((i - 1, ts + 2))
                    if tprev:
                        wait("ac", "VE", tprev)
                    ac_s = emi("ac", "AC", lambda e, gb=gb, ts=ts: e.activation(
                        out=t1[:, (ts % 2) * 512:(ts % 2 + 1) * 512], in_=gb[:], func=AF.Silu))
                    wait("ve", "AC", ac_s)
                    wait("ve", "AC", gd_cwb)
                    if i == 0 and ts == 0:
                        dpf = st_ffn["down_free"].get(e_idx - 1, 0)
                        if dpf:
                            wait("ve", "PE", dpf)
                    kb.semi("ve", "VE", lambda e, ub=ub, ts=ts: e.tensor_tensor(
                        out=t1[:, (ts % 2) * 512:(ts % 2 + 1) * 512],
                        in0=t1[:, (ts % 2) * 512:(ts % 2 + 1) * 512], in1=ub[:], op=ALU.mult))
                    ve_a[(i, ts)] = kb.semi("ve", "VE", lambda e, i=i, ts=ts: e.tensor_tensor(
                        out=ab_[:, i * TOK + ts * 512:i * TOK + (ts + 1) * 512],
                        in0=t1[:, (ts % 2) * 512:(ts % 2 + 1) * 512],
                        in1=cwb[:, ts * 512:(ts + 1) * 512], op=ALU.mult))
                st_ffn["piece_free"][(e_idx, i)] = val("PE")
                if i + nsl < ICH:
                    emit_piece_dma(e_idx, i + nsl, safe=safe)
            st_ffn["up_prev_ve"][e_idx + 1] = val("VE")
            st_ffn["cwb_free"][e_idx] = val("VE")
            ve_a_done = val("VE")
            ac_ev, gd_acc = {}, {}
            gd_down = {}

            def emit_down_dma(dsi, guard_pe):
                dw = dwn[dsi % 2]
                if guard_pe:
                    wait("gp", "PE", guard_pe)
                for i in range(ICH):
                    gd_down[dsi] = emi("gp", "DN", lambda e, i=i, dsi=dsi, dw=dw, e_idx=e_idx: e.dma_start(
                        out=dw[:, i * 512:(i + 1) * 512],
                        in_=downT[e_idx, i * 128:(i + 1) * 128, dsi * 512:(dsi + 1) * 512]), 16)

            if safe:
                wait("gp", "CC", val("CC"))
            for dsi in (0, 1):
                emit_down_dma(dsi, st_ffn["down_strip_free"].get((e_idx - 1, dsi + 2), 0))
            for dsi in range(4):
                dw = dwn[dsi % 2]
                if dsi == 2 and e_idx + 1 < NE and not st_ffn.get("no_prefetch"):
                    emit_piece_dma(e_idx + 1, 0)
                    emit_piece_dma(e_idx + 1, 1)
                wait("pe", "DN", gd_down[dsi])
                if dsi == 0:
                    wait("pe", "VE", ve_a_done)
                for sub in range(16):
                    par = sub % 2
                    bank = d_banks[par]
                    prev = ac_ev.get((dsi, sub - 2)) or ac_ev.get((dsi - 1, sub + 14))
                    if prev:
                        wait("pe", "AC", prev)
                    for i in range(ICH):
                        em("pe", lambda e, i=i, sub=sub: e.ldweights(
                            ab_[:, i * TOK + sub * 128:i * TOK + (sub + 1) * 128]))
                        fn = lambda e, i=i, sub=sub, dsi=dsi, bank=bank, dw=dw: e.matmul(
                            out=bank[:], lhsT=ab_[:, i * TOK + sub * 128:i * TOK + (sub + 1) * 128],
                            rhs=dw[:, i * 512:(i + 1) * 512],
                            start=(i == 0), stop=(i == ICH - 1))
                        if i < ICH - 1:
                            em("pe", fn)
                        else:
                            emi("pe", "PE", fn)
                    pe_sub = val("PE")
                    fs = fstage[par]
                    wait("ac", "PE", pe_sub)
                    prevg = gd_acc.get((dsi, sub - 2)) or gd_acc.get((dsi - 1, sub + 14))
                    if prevg:
                        wait("ac", "AA", prevg)
                    ac_ev[(dsi, sub)] = emi("ac", "AC", lambda e, fs=fs, bank=bank: e.activation(
                        out=fs[:], in_=bank[:], func=AF.Copy))
                    wait("gp", "AC", ac_ev[(dsi, sub)])
                    prev_cell = st_ffn["acc_cell"].get((dsi, sub), 0)
                    if prev_cell:
                        wait("gp", "AA", prev_cell)
                    op = ALU.bypass if (e_idx == 0 and first_expert == 0) else ALU.add
                    gd_acc[(dsi, sub)] = emi("gp", "AA", lambda e, fs=fs, sub=sub, dsi=dsi, op=op: e.dma_start(
                        out=ffnacc[sub * 128:(sub + 1) * 128, dsi * 512:(dsi + 1) * 512],
                        in_=fs[:], accum_op=op), 16)
                    st_ffn["acc_cell"][(dsi, sub)] = gd_acc[(dsi, sub)]
                st_ffn["down_strip_free"][(e_idx, dsi)] = val("PE")
                if dsi + 2 < 4:
                    emit_down_dma(dsi + 2, st_ffn["down_strip_free"][(e_idx, dsi)])
            st_ffn["down_free"][e_idx] = val("PE")
            st_ffn["down_ac"][e_idx] = val("AC")
            return val("AA")

        def ffn_phase_bufs(ph, tag):
            Ar = lambda *a: ph.enter_context(nc.sbuf_tensor(*a, side="right"))
            Pp = lambda n, s: ph.enter_context(nc.psum_tensor(n, s, F32))
            upg_sl = [Ar(f"upsl{i}{tag}", [128, 2 * NCH * 128], BF16) for i in range(2)]
            dwn = [Ar(f"dwn{i}{tag}", [128, ICH * 512], BF16) for i in range(2)]
            ab_ = Ar(f"ab{tag}", [128, ICH * TOK], BF16)
            cwb = Ar(f"cwb{tag}", [128, TOK], F32)
            cwrow = Ar(f"cwrow{tag}", [1, TOK], F32)
            stg32 = Ar(f"stg32{tag}", [128, NCH * 128], F32)
            t1 = Ar(f"t1{tag}", [128, 1024], F32)
            fstage = [Ar(f"fst{i}{tag}", [128, 512], F32) for i in range(2)]
            g_ps = [Pp(f"gps{i}{tag}", [128, 512]) for i in range(2)]
            u_ps = [Pp(f"ups{i}{tag}", [128, 512]) for i in range(2)]
            d_banks = [Pp(f"dps{i}{tag}", [128, 512]) for i in range(2)]
            cw_ps = Pp(f"cwps{tag}", [128, 512])
            return (upg_sl, dwn, ab_, cwb, t1, fstage, g_ps, u_ps, d_banks, cwrow,
                    cw_ps, stg32)

        gd_ffn_done = 0
        if PHASE_LIMIT >= 3:
            kb.phase_begin()
            with contextlib.ExitStack() as ph:
                bufs = ffn_phase_bufs(ph, "a")
                epd = make_piece_emitter(bufs)
                st_ffn["no_prefetch"] = False
                for e_i in range(E_SPLIT):
                    st_ffn["no_prefetch"] = (e_i == E_SPLIT - 1)
                    gd_ffn_done = ffn_expert(e_i, 0, bufs, epd)
                if "ab_sb" in DEBUG_TAPS:
                    abdbg = nc.dram_tensor("dbg_ab", [128, ICH * TOK], BF16,
                                           kind="ExternalOutput").ap()
                    wait("sy", "VE", val("VE"))
                    emi("sy", "OUT", lambda e: e.dma_start(
                        out=abdbg[:, :], in_=bufs[2][:]), 16)
            kb.phase_flush(block)

        ld_ctx_done = 0
        cc2 = 0
        if PHASE_LIMIT >= 4:
            kb.phase_begin()
            with contextlib.ExitStack() as ph:
                Ar = lambda *a: ph.enter_context(nc.sbuf_tensor(*a, side="right"))
                Pp = lambda n, s, dt=F32: ph.enter_context(nc.psum_tensor(n, s, dt))
                qh = Ar("qh", [128, S], BF16)
                kh = Ar("kh", [128, S], BF16)
                vf = Ar("vf", [128, S], BF16)
                vt = Ar("vt", [128, S], BF16)
                ssb = Ar("ssb", [128, 512], F32)
                ssb2 = Ar("ssb2", [128, 512], F32)
                mq = Ar("mq", [128, 4], F32)
                msl = Ar("msl", [128, 16], F32)
                nrow = Ar("nrow", [1, 512], F32)
                nmb = Ar("nmb", [128, 512], F32)
                rlrow = Ar("rlrow", [1, 512], F32)
                rlb = Ar("rlb", [128, 512], F32)
                pt = [Ar(f"pt{i}", [128, 512], BF16) for i in range(4)]
                ctxo = Ar("ctxo", [128, 512], BF16)
                sps = [Pp(f"sps{i}", [128, 512]) for i in range(2)]
                stp = [Pp(f"stp{i}", [128, 512]) for i in range(4)]
                ctx_ps = Pp("ctx_ps", [128, 512])
                br_ps = Pp("br_ps", [128, 512])
                vt_ps = [s[:].bitcast(BF16) for s in sps]

                kb.last_ctxo_ve = 0
                kb.last_ctx_ld = 0
                kb.last_rlb_ac = 0
                kb.last_pe_tr = 0
                kb.passa_prev_red = 0
                kb.passa_prev_combine = 0
                wait("ve", "LD", ld_consts)
                wait("gp", "LD", ld_consts)
                pair_pe, pair_ve, pair_ac, pair_pl = 0, 0, 0, 0
                for pair in range(4):
                    wait("gp", "CC", cc1)
                    if pair > 0:
                        wait("gp", "PE", pair_pe)
                        wait("gp", "VE", pair_ve)
                        wait("gp", "AC", pair_ac)
                    for t_i, dst in ((0, qh), (1, kh), (2, vf)):
                        for si in range(4):
                            col = (pair * 3 + t_i) * 4 + si
                            emi("gp", "GD", lambda e, dst=dst, si=si, col=col: e.indirect_dma_start(
                                out=dst[:, si * TOK:(si + 1) * TOK], out_offset=None,
                                in_=gath[:, :],
                                in_offset=IndirectOffsetOnAxis(aidx[:, col:col + 1], 0)), 16)
                    gd_qkv = val("GD")
                    wait("pe", "GD", gd_qkv)
                    if kb.passa_prev_red:
                        wait("pe", "VE", kb.passa_prev_red)
                    ve_vt = {}
                    for g in range(16):
                        bank = vt_ps[g % 2]
                        if g >= 2:
                            wait("pe", "VE", ve_vt[g - 2])
                        for b4 in range(4):
                            kk = g * 4 + b4
                            fn = lambda e, kk=kk, b4=b4, bank=bank: e.transpose(
                                out=bank[:, b4 * 128:(b4 + 1) * 128],
                                in_=vf[:, kk * 128:(kk + 1) * 128], identity=idb[:])
                            if b4 < 3:
                                em("pe", fn)
                            else:
                                emi("pe", "PE", fn)
                        wait("ve", "PE", val("PE"))
                        ve_vt[g] = emi("ve", "VE", lambda e, g=g, bank=bank: e.tensor_copy(
                            out=vt[:, g * 512:(g + 1) * 512], in_=bank[:, 0:512]))
                    ve_vt_done = val("VE")
                    kb.passa_prev_red = val("VE")

                    def passA_units(st):
                        units = []
                        state = {"prev_combine": kb.passa_prev_combine}
                        for b in range(4):
                            qt = st * 4 + b
                            nkc = qt // 4 + 1
                            ve_red = {}
                            for kc in range(nkc):
                                def unit(b=b, qt=qt, kc=kc, nkc=nkc, ve_red=ve_red, state=state):
                                    bank = sps[kc % 2]
                                    if kc >= 2:
                                        wait("pe", "VE", ve_red[kc - 2])
                                    elif kb.passa_prev_red:
                                        wait("pe", "VE", kb.passa_prev_red)
                                    em("pe", lambda e, qt=qt: e.ldweights(
                                        qh[:, qt * 128:(qt + 1) * 128]))
                                    pe_s = emi("pe", "PE", lambda e, qt=qt, kc=kc, bank=bank: e.matmul(
                                        out=bank[:], lhsT=qh[:, qt * 128:(qt + 1) * 128],
                                        rhs=kh[:, kc * 512:(kc + 1) * 512], start=True, stop=True))
                                    wait("ve", "PE", pe_s)
                                    if kc == 0 and state["prev_combine"]:
                                        wait("ve", "VE", state["prev_combine"])
                                    if kc == nkc - 1:
                                        emi("ve", "VE", lambda e, bank=bank, b=b: e.scalar_tensor_tensor(
                                            out=ssb2[:], in0=bank[:], scalar=1.0,
                                            in1=amask[:, b * 512:(b + 1) * 512],
                                            op0=ALU.mult, op1=ALU.add))
                                        wait("ve", "VE", val("VE"))
                                        ve_red[kc] = emi("ve", "VE", lambda e, kc=kc: e.reduce_max(
                                            out=msl[:, kc:kc + 1], in_=ssb2[:], axis=AX.X))
                                        kb.passa_prev_red = ve_red[kc]
                                        wait("ve", "VE", val("VE"))
                                        if kb.last_pe_tr:
                                            wait("ve", "PE", kb.last_pe_tr)
                                        state["prev_combine"] = emi("ve", "VE",
                                            lambda e, b=b, nkc=nkc: e.reduce_max(
                                                out=mq[:, b:b + 1], in_=msl[:, 0:nkc], axis=AX.X))
                                        kb.passa_prev_combine = state["prev_combine"]
                                    else:
                                        ve_red[kc] = emi("ve", "VE", lambda e, bank=bank, kc=kc: e.reduce_max(
                                            out=msl[:, kc:kc + 1], in_=bank[:], axis=AX.X))
                                        kb.passa_prev_red = ve_red[kc]
                                units.append(unit)
                        return units

                    def emit_negm(st):
                        wait("pe", "VE", kb.passa_prev_combine)
                        if kb.last_rlb_ac:
                            wait("pe", "AC", kb.last_rlb_ac)
                        pe_tr = 0
                        for b in range(4):
                            fn = lambda e, b=b: e.transpose(
                                out=br_ps[0:1, b * 128:(b + 1) * 128],
                                in_=mq[:, b:b + 1], identity=idf[:])
                            if b < 3:
                                em("pe", fn)
                            else:
                                pe_tr = emi("pe", "PE", fn)
                        kb.last_pe_tr = pe_tr
                        wait("ve", "PE", pe_tr)
                        ve_nrow = emi("ve", "VE", lambda e: e.tensor_scalar_mul(
                            out=nrow[:], in0=br_ps[0:1, 0:512], scalar1=-1.0))
                        wait("pe", "VE", ve_nrow)
                        pe_nb = emi("pe", "PE", lambda e: e.matmul(
                            out=br_ps[:], lhsT=ones_f[:], rhs=nrow[:], start=True, stop=True))
                        wait("ac", "PE", pe_nb)
                        return emi("ac", "AC", lambda e: e.activation(
                            out=nmb[:], in_=br_ps[:], func=AF.Copy))

                    def passB_units(st, ac_nmb, shared):
                        nkc8 = 4 * (st + 1)
                        ve_es, ac_pt, ctx_pe, pl_pt = {}, {}, {}, {}
                        shared["ctx_pe"] = ctx_pe
                        shared["nkc8"] = nkc8

                        def emit_ctx_l(kc):
                            p = pt[kc % 4]
                            if kc >= 4 * st:
                                wait("pe", "PL", pl_pt[kc])
                            else:
                                wait("pe", "AC", ac_pt[kc])
                            em("pe", lambda e, kc=kc: e.ldweights(vt[:, kc * 128:(kc + 1) * 128]))
                            em("pe", lambda e, kc=kc, p=p, nkc8=nkc8: e.matmul(
                                out=ctx_ps[:], lhsT=vt[:, kc * 128:(kc + 1) * 128], rhs=p[:],
                                start=(kc == 0), stop=(kc == nkc8 - 1),
                                skip_group_check=True))
                            em("pe", lambda e: e.ldweights(ones_b[:]))
                            ctx_pe[kc] = emi("pe", "PE", lambda e, kc=kc, p=p, nkc8=nkc8: e.matmul(
                                out=br_ps[0:1, :], lhsT=ones_b[:], rhs=p[:],
                                start=(kc == 0), stop=(kc == nkc8 - 1),
                                skip_group_check=True))

                        units = []
                        LOOK = 3
                        for kc in range(nkc8):
                            def unit(kc=kc, st=st, ac_nmb=ac_nmb):
                                bank = stp[kc % 4]
                                if kc == 0:
                                    wait("pe", "VE", max(ve_vt_done, kb.last_ctxo_ve))
                                prevv = ve_es.get(kc - 4)
                                if prevv:
                                    wait("pe", "VE", prevv)
                                em("pe", lambda e, kc=kc: e.ldweights(kh[:, kc * 128:(kc + 1) * 128]))
                                pe_st = emi("pe", "PE", lambda e, kc=kc, st=st, bank=bank: e.matmul(
                                    out=bank[:], lhsT=kh[:, kc * 128:(kc + 1) * 128],
                                    rhs=qh[:, st * 512:(st + 1) * 512], start=True, stop=True))
                                wait("ve", "PE", pe_st)
                                wait("ve", "AC", ac_nmb)
                                if kc > 0:
                                    wait("ve", "AC", ac_pt[kc - 1])
                                ve_es[kc] = emi("ve", "VE", lambda e, bank=bank: e.scalar_tensor_tensor(
                                    out=ssb[:], in0=bank[:], scalar=1.0, in1=nmb[:],
                                    op0=ALU.mult, op1=ALU.add))
                                wait("ac", "VE", ve_es[kc])
                                prev_pe = ctx_pe.get(kc - 4)
                                if prev_pe:
                                    wait("ac", "PE", prev_pe)
                                p = pt[kc % 4]
                                ac_pt[kc] = emi("ac", "AC", lambda e, p=p: e.activation(
                                    out=p[:], in_=ssb[:], func=AF.Exp))
                                if kc >= 4 * st:
                                    wait("gp", "AC", ac_pt[kc])
                                    pl_pt[kc] = emi("gp", "PL", lambda e, p=p, st=st, kc=kc: e.affine_select(
                                        out=p[:], in_=p[:], pattern=[[1, 512]],
                                        base=512 * st - 128 * kc, channel_multiplier=-1,
                                        compare_op=ALU.is_ge, fill=0.0))
                                if kc >= LOOK:
                                    emit_ctx_l(kc - LOOK)
                                if kc == nkc8 - 1:
                                    for t_ in range(max(0, nkc8 - LOOK), nkc8):
                                        emit_ctx_l(t_)
                            units.append(unit)
                        return units

                    def finish_strip(st, shared):
                        ctx_pe = shared["ctx_pe"]
                        nkc8 = shared["nkc8"]
                        wait("ve", "PE", ctx_pe[nkc8 - 1])
                        ve_rl = emi("ve", "VE", lambda e: e.reciprocal(
                            out=rlrow[:], in_=br_ps[0:1, :]))
                        wait("pe", "VE", ve_rl)
                        pe_rlb = emi("pe", "PE", lambda e: e.matmul(
                            out=br_ps[:], lhsT=ones_f[:], rhs=rlrow[:], start=True, stop=True))
                        wait("ac", "PE", pe_rlb)
                        kb.last_rlb_ac = emi("ac", "AC", lambda e: e.activation(
                            out=rlb[:], in_=br_ps[:], func=AF.Copy))
                        wait("ve", "AC", kb.last_rlb_ac)
                        if kb.last_ctx_ld:
                            wait("ve", "LD", kb.last_ctx_ld)
                        kb.last_ctxo_ve = emi("ve", "VE", lambda e: e.tensor_tensor(
                            out=ctxo[:], in0=ctx_ps[:], in1=rlb[:], op=ALU.mult))
                        wait("sy", "VE", kb.last_ctxo_ve)
                        rj, stl = st // 4, st % 4
                        r0 = 512 * rj + 128 * pair
                        kb.last_ctx_ld = emi("sy", "LD", lambda e, r0=r0, stl=stl: e.dma_start(
                            out=ctxc[r0:r0 + 128, stl * 512:(stl + 1) * 512], in_=ctxo[:]), 16)

                    for u in passA_units(0):
                        u()
                    for st_i in range(16):
                        ac_nmb = emit_negm(st_i)
                        ua = passA_units(st_i + 1) if st_i < 15 else []
                        shared = {}
                        ub = passB_units(st_i, ac_nmb, shared)
                        ia, ib = 0, 0
                        while ia < len(ua) or ib < len(ub):
                            if ib < len(ub):
                                ub[ib]()
                                ib += 1
                            if ia < len(ua):
                                ua[ia]()
                                ia += 1
                        finish_strip(st_i, shared)
                    pair_pe, pair_ve, pair_ac = val("PE"), val("VE"), val("AC")
                ld_ctx_done = val("LD")
            kb.phase_flush(block)

            kb.phase_begin()
            wait("gp", "LD", ld_ctx_done)
            for s_ in ("GD", "UP", "DN", "AA", "CW"):
                wait("gp", s_, val(s_))
            cc2 = emi("gp", "CC", lambda e: e.collective_compute(
                "AllGather", ALU.bypass,
                replica_groups=[[0, 1, 2, 3], [4, 5, 6, 7]],
                ins=[ctxc[:, :].opt()], outs=[gath2[:, :].opt()]))
            kb.phase_flush(block)

        if PHASE_LIMIT >= 5:
            kb.phase_begin()
            with contextlib.ExitStack() as ph:
                bufs = ffn_phase_bufs(ph, "b")
                st_ffn["piece_free"] = {}
                st_ffn["down_free"] = {}
                st_ffn["down_ac"] = {}
                st_ffn["cwb_free"] = {}
                st_ffn["up_prev_ve"] = {}
                st_ffn["down_strip_free"] = {}
                st_ffn["gd_piece"] = {}
                epd = make_piece_emitter(bufs)
                st_ffn["no_prefetch"] = False
                for e_i in range(E_SPLIT, NE):
                    st_ffn["no_prefetch"] = (e_i == NE - 1)
                    gd_ffn_done = ffn_expert(e_i, E_SPLIT, bufs, epd)
            kb.phase_flush(block)

        if PHASE_LIMIT >= 6:
            kb.phase_begin()
            with contextlib.ExitStack() as ph:
                Ar = lambda *a: ph.enter_context(nc.sbuf_tensor(*a, side="right"))
                Pp = lambda n, s: ph.enter_context(nc.psum_tensor(n, s, F32))
                oww = Ar("oww", [128, NCH * D], BF16)
                octx = Ar("octx", [128, NCH * 512], BF16)
                ffn_sb = [Ar(f"ffn_sb{i}", [128, D], F32) for i in range(2)]
                x_sb = [Ar(f"x_sb{i}", [128, D], F32) for i in range(2)]
                o_ps = [Pp(f"ops{i}", [128, 512]) for i in range(8)]

                wait("gp", "AA", gd_ffn_done)
                wait("sy", "AA", gd_ffn_done)
                for chn in range(NCH):
                    emi("gp", "GD", lambda e, chn=chn: e.dma_start(
                        out=oww[:, chn * D:(chn + 1) * D],
                        in_=outw[chn * 128:(chn + 1) * 128, :]), 16)
                gd_oww = val("GD")
                ve_fin = {}
                out_cnt = {}
                pe_strip = {}
                for st_i in range(4):
                    wait("gp", "CC", cc2)
                    if st_i >= 1:
                        wait("gp", "PE", pe_strip[st_i - 1])
                    for chn in range(NCH):
                        col = st_i * 16 + chn
                        emi("gp", "GD", lambda e, chn=chn, col=col: e.indirect_dma_start(
                            out=octx[:, chn * 512:(chn + 1) * 512], out_offset=None,
                            in_=gath2.rearrange("r (s c) -> (r s) c", c=512),
                            in_offset=IndirectOffsetOnAxis(eidx[:, col:col + 1], 0)), 16)
                    gd_ctx = val("GD")
                    for sub in range(4):
                        tok0 = st_i * 512 + sub * 128
                        par = sub % 2
                        prevv = ve_fin.get((st_i, sub - 2)) or ve_fin.get((st_i - 1, sub + 2))
                        if prevv:
                            wait("sy", "VE", prevv)
                        emi("sy", "LD", lambda e, par=par, tok0=tok0: e.dma_start(
                            out=ffn_sb[par][:], in_=ffnacc[tok0:tok0 + 128, :]), 16)
                        ld_f = emi("sy", "LD", lambda e, par=par, tok0=tok0: e.dma_start(
                            out=x_sb[par][:], in_=x_in[tok0:tok0 + 128, :]), 16)
                        wait("pe", "GD", gd_ctx)
                        wait("pe", "GD", gd_oww)
                        if prevv:
                            wait("pe", "VE", prevv)
                        for chn in range(NCH):
                            em("pe", lambda e, chn=chn, sub=sub: e.ldweights(
                                octx[:, chn * 512 + sub * 128:chn * 512 + (sub + 1) * 128]))
                            for dsi in range(4):
                                bank = o_ps[par * 4 + dsi]
                                fn = lambda e, chn=chn, sub=sub, dsi=dsi, bank=bank: e.matmul(
                                    out=bank[:],
                                    lhsT=octx[:, chn * 512 + sub * 128:chn * 512 + (sub + 1) * 128],
                                    rhs=oww[:, chn * D + dsi * 512:chn * D + (dsi + 1) * 512],
                                    start=(chn == 0), stop=(chn == NCH - 1),
                                    skip_group_check=True)
                                if not (chn == NCH - 1 and dsi == 3):
                                    em("pe", fn)
                                else:
                                    emi("pe", "PE", fn)
                        pe_sub = val("PE")
                        pe_strip[st_i] = pe_sub
                        wait("ve", "PE", pe_sub)
                        wait("ve", "LD", ld_f)
                        prev_out = out_cnt.get((st_i, sub - 2)) or out_cnt.get((st_i - 1, sub + 2))
                        if prev_out:
                            wait("ve", "OUT", prev_out)
                        for dsi in range(4):
                            bank = o_ps[par * 4 + dsi]
                            fn = lambda e, par=par, dsi=dsi, bank=bank: e.tensor_tensor(
                                out=ffn_sb[par][:, dsi * 512:(dsi + 1) * 512], in0=bank[:],
                                in1=ffn_sb[par][:, dsi * 512:(dsi + 1) * 512], op=ALU.add)
                            if dsi < 3:
                                em("ve", fn)
                            else:
                                emi("ve", "VE", fn)
                        ve_fin[(st_i, sub)] = kb.semi("ve", "VE", lambda e, par=par: e.tensor_tensor(
                            out=x_sb[par][:], in0=ffn_sb[par][:], in1=x_sb[par][:], op=ALU.add))
                        wait("sy", "VE", ve_fin[(st_i, sub)])
                        out_cnt[(st_i, sub)] = emi("sy", "OUT", lambda e, par=par, tok0=tok0: e.dma_start(
                            out=out_t[tok0:tok0 + 128, :], in_=x_sb[par][:]), 16)
            kb.phase_flush(block)

        kb.phase_begin()
        if "xn_sb" in DEBUG_TAPS:
            xndbg = nc.dram_tensor("dbg_xn", [128, NCH * TOK], BF16,
                                   kind="ExternalOutput").ap()
            for s in ("PE", "VE", "AC"):
                wait("sy", s, val(s))
            emi("sy", "OUT", lambda e: e.dma_start(out=xndbg[:, :], in_=xn[:]), 16)
        for name, (dst, src) in dbg.items():
            for s in ("GD", "LD", "CC", "PE", "VE", "AC", "PL"):
                wait("sy", s, val(s))
            emi("sy", "OUT", lambda e, dst=dst, src=src: e.dma_start(
                out=dst[:, :], in_=src[:, :]), 16)
        for s in ("GD", "LD", "CC", "PE", "VE", "AC", "PL", "OUT",
                  "CW", "UP", "DN", "AA"):
            wait("sy", s, val(s))
        for eng in ("pe", "ve", "ac", "gp"):
            wait(eng, "OUT", val("OUT"))
            wait(eng, "LD", val("LD"))
            wait(eng, "GD", val("GD"))
        kb.phase_flush(block)

    ctx.close()
    return nc


def _host_inputs(inputs):
    x = np.asarray(inputs["L0_attn_x"])
    wqkv = np.ascontiguousarray(np.asarray(inputs["L0_attn_Wqkv"], dtype=np.float32))
    outw = np.ascontiguousarray(np.asarray(inputs["L0_attn_out_W"], dtype=np.float32))
    qls = np.asarray(inputs["L0_attn_q_ln_scale"], dtype=np.float32)
    kls = np.asarray(inputs["L0_attn_k_ln_scale"], dtype=np.float32)
    gate = np.asarray(inputs["L0_ffn_gate_w"], dtype=np.float32)
    eup = np.asarray(inputs["L0_ffn_expert_up_w"], dtype=np.float32)
    edn = np.asarray(inputs["L0_ffn_expert_down_w"], dtype=np.float32)
    sup = np.asarray(inputs["L0_ffn_shared_up_w"], dtype=np.float32)
    sdn = np.asarray(inputs["L0_ffn_shared_down_w"], dtype=np.float32)

    upT = np.empty((NE, D, 2816), np.float32)
    upT[:16] = eup.transpose(0, 2, 1)
    supT = sup.T
    upT[16] = np.concatenate([supT[:, 0:1408], supT[:, 2816:4224]], axis=1)
    upT[17] = np.concatenate([supT[:, 1408:2816], supT[:, 4224:5632]], axis=1)
    downT = np.empty((NE, 1408, D), np.float32)
    downT[:16] = edn.transpose(0, 2, 1)
    sdnT = sdn.T
    downT[16] = sdnT[0:1408]
    downT[17] = sdnT[1408:2816]
    upT = np.ascontiguousarray(upT)
    downT = np.ascontiguousarray(downT)
    gwT = np.ascontiguousarray(gate.T)
    qgc = np.ascontiguousarray((qls / np.sqrt(128.0)).reshape(NCH, 128).T)
    kgc = np.ascontiguousarray(kls.reshape(NCH, 128).T)
    idb = np.eye(128, dtype=BF)
    idf = np.eye(128, dtype=np.float32)
    am = np.empty((128, 4 * 512), np.float32)
    ii = np.arange(128)[:, None]
    jj = np.arange(512)[None, :]
    for b in range(4):
        am[:, b * 512:(b + 1) * 512] = np.where((jj - ii) <= 128 * b, 0.0, NEGBIG)

    per_core = []
    for c in range(8):
        bb, rr = c // 4, c % 4
        xs = np.ascontiguousarray(x[bb, rr * TOK:(rr + 1) * TOK, :], dtype=np.float32)
        a_idx = np.empty((128, 48), np.int32)
        for p in range(4):
            for t_i in range(3):
                for si in range(4):
                    col = (p * 3 + t_i) * 4 + si
                    base = 6144 * si + 2048 * t_i + 512 * rr + 128 * p
                    a_idx[:, col] = base + np.arange(128)
        e_idx = np.empty((128, 64), np.int32)
        for st_i in range(4):
            for chn in range(16):
                si, j = divmod(chn, 4)
                base = 2048 * si + 512 * rr + 128 * j
                e_idx[:, st_i * 16 + chn] = 4 * (base + np.arange(128)) + st_i
        per_core.append({
            "x": xs, "wqkv": wqkv, "outw": outw, "upT": upT, "downT": downT,
            "gwT": gwT, "qg": qgc, "kg": kgc, "idb": idb, "idf": idf,
            "amask": am, "aidx": a_idx, "eidx": e_idx,
        })
    return per_core


def kernel(**inputs):
    per_core = _host_inputs(inputs)
    nc = build_program()
    res = run_bass_kernel_spmd(nc, per_core, core_ids=list(range(8)))
    y = np.empty((NB, S, D), np.float32)
    for c in range(8):
        bb, rr = c // 4, c % 4
        y[bb, rr * TOK:(rr + 1) * TOK, :] = res.results[c]["out"]
    return y
